# version 05
# baseline (speedup 1.0000x reference)
import numpy as np
import ml_dtypes
import concourse.bass as bass
import concourse.mybir as mybir
from concourse.bass_utils import run_bass_kernel_spmd

F32 = mybir.dt.float32
F32R = mybir.dt.float32r
BF16 = mybir.dt.bfloat16
I32 = mybir.dt.int32
AL = mybir.AluOpType
AF = mybir.ActivationFunctionType
BFNP = ml_dtypes.bfloat16

DM = 2048
DIN = 4096
DSL = 1024
L = 8192
TC = 512
T2 = 2048
NT = 128
HID = 5632

ENGINES = ("sync", "tensor", "scalar", "vector", "gpsimd")


class Prog:

    def __init__(self, nc):
        self.nc = nc
        self.ops = {e: [] for e in ENGINES}
        self.cnt = {}
        self.sems = {}

    def sem(self, name):
        if name not in self.sems:
            self.sems[name] = self.nc.alloc_semaphore(name)
            self.cnt[name] = 0
        return self.sems[name]

    def emit(self, eng, fn, waits=(), inc=None, incn=1):
        waits = tuple((s, v) for (s, v) in waits if v > 0)
        for s, v in waits:
            assert v <= self.cnt[s], (eng, s, v, self.cnt[s])
        self.ops[eng].append((fn, waits, inc, incn))
        if inc is not None:
            self.cnt[inc] += incn
            return self.cnt[inc]
        return None

    def v(self, name):
        return self.cnt.get(name, 0)

    def build(self):
        sems = self.sems
        with self.nc.Block() as blk:
            for ename in ENGINES:
                ops = self.ops[ename]
                if not ops:
                    continue

                def make(ops=ops):
                    def body(eng):
                        for fn, waits, inc, incn in ops:
                            for s, v in waits:
                                eng.wait_ge(sems[s], v)
                            inst = fn(eng)
                            if inc is not None:
                                inst.then_inc(sems[inc], incn)
                    return body

                getattr(blk, ename)(make())


def _t(h, i, w):
    return h[:, i * w:(i + 1) * w]


class Common:
    def __init__(self, p):
        self.p = p

    def dma(self, dst, src, waits=(), sem="Sd", throttle=True):
        w = list(waits)
        if throttle:
            w.append((sem, self.p.v(sem)))
        return self.p.emit("sync", lambda e, d=dst, s=src: e.dma_start(d, s),
                           w, sem, 16)

    def mm(self, pt, lh, rh, st, sp, waits=(), tp=None, skip=False):
        def f(e, pt=pt, lh=lh, rh=rh, st=st, sp=sp, tp=tp, skip=skip):
            return e.matmul(pt, lh, rh, start=st, stop=sp, tile_position=tp,
                            skip_group_check=skip)
        return self.p.emit("tensor", f, waits, "St")

    def act(self, out, in_, fn, bias=0.0, scale=1.0, waits=()):
        def f(e, out=out, in_=in_, fn=fn, bias=bias, scale=scale):
            return e.activation(out, in_, fn, bias=bias, scale=scale)
        return self.p.emit("scalar", f, waits, "Ss")

    def sccp(self, out, in_, waits=()):
        return self.p.emit("scalar", lambda e, o=out, i=in_: e.copy(o, i),
                           waits, "Ss")

    def vtt(self, out, a, b, op, waits=()):
        return self.p.emit("vector", lambda e, o=out, a=a, b=b, op=op:
                           e.tensor_tensor(o, a, b, op), waits, "Sv")

    def vcp(self, out, in_, waits=()):
        return self.p.emit("vector", lambda e, o=out, i=in_:
                           e.tensor_copy(o, i), waits, "Sv")




def build_p1(l=L, dbg=False, barrier=False):
    nch = l // TC
    nc = bass.Bass("TRN2", target_bir_lowering=False,
                   dynamic_dma_scratch_size=2048)
    p = Prog(nc)
    for s in ("Sd", "St", "Ss", "Sv", "Sg", "So",
              "Si", "Swc", "Sx0", "Sx1", "Sgx"):
        p.sem(s)
    C = Common(p)
    dma, mm, act, sccp, vtt, vcp = C.dma, C.mm, C.act, C.sccp, C.vtt, C.vcp

    DT = nc.dram_tensor
    xt = DT("xt", [DM, l], F32, kind="ExternalInput").ap()
    wx = DT("wx", [128, 2 * 16 * 128], F32, kind="ExternalInput").ap()
    wbm = DT("wbm", [128, 2 * 16 * 128], F32, kind="ExternalInput").ap()
    wc = DT("wc", [128, 8 * 16 * 128], F32, kind="ExternalInput").ap()
    wdtd = DT("wdtd", [128, 16 * 128], F32, kind="ExternalInput").ap()
    wdt = DT("wdt", [128, 8 * 128], F32, kind="ExternalInput").ap()
    alog = DT("alog", [128, 4096], F32, kind="ExternalInput").ap()
    e8i = DT("e8", [128, 512], F32R, kind="ExternalInput").ap()
    e16i = DT("e16", [128, 256], F32R, kind="ExternalInput").ap()
    e16bi = DT("e16b", [128, 256], BF16, kind="ExternalInput").ap()
    smati = DT("smat", [128, 128], BF16, kind="ExternalInput").ap()
    cwi = DT("cw", [128, 32], F32, kind="ExternalInput").ap()
    cbi = DT("cb", [128, 8], F32, kind="ExternalInput").ap()
    bxi = DT("bx", [128, 2], F32, kind="ExternalInput").ap()
    bdti = DT("bdt", [128, 8], F32, kind="ExternalInput").ap()
    dvi = DT("dvec", [128, 8], F32, kind="ExternalInput").ap()
    csti = DT("cst", [128, 4], F32, kind="ExternalInput").ap()
    onesdi = DT("onesd", [128, 1], BF16, kind="ExternalInput").ap()
    ones1i = DT("ones1", [1, 128], BF16, kind="ExternalInput").ap()
    yp = DT("yp", [DSL, l], F32, kind="ExternalOutput").ap()
    if dbg:
        xn_d = DT("xn_d", [128, 16 * l], BF16, kind="ExternalOutput").ap()
        dl_d = DT("dl_d", [128, 8 * l], F32, kind="ExternalOutput").ap()
        u_d = DT("u_d", [128, 8 * l], F32, kind="ExternalOutput").ap()
        b_d = DT("b_d", [128, 2 * l], F32, kind="ExternalOutput").ap()
        c_d = DT("c_d", [128, 8 * l], BF16, kind="ExternalOutput").ap()
        h_d = DT("h_d", [128, 4 * l], F32, kind="ExternalOutput").ap()

    A = nc.alloc_sbuf_tensor
    wx_sb = A("wx_sb", [128, 2 * 16 * 128], BF16)
    wb_sb = A("wb_sb", [128, 2 * 16 * 128], BF16)
    wdtd_sb = A("wdtd_sb", [128, 16 * 128], BF16)
    wdt_sb = A("wdt_sb", [128, 8 * 128], F32R)
    wcb_sb = A("wcb_sb", [128, 2 * 2048], BF16)
    ea_sb = A("ea_sb", [128, 4096], F32R)
    e8_sb = A("e8_sb", [128, 512], F32R)
    e16_sb = A("e16_sb", [128, 256], F32R)
    e16b_sb = A("e16b_sb", [128, 256], BF16)
    s_sb = A("s_sb", [128, 128], BF16)
    cw_sb = A("cw_sb", [128, 32], F32)
    cb_sb = A("cb_sb", [128, 8], F32)
    bx_sb = A("bx_sb", [128, 2], F32)
    bdt2_sb = A("bdt2_sb", [128, 8], F32)
    d_sb = A("d_sb", [128, 8], F32)
    cst_sb = A("cst_sb", [128, 4], F32)
    onesd_sb = A("onesd_sb", [128, 1], BF16)
    ones1_sb = A("ones1_sb", [1, 128], BF16)
    x_ld = A("x_ld", [128, 2 * 512], F32)
    xw = A("xw", [128, 16 * 512], BF16)
    xsq = A("xsq", [128, 512], BF16)
    mean_f = A("mean_f", [1, 512], F32)
    msq_f = A("msq_f", [1, 512], F32)
    m2_f = A("m2_f", [1, 512], F32)
    sd_f = A("sd_f", [1, 512], F32)
    rr_f = A("rr_f", [1, 512], F32)
    mean_bf = A("mean_bf", [1, 512], BF16)
    rr_bf = A("rr_bf", [1, 512], BF16)
    mbc_sb = A("mbc_sb", [128, 512], F32)
    rbc_sb = A("rbc_sb", [128, 512], F32)
    dt1_sb = A("dt1_sb", [128, 512], F32R)
    dl_sb = A("dl_sb", [128, 8 * 512], F32R)
    u_sb = A("u_sb", [128, 8 * 512], F32R)
    wcs_sb = A("wcs_sb", [128, 1024], F32)
    w_sb = A("w_sb", [128, 8 * 512], F32R)
    c_sb = A("c_sb", [128, 8 * 512], BF16)
    b_sb = A("b_sb", [128, 2 * 512], F32R)
    xb_sb = A("xb_sb", [128, 2 * 515], F32)
    xbc_sb = A("xbc_sb", [128, 515], F32)
    tmpa_sb = A("tmpa_sb", [128, 512], F32)
    tmpb_sb = A("tmpb_sb", [128, 512], F32)
    acc_sb = A("acc_sb", [128, 2 * 512], F32)
    da_sb = A("da_sb", [128, 2 * 1024], F32)
    dbu_sb = A("dbu_sb", [128, 1024], F32)
    hp_sb = A("hp_sb", [128, 4 * 512], F32)
    st_sb = A("st_sb", [128, 128], F32)
    be_sb = A("be_sb", [128, 512], F32)
    ce_sb = A("ce_sb", [128, 512], F32)
    ch_sb = A("ch_sb", [128, 2 * 1024], BF16)
    yps_sb = A("yps_sb", [128, 512], F32)
    fence_sb = A("fence_sb", [128, 512], F32)

    AP_ = nc.alloc_psum_tensor
    P_AD = AP_("P_AD", [128, 2 * 512], F32)
    P_WE = AP_("P_WE", [128, 2 * 512], F32)
    P_CE = AP_("P_CE", [128, 512], F32)
    P_Y = AP_("P_Y", [128, 512], F32)
    P_PR = AP_("P_PR", [128, 512], F32)
    P_XE = AP_("P_XE", [128, 512], F32)

    NEG1 = cst_sb[:, 0:1]
    ONE = cst_sb[:, 1:2]
    EPS1 = cst_sb[0:1, 2:3]
    NEGH = cst_sb[0:1, 3:4]

    for dst, src in ((cw_sb, cwi), (cb_sb, cbi), (bx_sb, bxi),
                     (bdt2_sb, bdti), (d_sb, dvi), (cst_sb, csti),
                     (e8_sb, e8i), (e16_sb, e16i), (e16b_sb, e16bi),
                     (s_sb, smati), (onesd_sb, onesdi), (ones1_sb, ones1i)):
        dma(dst[:], src, sem="Si", throttle=False)
    v0 = p.v("Si")
    bdt2v = p.emit("vector",
                   lambda e: e.tensor_scalar_mul(bdt2_sb[:], bdt2_sb[:], 2.0),
                   [("Si", v0)], "Sv")
    for (src, dst, ncols) in ((wx, wx_sb, 2 * 16), (wbm, wb_sb, 2 * 16),
                              (wdtd, wdtd_sb, 16), (wdt, wdt_sb, 8)):
        total = ncols * 128
        off = 0
        while off < total:
            n = min(1024, total - off)
            stg = wcs_sb[:, 0:n]
            ld = dma(stg, src[:, off:off + n], [("Sv", p.v("Sv"))], sem="Swc")
            p.emit("vector",
                   lambda e, o=dst[:, off:off + n], i=stg: e.tensor_copy(o, i),
                   [("Swc", ld)], "Sv")
            off += n
    for kk in range(8):
        ld = dma(da_sb[:, 0:512], alog[:, kk * 512:(kk + 1) * 512],
                 [("Sv", p.v("Sv")), ("Ss", p.v("Ss"))], sem="Swc")
        a1 = act(da_sb[:, 512:1024], da_sb[:, 0:512], AF.Exp,
                 waits=[("Swc", ld)])
        p.emit("vector",
               lambda e, kk=kk: e.tensor_scalar_mul(
                   _t(ea_sb, kk, 512), da_sb[:, 512:1024], -1.0),
               [("Ss", a1)], "Sv")
    p.emit("vector", lambda e: e.memset(st_sb[:], 0.0), [], "Sv")
    p.emit("vector", lambda e: e.memset(xb_sb[:, 0:3], 0.0), [], "Sv")
    p.emit("vector", lambda e: e.memset(xb_sb[:, 515:518], 0.0), [], "Sv")

    bank = {k: ("Ss", 0) for k in ("AD", "WE", "CE", "Y", "PR", "XE")}
    xw_free = ("St", 0)
    dl_free = [("St", 0)] * 8
    w_free = [("St", 0)] * 8
    c_free = [("St", 0)] * 8
    b_free = [("St", 0)] * 2
    u_freev = [("Sv", 0)] * 8
    xld_cast = [("Sv", 0), ("Sv", 0)]
    yps_free = ("So", 0)
    da_free = [("Sv", 0), ("Sv", 0)]
    we_freed = ("Sv", 0)
    ce_freed = ("Sg", 0)
    ch_free = [("St", 0), ("St", 0)]
    be_freed = ("Sv", 0)
    xbc_free = ("Sv", 0)
    tmp_free = ("Sv", 0)
    acc_free = ("Ss", 0)
    xb_read = ("Sgx", 0)
    halo_done = ("Ss", 0)
    wcb_free = [("St", 0), ("St", 0)]
    wcs_free = ("Sv", 0)
    slot_free = {0: ("Sg", 0), 1: ("Sg", 0), 2: ("Sg", 0), 3: ("Sg", 0)}
    projb = [("PR", P_PR[:]), ("XE", P_XE[:])]
    pb_i = [0]

    def nb():
        nm, pt = projb[pb_i[0] % 2]
        pb_i[0] += 1
        return nm, pt

    for c in range(nch):
        if barrier and c > 0:
            bvals = [(sn, p.v(sn)) for sn in p.sems]
            for en in ENGINES:
                p.emit(en, lambda e: e.nop(), bvals, None)
        col = slice(c * TC, (c + 1) * TC)
        last_q = 0
        for t in range(16):
            par = t % 2
            sx = "Sx%d" % par
            ld = dma(_t(x_ld, par, 512), xt[128 * t:128 * (t + 1), col],
                     [xld_cast[par]], sem=sx)
            cwv = p.emit("vector",
                         lambda e, t=t, par=par: e.tensor_copy(
                             _t(xw, t, 512), _t(x_ld, par, 512)),
                         [(sx, ld), xw_free] if t == 0 else [(sx, ld)],
                         "Sv")
            xld_cast[par] = ("Sv", cwv)
            sq = p.emit("vector",
                        lambda e, t=t: e.tensor_tensor(
                            xsq[:], _t(xw, t, 512), _t(xw, t, 512), AL.mult),
                        [("St", last_q)], "Sv")
            mm(P_PR[0:1, :], onesd_sb[:], _t(xw, t, 512), t == 0, t == 15,
               [("Sv", cwv)] + ([bank["PR"], ("Si", v0)] if t == 0 else []))
            last_q = mm(P_XE[0:1, :], onesd_sb[:], xsq[:], t == 0, t == 15,
                        [("Sv", sq)] + ([bank["XE"]] if t == 0 else []))
        stats_done = p.v("St")
        c1 = vcp(mean_f[:], P_PR[0:1, :], [("St", stats_done)])
        vcp(mean_bf[:], P_PR[0:1, :])
        c2 = vcp(msq_f[:], P_XE[0:1, :])
        vtt(m2_f[:], mean_f[:], mean_f[:], AL.mult, [])
        v_var = vtt(sd_f[:], msq_f[:], m2_f[:], AL.subtract, [])
        a_ln = act(rr_f[:], sd_f[:], AF.Ln, bias=EPS1,
                   waits=[("Sv", v_var)])
        a_rs = act(sd_f[:], rr_f[:], AF.Exp, scale=NEGH)
        v_rrb = vcp(rr_bf[:], sd_f[:], [("Ss", a_rs)])
        mm(P_PR[:], ones1_sb[:], mean_bf[:], True, True,
           [("Sv", v_rrb), bank["PR"]])
        rb = mm(P_XE[:], ones1_sb[:], rr_bf[:], True, True, [bank["XE"]])
        e_mb = sccp(mbc_sb[:], P_PR[:], [("St", rb)])
        e_rb = sccp(rbc_sb[:], P_XE[:], [])
        bank["PR"] = ("Ss", e_mb)
        bank["XE"] = ("Ss", e_rb)
        lnt = 0
        for t in range(16):
            par = t % 2
            sx = "Sx%d" % par
            ld2 = dma(_t(x_ld, par, 512), xt[128 * t:128 * (t + 1), col],
                      [xld_cast[par]], sem=sx)
            w1 = [(sx, ld2), ("Ss", e_rb), tmp_free]
            if t == 0:
                w1.append(("St", stats_done))
            lnt = vtt(tmpa_sb[:], _t(x_ld, par, 512), mbc_sb[:], AL.subtract,
                      w1 if t == 0 else [(sx, ld2)])
            lnt = vtt(_t(xw, t, 512), tmpa_sb[:], rbc_sb[:], AL.mult, [])
            xld_cast[par] = ("Sv", lnt)
        tmp_free = ("Sv", lnt)
        xn_done = p.v("Sv")

        nm, pt = nb()
        lastm = 0
        for k in range(16):
            lastm = mm(pt, wdtd_sb[:, k * 128:k * 128 + 128], _t(xw, k, 512),
                       k == 0, k == 15,
                       [bank[nm], ("Sv", xn_done)] if k == 0 else [])
        e1 = sccp(dt1_sb[:], pt, [("St", lastm)])
        bank[nm] = ("Ss", e1)

        def proj16(w_handle, mt, extra, nk=16):
            nm_, pt_ = nb()
            last_ = 0
            for k in range(nk):
                lh = w_handle[:, (mt * nk + k) * 128:(mt * nk + k) * 128 + 128]
                last_ = mm(pt_, lh, _t(xw, k, 512), k == 0, k == nk - 1,
                           ([bank[nm_]] + extra) if k == 0 else [])
            return nm_, pt_, last_

        def emit_delta(m):
            nonlocal tmp_free
            nm_, pt_ = nb()
            mv = mm(pt_, wdt_sb[:, 128 * m:128 * m + 128], dt1_sb[:],
                    True, True, [bank[nm_], ("Ss", e1), dl_free[m]])
            bias = bdt2_sb[:, m:m + 1]
            act(tmpa_sb[:], pt_, AF.Abs, bias=bias,
                waits=[("St", mv), tmp_free])
            act(tmpb_sb[:], tmpa_sb[:], AF.Exp, scale=NEG1)
            act(tmpa_sb[:], tmpb_sb[:], AF.Ln, bias=ONE)
            a4 = act(tmpb_sb[:], pt_, AF.Relu, bias=bias)
            bank[nm_] = ("Ss", a4)
            dv_ = vtt(_t(dl_sb, m, 512), tmpa_sb[:], tmpb_sb[:], AL.add,
                      [("Ss", a4)])
            tmp_free = ("Sv", dv_)
            return dv_

        def emit_cproj(m):
            nonlocal wcs_free
            wpar = m % 2
            for hh in range(2):
                ldw = dma(wcs_sb[:], wc[:, m * 2048 + 1024 * hh:
                                        m * 2048 + 1024 * hh + 1024],
                          [wcs_free], sem="Swc")
                cv = p.emit("vector",
                            lambda e, wpar=wpar, hh=hh: e.tensor_copy(
                                wcb_sb[:, wpar * 2048 + 1024 * hh:
                                       wpar * 2048 + 1024 * hh + 1024],
                                wcs_sb[:]),
                            [("Swc", ldw)] +
                            ([wcb_free[wpar]] if hh == 0 else []), "Sv")
                wcs_free = ("Sv", cv)
            nm_, pt_ = nb()
            last_ = 0
            for k in range(16):
                last_ = mm(pt_, wcb_sb[:, wpar * 2048 + k * 128:
                                       wpar * 2048 + k * 128 + 128],
                           _t(xw, k, 512), k == 0, k == 15,
                           [bank[nm_], ("Sv", cv), c_free[m]] if k == 0
                           else [])
            wcb_free[wpar] = ("St", last_)
            ev_ = sccp(_t(c_sb, m, 512), pt_, [("St", last_)])
            bank[nm_] = ("Ss", ev_)
            return ev_

        def emit_xb_b(m):
            nm_, pt_, lastm_ = proj16(wx_sb, m, [halo_done] if m == 0 else [])
            ev_ = act(xb_sb[:, 515 * m + 3:515 * m + 515], pt_, AF.Identity,
                      bias=bx_sb[:, m:m + 1],
                      waits=[("St", lastm_), xb_read])
            bank[nm_] = ("Ss", ev_)
            nm_, pt_, lastm_ = proj16(wb_sb, m, [b_free[m]])
            ev_ = sccp(_t(b_sb, m, 512), pt_, [("St", lastm_)])
            bank[nm_] = ("Ss", ev_)
            return ev_

        def emit_conv(k):
            nonlocal xbc_free, acc_free
            bt, o = k // 4, (32 * k) % 128
            g_ = 0
            for hb in range(2):
                for rr in range(4):
                    first = hb == 0 and rr == 0
                    g_ = dma(xbc_sb[hb * 64 + rr * 16:hb * 64 + rr * 16 + 16,
                                    :],
                             xb_sb[o + 16 * hb:o + 16 * hb + 16,
                                   515 * bt:515 * bt + 515],
                             [("Ss", p.v("Ss")), xbc_free] if first else [],
                             sem="Sgx", throttle=first)
            t0 = p.emit("vector",
                        lambda e, k=k: e.tensor_scalar(
                            _t(acc_sb, 0, 512), xbc_sb[:, 0:512],
                            cw_sb[:, 4 * k:4 * k + 1], None, AL.mult),
                        [("Sgx", g_), acc_free], "Sv")
            for tp_ in range(1, 4):
                t0 = p.emit("vector",
                            lambda e, k=k, tp_=tp_: e.scalar_tensor_tensor(
                                _t(acc_sb, tp_ % 2, 512),
                                xbc_sb[:, tp_:tp_ + 512],
                                cw_sb[:, 4 * k + tp_:4 * k + tp_ + 1],
                                _t(acc_sb, (tp_ + 1) % 2, 512),
                                AL.mult, AL.add),
                            [], "Sv")
            xbc_free = ("Sv", t0)
            su = act(_t(u_sb, k, 512), _t(acc_sb, 1, 512), AF.Silu,
                     bias=cb_sb[:, k:k + 1],
                     waits=[("Sv", t0), u_freev[k]])
            acc_free = ("Ss", su)
            wv = vtt(_t(w_sb, k, 512), _t(dl_sb, k, 512), _t(u_sb, k, 512),
                     AL.mult, [("Ss", su), w_freetk[k]])
            return wv

        w_freetk = w_free

        dlv = [0] * 8
        cev = [0] * 8
        wvv = [0] * 8
        bev = [0, 0]
        dlv[0] = emit_delta(0)
        cev[0] = emit_cproj(0)
        bev[0] = emit_xb_b(0)
        dlv[1] = emit_delta(1)
        cev[1] = emit_cproj(1)
        wvv[0] = emit_conv(0)

        for k in range(8):
            if k + 2 < 8:
                dlv[k + 2] = emit_delta(k + 2)
                cev[k + 2] = emit_cproj(k + 2)
            if k == 2:
                bev[1] = emit_xb_b(1)
            if k + 1 < 8:
                wvv[k + 1] = emit_conv(k + 1)
            if k == 7:
                xbr = ("Sgx", p.v("Sgx"))
                h1 = sccp(xb_sb[:, 0:3], xb_sb[:, 512:515], [xbr])
                h2 = sccp(xb_sb[:, 515:518], xb_sb[:, 1027:1030], [])
                xb_read = xbr

            for pr in range(8):
                j0 = 16 * k + 2 * pr
                j1 = j0 + 1
                ppar = (j0 // 2) % 2
                g4 = (j0 % 16) // 4
                base = 32 * g4
                kt = k
                if j0 % 8 == 0:
                    hb = j0 // 8
                    bb = 32 * ((hb % 8) // 2)
                    vv = hb % 2
                    mb_ = mm(P_CE[:],
                             e16_sb[bb:bb + 32, 128 * vv:128 * vv + 128],
                             b_sb[bb:bb + 32,
                                  (hb // 8) * 512:(hb // 8) * 512 + 512],
                             True, True,
                             [bank["CE"], ("Ss", bev[hb // 8])],
                             tp=(bb, 0))
                    eb = sccp(be_sb[:], P_CE[:], [("St", mb_), be_freed])
                    bank["CE"] = ("Ss", eb)
                    be_ready = eb
                gl_ = j0 // 2
                bc_ = 32 * ((gl_ % 8) // 2)
                vc = gl_ % 2
                mc_ = mm(P_CE[:],
                         e16b_sb[bc_:bc_ + 32, 128 * vc:128 * vc + 128],
                         c_sb[bc_:bc_ + 32, kt * 512:kt * 512 + 512],
                         True, True, [bank["CE"], ("Ss", cev[k])],
                         tp=(bc_, 0))
                ec = sccp(ce_sb[:], P_CE[:], [("St", mc_), ce_freed])
                bank["CE"] = ("Ss", ec)
                for jj, half in ((j0, 0), (j1, 1)):
                    eidx = 4 * (jj // 16) + (jj % 4)
                    mm(P_AD[:, half * 512:half * 512 + 512],
                       ea_sb[base:base + 32, 128 * eidx:128 * eidx + 128],
                       dl_sb[base:base + 32, kt * 512:kt * 512 + 512],
                       True, True,
                       ([bank["AD"], ("Sv", dlv[k])] if half == 0 else []),
                       tp=(base, 0))
                ma = p.v("St")
                for jj, half in ((j0, 0), (j1, 1)):
                    mm(P_WE[:, half * 512:half * 512 + 512],
                       e8_sb[base:base + 32,
                             128 * (jj % 4):128 * (jj % 4) + 128],
                       w_sb[base:base + 32, kt * 512:kt * 512 + 512],
                       True, True,
                       ([bank["WE"], we_freed, ("Sv", wvv[k])]
                        if half == 0 else []),
                       tp=(base, 0))
                mw = p.v("St")
                da = act(_t(da_sb, ppar, 1024), P_AD[:], AF.Exp,
                         waits=[("St", ma), da_free[ppar]])
                bank["AD"] = ("Ss", da)
                db = vtt(dbu_sb[:], P_WE[:],
                         be_sb[:].unsqueeze(1).broadcast_to([128, 2, 512]),
                         AL.mult,
                         [("St", mw), ("Ss", be_ready), ("Ss", da)])
                we_freed = ("Sv", db)
                be_freed = ("Sv", db)
                if j0 % 4 == 0 and not (k == 0 and pr == 0):
                    jprev = j0 - 1
                    p.emit("vector",
                           lambda e, j0=j0: e.tensor_copy(
                               st_sb[:, j0 - 4:j0].unsqueeze(2),
                               hp_sb[:].rearrange("p (s t) -> p s t", s=4)
                               [:, :, 511:512]),
                           [], "Sv")
                sc = 0
                for jj, half in ((j0, 0), (j1, 1)):
                    slot = jj % 4
                    sc = p.emit("vector",
                                lambda e, slot=slot, ppar=ppar, jj=jj,
                                half=half:
                                e.tensor_tensor_scan(
                                    _t(hp_sb, slot, 512),
                                    da_sb[:, ppar * 1024 + half * 512:
                                          ppar * 1024 + half * 512 + 512],
                                    dbu_sb[:, half * 512:half * 512 + 512],
                                    st_sb[:, jj:jj + 1], AL.mult, AL.add),
                                [slot_free[slot]], "Sv")
                da_free[ppar] = ("Sv", sc)
                fence = p.emit("vector",
                               lambda e: e.tensor_copy(fence_sb[:],
                                                       da_sb[:, 0:512]),
                               [], "Sv")
                cpar = (j0 // 2) % 2
                sl0 = j0 % 4
                ch = p.emit("gpsimd",
                            lambda e, cpar=cpar, sl0=sl0: e.tensor_tensor(
                                _t(ch_sb, cpar, 1024),
                                hp_sb[:, sl0 * 512:sl0 * 512 + 1024],
                                ce_sb[:].unsqueeze(1)
                                .broadcast_to([128, 2, 512]),
                                AL.mult),
                            [("Sv", fence), ("Ss", ec), ch_free[cpar]],
                            "Sg")
                ce_freed = ("Sg", ch)
                slot_free[sl0] = ("Sg", ch)
                slot_free[(sl0 + 1) % 4] = ("Sg", ch)
                ys = 0
                for jj, half in ((j0, 0), (j1, 1)):
                    cc = (jj % 16) // 4
                    tpp = (0, 96) if cc == 3 else None
                    ys = mm(P_Y[32 * cc:32 * cc + 32, :],
                            s_sb[:, 32 * (jj % 4):32 * (jj % 4) + 32],
                            ch_sb[:, cpar * 1024 + half * 512:
                                  cpar * 1024 + half * 512 + 512],
                            jj % 4 == 0, jj % 4 == 3,
                            [("Sg", ch)] + ([bank["Y"]] if jj % 16 == 0
                                            else []),
                            tp=tpp, skip=True)
                ch_free[cpar] = ("St", ys)
            ev = p.emit("vector",
                        lambda e, k=k: e.scalar_tensor_tensor(
                            yps_sb[:], _t(u_sb, k, 512),
                            d_sb[:, k:k + 1], P_Y[:], AL.mult, AL.add),
                        [("St", p.v("St")), yps_free], "Sv")
            bank["Y"] = ("Sv", ev)
            dma(yp[128 * k:128 * k + 128, col], yps_sb[:],
                [("Sv", ev)], sem="So")
            yps_free = ("So", p.v("So"))
            dl_free[k] = ("St", p.v("St"))
            w_free[k] = ("St", p.v("St"))
            c_free[k] = ("St", p.v("St"))
            u_freev[k] = ("Sv", p.v("Sv"))
        p.emit("vector",
               lambda e: e.tensor_copy(
                   st_sb[:, NT - 4:NT].unsqueeze(2),
                   hp_sb[:].rearrange("p (s t) -> p s t", s=4)
                   [:, :, 511:512]),
               [], "Sv")
        halo_done = ("Ss", p.v("Ss"))
        b_free = [("St", p.v("St"))] * 2
        xw_free = ("St", p.v("St"))
        if dbg:
            nchk = l // TC
            for bufs, dst, nt_, cast in ((xw, xn_d, 16, False),
                                         (dl_sb, dl_d, 8, True),
                                         (u_sb, u_d, 8, True),
                                         (b_sb, b_d, 2, True),
                                         (c_sb, c_d, 8, False)):
                for tt in range(nt_):
                    srcap = _t(bufs, tt, 512)
                    if cast:
                        srcap = srcap.bitcast(F32)
                    dma(dst[:, (tt * nchk + c) * 512:
                            (tt * nchk + c) * 512 + 512],
                        srcap, [("Sv", p.v("Sv")), ("Sg", p.v("Sg"))],
                        sem="So", throttle=False)
            for tt in range(4):
                dma(h_d[:, (tt * nchk + c) * 512:(tt * nchk + c) * 512 + 512],
                    _t(hp_sb, tt, 512), [("Sv", p.v("Sv"))], sem="So",
                    throttle=False)
            p.emit("sync", lambda e: e.nop(),
                   [(sn, p.v(sn)) for sn in p.sems if sn != "Sd"],
                   "Sd", 16)
            bar = ("Sd", p.v("Sd"))
            xld_cast = [bar, bar]
            dl_free = [bar] * 8
            w_free = [bar] * 8
            c_free = [bar] * 8
            u_freev = [bar] * 8
            b_free = [bar] * 2
            tmp_free = bar
            for _sl in range(4):
                slot_free[_sl] = bar

    std_o = DT("st_d", [128, 128], F32, kind="ExternalOutput").ap()
    dma(std_o, st_sb[:], [("Sv", p.v("Sv"))], sem="So")
    p.emit("sync", lambda e: e.nop(), [("So", p.v("So"))], None)
    p.build()
    return nc



def build_p2(t2=T2, dbg=False):
    nc = bass.Bass("TRN2", target_bir_lowering=False,
                   dynamic_dma_scratch_size=4096)
    p = Prog(nc)
    for s in ("Sd", "St", "Ss", "Sv", "Sg", "So", "Si",
              "Sx0", "Sx1", "Sw0", "Sw1", "Sy0", "Sy1",
              "Sst0", "Sst1", "Sp0", "Sp1", "Swd0", "Swd1",
              "Spl", "Sl0", "Sl1", "So0", "So1"):
        p.sem(s)
    C = Common(p)
    dma, mm, act, sccp, vtt, vcp = C.dma, C.mm, C.act, C.sccp, C.vtt, C.vcp
    ncc = t2 // 512

    DT = nc.dram_tensor
    xt2 = DT("xt2", [DM, t2], F32, kind="ExternalInput").ap()
    yp2 = DT("yp2", [DIN, t2], F32, kind="ExternalInput").ap()
    wz = DT("wz", [128, 32 * 16 * 128], F32, kind="ExternalInput").ap()
    wo = DT("wo", [128, 16 * 32 * 128], F32, kind="ExternalInput").ap()
    wg = DT("wg", [128, 44 * 16 * 128], I32, kind="ExternalInput").ap()
    wu = DT("wu", [128, 44 * 16 * 128], I32, kind="ExternalInput").ap()
    wd = DT("wd", [128, 16 * 44 * 128], I32, kind="ExternalInput").ap()
    bzi = DT("bz", [128, 32], F32, kind="ExternalInput").ap()
    boi = DT("bo", [128, 16], F32, kind="ExternalInput").ap()
    sgi = DT("sg", [128, 44], F32, kind="ExternalInput").ap()
    sui = DT("su", [128, 44], F32, kind="ExternalInput").ap()
    sdi = DT("sdn", [128, 16], F32, kind="ExternalInput").ap()
    onesdi = DT("onesd", [128, 1], BF16, kind="ExternalInput").ap()
    ones1i = DT("ones1", [1, 128], BF16, kind="ExternalInput").ap()
    csti = DT("cst", [128, 4], F32, kind="ExternalInput").ap()
    outt = DT("outt", [DM, t2], F32, kind="ExternalOutput").ap()
    s_dr = DT("s_dr", [DM, t2], F32,
              kind="ExternalOutput" if dbg else "Internal").ap()
    if dbg:
        g_d = DT("g_d", [DM, t2], F32, kind="ExternalOutput").ap()
        sp_d = DT("sp_d", [DIN, t2], BF16, kind="ExternalOutput").ap()
    p_dr = DT("p_dr", [HID, t2], BF16, kind="Internal").ap()
    wd_dr = DT("wd_dr", [128, 16 * 44 * 128], BF16, kind="Internal").ap()

    A = nc.alloc_sbuf_tensor
    arena = A("arena", [128, 96 * 1024], mybir.dt.uint8)
    ARE = arena[:].bitcast(BF16)
    xn2 = ARE[:, 0:16 * t2]
    spc = ARE[:, 16 * t2:16 * t2 + 32 * 512]
    pch = ARE[:, 0:44 * 1024]
    x_ld = A("x_ld", [128, 2 * 512], F32)
    xsq = A("xsq", [128, 512], BF16)
    mean_f = A("mean_f", [1, 512], F32)
    msq_f = A("msq_f", [1, 512], F32)
    m2_f = A("m2_f", [1, 512], F32)
    sd_f = A("sd_f", [1, 512], F32)
    rr_f = A("rr_f", [1, 512], F32)
    mean_bf = A("mean_bf", [1, 512], BF16)
    rr_bf = A("rr_bf", [1, 512], BF16)
    mbc_sb = A("mbc_sb", [128, 512], F32)
    rbc_sb = A("rbc_sb", [128, 512], F32)
    tmpa_sb = A("tmpa_sb", [128, 512], F32)
    onesd_sb = A("onesd_sb", [128, 1], BF16)
    ones1_sb = A("ones1_sb", [1, 128], BF16)
    cst_sb = A("cst_sb", [128, 4], F32)
    bz_sb = A("bz_sb", [128, 32], F32)
    bo_sb = A("bo_sb", [128, 16], F32)
    sg_sb = A("sg_sb", [128, 44], F32)
    su_sb = A("su_sb", [128, 44], F32)
    sd2_sb = A("sd2_sb", [128, 16], F32)
    wst = A("wst", [128, 2 * 5632], F32)
    wbf = A("wbf", [128, 2 * 5632], BF16)
    ypt = A("ypt", [128, 2 * 512], F32)
    szt = A("szt", [128, 512], F32)
    gt_sb = A("gt_sb", [128, 512], F32)
    ut_sb = A("ut_sb", [128, 512], F32)
    pt_sb = A("pt_sb", [128, 2 * 512], BF16)
    st2_sb = A("st2_sb", [128, 2 * 512], F32)
    g2_sb = A("g2_sb", [128, 512], F32)
    sgl_sb = A("sgl_sb", [128, 2 * 512], F32)
    xtl_sb = A("xtl_sb", [128, 2 * 512], F32)
    ot_sb = A("ot_sb", [128, 2 * 512], F32)

    AP_ = nc.alloc_psum_tensor
    P_A = AP_("P_A", [128, 512], F32)
    P_B = AP_("P_B", [128, 512], F32)
    P_C = AP_("P_C", [128, 512], F32)
    P_D = AP_("P_D", [128, 512], F32)
    banks = [("A", P_A[:]), ("B", P_B[:]), ("C", P_C[:]), ("D", P_D[:])]

    EPS1 = cst_sb[0:1, 2:3]
    NEGH = cst_sb[0:1, 3:4]

    for dst, src in ((bz_sb, bzi), (bo_sb, boi), (sg_sb, sgi), (su_sb, sui),
                     (sd2_sb, sdi), (onesd_sb, onesdi), (ones1_sb, ones1i),
                     (cst_sb, csti)):
        dma(dst[:], src, sem="Si", throttle=False)
    v0 = p.v("Si")

    bank = {k: ("Ss", 0) for k in ("A", "B", "C", "D")}
    xld_cast = [("Sv", 0), ("Sv", 0)]
    wst_free = [("Sv", 0), ("Sv", 0)]
    wbf_free = [("St", 0), ("St", 0)]

    def xslice(t, c):
        off = (t * ncc + c) * 512
        return xn2[:, off:off + 512]

    for c in range(ncc):
        col = slice(c * 512, (c + 1) * 512)
        last_q = 0
        for t in range(16):
            par = t % 2
            sx = "Sx%d" % par
            ld = dma(_t(x_ld, par, 512), xt2[128 * t:128 * (t + 1), col],
                     [xld_cast[par]], sem=sx)
            cwv = p.emit("vector",
                         lambda e, t=t, par=par, c=c: e.tensor_copy(
                             xslice(t, c), _t(x_ld, par, 512)),
                         [(sx, ld)], "Sv")
            xld_cast[par] = ("Sv", cwv)
            sq = p.emit("vector",
                        lambda e, t=t, c=c: e.tensor_tensor(
                            xsq[:], xslice(t, c), xslice(t, c), AL.mult),
                        [("St", last_q)], "Sv")
            mm(P_A[0:1, :], onesd_sb[:], xslice(t, c), t == 0, t == 15,
               [("Sv", cwv)] + ([bank["A"], ("Si", v0)] if t == 0 else []))
            last_q = mm(P_B[0:1, :], onesd_sb[:], xsq[:], t == 0, t == 15,
                        [("Sv", sq)] + ([bank["B"]] if t == 0 else []))
        stats_done = p.v("St")
        c1 = vcp(mean_f[:], P_A[0:1, :], [("St", stats_done)])
        vcp(mean_bf[:], P_A[0:1, :])
        c2 = vcp(msq_f[:], P_B[0:1, :])
        vtt(m2_f[:], mean_f[:], mean_f[:], AL.mult, [])
        v_var = vtt(sd_f[:], msq_f[:], m2_f[:], AL.subtract, [])
        a_ln = act(rr_f[:], sd_f[:], AF.Ln, bias=EPS1,
                   waits=[("Sv", v_var)])
        a_rs = act(sd_f[:], rr_f[:], AF.Exp, scale=NEGH)
        v_rrb = vcp(rr_bf[:], sd_f[:], [("Ss", a_rs)])
        mm(P_A[:], ones1_sb[:], mean_bf[:], True, True,
           [("Sv", v_rrb), bank["A"]])
        rb = mm(P_B[:], ones1_sb[:], rr_bf[:], True, True, [bank["B"]])
        e_mb = sccp(mbc_sb[:], P_A[:], [("St", rb)])
        e_rb = sccp(rbc_sb[:], P_B[:], [])
        bank["A"] = ("Ss", e_mb)
        bank["B"] = ("Ss", e_rb)
        lnt = 0
        for t in range(16):
            par = t % 2
            sx = "Sx%d" % par
            ld2 = dma(_t(x_ld, par, 512), xt2[128 * t:128 * (t + 1), col],
                      [xld_cast[par]], sem=sx)
            w1 = [(sx, ld2), ("Ss", e_rb)]
            if t == 0:
                w1.append(("St", stats_done))
            lnt = vtt(tmpa_sb[:], _t(x_ld, par, 512), mbc_sb[:], AL.subtract,
                      w1)
            lnt = vtt(xslice(t, c), tmpa_sb[:], rbc_sb[:], AL.mult, [])
            xld_cast[par] = ("Sv", lnt)
    xn_done = p.v("Sv")

    def wload(src, mt, width, dt_in, par, waits=()):
        sw = "Sw%d" % par
        if dt_in == F32:
            stg = wst[:, par * 5632:par * 5632 + width]
        else:
            stg = wst[:, par * 5632:par * 5632 + width].bitcast(I32)
        ld = dma(stg, src[:, mt * width:mt * width + width],
                 [wst_free[par]] + list(waits), sem=sw)
        dstb = wbf[:, par * 5632:par * 5632 + width]
        if dt_in == F32:
            cv = p.emit("vector",
                        lambda e, o=dstb, i=stg: e.tensor_copy(o, i),
                        [(sw, ld), wbf_free[par]], "Sv")
        else:
            cv = p.emit("vector",
                        lambda e, o=dstb, i=stg: e.tensor_scalar(
                            o, i, 128.0, None, AL.subtract),
                        [(sw, ld), wbf_free[par]], "Sv")
        wst_free[par] = ("Sv", cv)
        return dstb, cv

    spc_free = ("St", 0)
    ypt_free = [("Sv", 0), ("Sv", 0)]
    szt_free = ("Sv", 0)
    st2_free = [("Sst0", 0), ("Sst1", 0)]
    for c in range(ncc):
        for m in range(32):
            par = m % 2
            dstb, cv = wload(wz, m, 2048, F32, par,
                             [("Sv", xn_done)] if (c == 0 and m < 2) else ())
            nm, pt_ = banks[m % 4]
            last = 0
            for k in range(16):
                last = mm(pt_, dstb[:, k * 128:k * 128 + 128], xslice(k, c),
                          k == 0, k == 15,
                          [("Sv", cv), bank[nm]] if k == 0 else [])
            wbf_free[par] = ("St", last)
            a_sz = act(szt[:], pt_, AF.Silu, bias=bz_sb[:, m:m + 1],
                       waits=[("St", last), szt_free])
            bank[nm] = ("Ss", a_sz)
            ldy = dma(_t(ypt, par, 512),
                      yp2[128 * m:128 * m + 128, c * 512:c * 512 + 512],
                      [ypt_free[par]], sem="Sy%d" % par)
            sv = vtt(_t(spc, m, 512), _t(ypt, par, 512), szt[:], AL.mult,
                     [("Ss", a_sz), ("Sy%d" % par, ldy), spc_free] if m == 0
                     else [("Ss", a_sz), ("Sy%d" % par, ldy)])
            ypt_free[par] = ("Sv", sv)
            szt_free = ("Sv", sv)
        sp_done = p.v("Sv")
        if dbg:
            for mm_ in range(32):
                dma(sp_d[128 * mm_:128 * mm_ + 128, c * 512:c * 512 + 512],
                    _t(spc, mm_, 512), [("Sv", sp_done)], sem="So",
                    throttle=False)
        for m in range(16):
            par = m % 2
            dstb, cv = wload(wo, m, 4096, F32, par)
            nm, pt_ = banks[m % 4]
            last = 0
            for k in range(32):
                last = mm(pt_, dstb[:, k * 128:k * 128 + 128],
                          _t(spc, k, 512), k == 0, k == 31,
                          [("Sv", max(cv, sp_done)), bank[nm]] if k == 0
                          else [])
            wbf_free[par] = ("St", last)
            ev = act(_t(st2_sb, par, 512), pt_, AF.Identity,
                     bias=bo_sb[:, m:m + 1],
                     waits=[("St", last), st2_free[par]])
            bank[nm] = ("Ss", ev)
            sd_ = dma(s_dr[128 * m:128 * m + 128, c * 512:c * 512 + 512],
                      _t(st2_sb, par, 512), [("Ss", ev)], sem="Sst%d" % par)
            st2_free[par] = ("Sst%d" % par, sd_)
        spc_free = ("St", p.v("St"))

    pt_free = [("Sp0", 0), ("Sp1", 0)]
    gu_free = ("Sv", 0)
    for hm in range(44):
        dg, cg = wload(wg, hm, 2048, I32, 0)
        du, cu = wload(wu, hm, 2048, I32, 1)
        lg = lu = 0
        for c in range(ncc):
            for k in range(16):
                lg = mm(P_A[:], dg[:, k * 128:k * 128 + 128], xslice(k, c),
                        k == 0, k == 15,
                        [("Sv", cg), bank["A"]] if k == 0 else [])
            for k in range(16):
                lu = mm(P_B[:], du[:, k * 128:k * 128 + 128], xslice(k, c),
                        k == 0, k == 15,
                        [("Sv", cu), bank["B"]] if k == 0 else [])
            ag = act(gt_sb[:], P_A[:], AF.Silu, scale=sg_sb[:, hm:hm + 1],
                     waits=[("St", lg), gu_free])
            au = act(ut_sb[:], P_B[:], AF.Identity, scale=su_sb[:, hm:hm + 1],
                     waits=[("St", lu)])
            bank["A"] = ("Ss", ag)
            bank["B"] = ("Ss", au)
            ppar = c % 2
            pv = vtt(_t(pt_sb, ppar, 512), gt_sb[:], ut_sb[:], AL.mult,
                     [("Ss", au), pt_free[ppar]])
            gu_free = ("Sv", pv)
            pd = dma(p_dr[128 * hm:128 * hm + 128, c * 512:c * 512 + 512],
                     _t(pt_sb, ppar, 512), [("Sv", pv)], sem="Sp%d" % ppar)
            pt_free[ppar] = ("Sp%d" % ppar, pd)
        wbf_free[0] = ("St", lg)
        wbf_free[1] = ("St", lu)

    for m in range(16):
        par = m % 2
        dstb, cv = wload(wd, m, 44 * 128, I32, par)
        wds = dma(wd_dr[:, m * 44 * 128:(m + 1) * 44 * 128], dstb,
                  [("Sv", cv)], sem="Swd%d" % par)
        wbf_free[par] = ("Swd%d" % par, wds)

    p_stored = [("Sp0", p.v("Sp0")), ("Sp1", p.v("Sp1"))]
    wd_stored = [("Swd0", p.v("Swd0")), ("Swd1", p.v("Swd1"))]
    s_stored = [("Sst0", p.v("Sst0")), ("Sst1", p.v("Sst1"))]
    pch_free = ("St", p.v("St"))
    g_done = ("Sv", 0)
    sgl_free = [("Sv", 0), ("Sv", 0)]
    xtl_free = [("Sv", 0), ("Sv", 0)]
    ot_free = [("So0", 0), ("So1", 0)]
    for c3 in range(t2 // 1024):
        ldp = 0
        for hh in range(44):
            first = hh == 0
            ldp = dma(_t(pch, hh, 1024),
                      p_dr[128 * hh:128 * hh + 128,
                           c3 * 1024:c3 * 1024 + 1024],
                      ([pch_free] + (p_stored if c3 == 0 else []))
                      if first else [],
                      sem="Spl", throttle=first)
        for m in range(16):
            par = m % 2
            stg = wst[:, par * 5632:par * 5632 + 2816].bitcast(BF16)
            ldw = dma(stg, wd_dr[:, m * 44 * 128:(m + 1) * 44 * 128],
                      [wst_free[par]] +
                      (wd_stored if (c3 == 0 and m < 2) else []),
                      sem="Sw%d" % par)
            for half in range(2):
                nm, pt_ = banks[(m * 2 + half) % 4]
                last = 0
                for k in range(44):
                    last = mm(pt_, stg[:, k * 128:k * 128 + 128],
                              pch[:, (k * 2 + half) * 512:
                                  (k * 2 + half) * 512 + 512],
                              k == 0, k == 43,
                              [("Sw%d" % par, ldw), ("Spl", ldp), bank[nm]]
                              if k == 0 else [])
                ag = act(g2_sb[:], pt_, AF.Silu, scale=sd2_sb[:, m:m + 1],
                         waits=[("St", last), g_done])
                bank[nm] = ("Ss", ag)
                colh = slice(c3 * 1024 + half * 512,
                             c3 * 1024 + half * 512 + 512)
                if dbg:
                    dma(g_d[128 * m:128 * m + 128, colh], g2_sb[:],
                        [("Ss", ag)], sem="So", throttle=False)
                lds = dma(_t(sgl_sb, half, 512),
                          s_dr[128 * m:128 * m + 128, colh],
                          [sgl_free[half]] +
                          (s_stored if (c3 == 0 and m == 0) else []),
                          sem="Sl%d" % half)
                ldx = dma(_t(xtl_sb, half, 512),
                          xt2[128 * m:128 * m + 128, colh],
                          [xtl_free[half]], sem="Sx%d" % half)
                v1 = vtt(_t(ot_sb, half, 512), g2_sb[:],
                         _t(sgl_sb, half, 512), AL.mult,
                         [("Ss", ag), ("Sl%d" % half, lds),
                          ("Sx%d" % half, ldx), ot_free[half]])
                g_done = ("Sv", v1)
                sgl_free[half] = ("Sv", v1)
                v2 = vtt(_t(ot_sb, half, 512), _t(ot_sb, half, 512),
                         _t(xtl_sb, half, 512), AL.add, [])
                xtl_free[half] = ("Sv", v2)
                od = dma(outt[128 * m:128 * m + 128, colh],
                         _t(ot_sb, half, 512), [("Sv", v2)],
                         sem="So%d" % half)
                ot_free[half] = ("So%d" % half, od)
            wst_free[par] = ("St", p.v("St"))
        pch_free = ("St", p.v("St"))

    p.emit("sync", lambda e: e.nop(),
           [("So0", p.v("So0")), ("So1", p.v("So1")), ("So", p.v("So"))],
           None)
    p.build()
    return nc



def _tile_w(wT):
    K, M = wT.shape
    nk, nm = K // 128, M // 128
    a = wT.reshape(nk, 128, nm, 128).transpose(1, 2, 0, 3).reshape(128, -1)
    return np.ascontiguousarray(a)


def _pp(vec):
    n = vec.shape[0] // 128
    return np.ascontiguousarray(vec.reshape(n, 128).T)


def _mk_e8():
    e = np.zeros((128, 512), np.float32)
    for g4 in range(4):
        for v in range(4):
            for dd in range(8):
                for s in range(16):
                    e[32 * g4 + 8 * v + dd, 128 * v + 16 * dd + s] = 1.0
    return e


def _mk_e16():
    e = np.zeros((128, 256), np.float32)
    for g4 in range(4):
        for v in range(2):
            for dd in range(8):
                for s in range(16):
                    e[32 * g4 + 16 * v + s, 128 * v + 16 * dd + s] = 1.0
    return e


def _mk_smat():
    s = np.zeros((128, 128), np.float32)
    for m in range(4):
        for pp_ in range(128):
            s[pp_, 32 * m + 8 * m + pp_ // 16] = 1.0
    return s


def _mk_alog_ea(alog_slice):
    out = np.full((128, 4096), -1e30, np.float32)
    for j in range(128):
        g4 = (j % 16) // 4
        eidx = 4 * (j // 16) + (j % 4)
        r8 = 8 * (j % 4)
        for dd in range(8):
            out[32 * g4 + r8 + dd,
                128 * eidx + 16 * dd:128 * eidx + 16 * dd + 16] = \
                alog_slice[8 * j + dd, :]
    return out


_CACHE = {}


def kernel(**inp):
    x = np.ascontiguousarray(np.asarray(inp["L0_attn_hidden_states"],
                                        np.float32))
    Wx = np.asarray(inp["L0_attn_Wx"], np.float32)
    bx = np.asarray(inp["L0_attn_bx"], np.float32)
    Wz = np.asarray(inp["L0_attn_Wz"], np.float32)
    bz = np.asarray(inp["L0_attn_bz"], np.float32)
    conv_w = np.asarray(inp["L0_attn_conv_w"], np.float32)
    conv_b = np.asarray(inp["L0_attn_conv_b"], np.float32)
    WB = np.asarray(inp["L0_attn_WB"], np.float32)
    WC = np.asarray(inp["L0_attn_WC"], np.float32)
    Wdtd = np.asarray(inp["L0_attn_Wdt_down"], np.float32)
    Wdt = np.asarray(inp["L0_attn_Wdt"], np.float32)
    bdt = np.asarray(inp["L0_attn_bdt"], np.float32)
    A_log = np.asarray(inp["L0_attn_A_log"], np.float32)
    Dv = np.asarray(inp["L0_attn_D"], np.float32)
    Wout = np.asarray(inp["L0_attn_Wout"], np.float32)
    bout = np.asarray(inp["L0_attn_bout"], np.float32)
    wqg = np.asarray(inp["L0_ffn_wq_gate"], np.int32)
    sg = np.asarray(inp["L0_ffn_s_gate"], np.float32)
    wqu = np.asarray(inp["L0_ffn_wq_up"], np.int32)
    su = np.asarray(inp["L0_ffn_s_up"], np.float32)
    wqd = np.asarray(inp["L0_ffn_wq_down"], np.int32)
    sdn = np.asarray(inp["L0_ffn_s_down"], np.float32)

    if "p1" not in _CACHE:
        _CACHE["p1"] = build_p1()
        _CACHE["p2"] = build_p2()
    nc1, nc2_ = _CACHE["p1"], _CACHE["p2"]

    e8 = _mk_e8()
    e16 = _mk_e16()
    e16b = e16.astype(BFNP)
    smat = _mk_smat().astype(BFNP)
    cst = np.tile(np.array([[-1.0, 1.0, 1e-5, -0.5]], np.float32), (128, 1))
    onesd = np.full((128, 1), 1.0 / 2048.0, BFNP)
    ones1 = np.ones((1, 128), BFNP)

    xts = [np.ascontiguousarray(x[b].T) for b in range(2)]

    in1 = []
    for core in range(8):
        b, q = core // 4, core % 4
        sl1 = slice(DSL * q, DSL * (q + 1))
        sl4 = slice(256 * q, 256 * (q + 1))
        in1.append({
            "xt": xts[b],
            "wx": _tile_w(np.ascontiguousarray(Wx[sl4].T)),
            "wbm": _tile_w(np.ascontiguousarray(WB[sl4].T)),
            "wc": _tile_w(np.ascontiguousarray(WC[sl1].T)),
            "wdtd": _tile_w(np.ascontiguousarray(Wdtd.T)),
            "wdt": _tile_w(np.ascontiguousarray(Wdt[sl1].T)),
            "alog": _mk_alog_ea(A_log[sl1]),
            "e8": e8, "e16": e16, "e16b": e16b, "smat": smat,
            "cw": np.ascontiguousarray(
                conv_w[sl1, 0, :].reshape(8, 128, 4).transpose(1, 0, 2)
                .reshape(128, 32)),
            "cb": _pp(conv_b[sl1]),
            "bx": _pp(bx[sl4]),
            "bdt": _pp(bdt[sl1]),
            "dvec": _pp(Dv[sl1]),
            "cst": cst, "onesd": onesd, "ones1": ones1,
        })
    r1 = run_bass_kernel_spmd(nc1, in1, core_ids=list(range(8)))
    ypl = [np.asarray(r1.results[i]["yp"]) for i in range(8)]

    wz_t = _tile_w(np.ascontiguousarray(Wz.T))
    wo_t = _tile_w(np.ascontiguousarray(Wout.T))
    wg_t = _tile_w(np.ascontiguousarray(wqg.T))
    wu_t = _tile_w(np.ascontiguousarray(wqu.T))
    wd_t = _tile_w(np.ascontiguousarray(wqd.T))
    bz_p = _pp(bz)
    bo_p = _pp(bout)
    sg_p = _pp(sg)
    su_p = _pp(su)
    sd_p = _pp(sdn)

    in2 = []
    for core in range(8):
        b, q = core // 4, core % 4
        ybf = np.concatenate([ypl[4 * b + qq] for qq in range(4)], axis=0)
        in2.append({
            "xt2": np.ascontiguousarray(xts[b][:, T2 * q:T2 * (q + 1)]),
            "yp2": np.ascontiguousarray(ybf[:, T2 * q:T2 * (q + 1)]),
            "wz": wz_t, "wo": wo_t, "wg": wg_t, "wu": wu_t, "wd": wd_t,
            "bz": bz_p, "bo": bo_p, "sg": sg_p, "su": su_p, "sdn": sd_p,
            "onesd": onesd, "ones1": ones1, "cst": cst,
        })
    r2 = run_bass_kernel_spmd(nc2_, in2, core_ids=list(range(8)))

    out = np.empty((2, L, DM), np.float32)
    for core in range(8):
        b, q = core // 4, core % 4
        out[b, T2 * q:T2 * (q + 1), :] = np.asarray(
            r2.results[core]["outt"]).T
    return out


# version 06
# speedup vs baseline: 1.1540x; 1.1540x over previous
import numpy as np
import ml_dtypes
import concourse.bass as bass
import concourse.mybir as mybir
from concourse.bass_utils import run_bass_kernel_spmd

F32 = mybir.dt.float32
F32R = mybir.dt.float32r
BF16 = mybir.dt.bfloat16
I32 = mybir.dt.int32
AL = mybir.AluOpType
AF = mybir.ActivationFunctionType
BFNP = ml_dtypes.bfloat16

DM = 2048
DIN = 4096
DSL = 1024
L = 8192
TC = 512
T2 = 2048
NT = 128
HID = 5632

ENGINES = ("sync", "tensor", "scalar", "vector", "gpsimd")


class Prog:

    def __init__(self, nc):
        self.nc = nc
        self.ops = {e: [] for e in ENGINES}
        self.cnt = {}
        self.sems = {}

    def sem(self, name):
        if name not in self.sems:
            self.sems[name] = self.nc.alloc_semaphore(name)
            self.cnt[name] = 0
        return self.sems[name]

    def emit(self, eng, fn, waits=(), inc=None, incn=1):
        waits = tuple((s, v) for (s, v) in waits if v > 0)
        for s, v in waits:
            assert v <= self.cnt[s], (eng, s, v, self.cnt[s])
        self.ops[eng].append((fn, waits, inc, incn))
        if inc is not None:
            self.cnt[inc] += incn
            return self.cnt[inc]
        return None

    def v(self, name):
        return self.cnt.get(name, 0)

    def build(self):
        sems = self.sems
        with self.nc.Block() as blk:
            for ename in ENGINES:
                ops = self.ops[ename]
                if not ops:
                    continue

                def make(ops=ops):
                    def body(eng):
                        for fn, waits, inc, incn in ops:
                            for s, v in waits:
                                eng.wait_ge(sems[s], v)
                            inst = fn(eng)
                            if inc is not None:
                                inst.then_inc(sems[inc], incn)
                    return body

                getattr(blk, ename)(make())


def _t(h, i, w):
    return h[:, i * w:(i + 1) * w]


class Common:
    def __init__(self, p):
        self.p = p

    def dma(self, dst, src, waits=(), sem="Sd", throttle=True):
        w = list(waits)
        if throttle:
            w.append((sem, self.p.v(sem)))
        return self.p.emit("sync", lambda e, d=dst, s=src: e.dma_start(d, s),
                           w, sem, 16)

    def mm(self, pt, lh, rh, st, sp, waits=(), tp=None, skip=False):
        def f(e, pt=pt, lh=lh, rh=rh, st=st, sp=sp, tp=tp, skip=skip):
            return e.matmul(pt, lh, rh, start=st, stop=sp, tile_position=tp,
                            skip_group_check=skip)
        return self.p.emit("tensor", f, waits, "St")

    def act(self, out, in_, fn, bias=0.0, scale=1.0, waits=()):
        def f(e, out=out, in_=in_, fn=fn, bias=bias, scale=scale):
            return e.activation(out, in_, fn, bias=bias, scale=scale)
        return self.p.emit("scalar", f, waits, "Ss")

    def sccp(self, out, in_, waits=()):
        return self.p.emit("scalar", lambda e, o=out, i=in_: e.copy(o, i),
                           waits, "Ss")

    def vtt(self, out, a, b, op, waits=()):
        return self.p.emit("vector", lambda e, o=out, a=a, b=b, op=op:
                           e.tensor_tensor(o, a, b, op), waits, "Sv")

    def vcp(self, out, in_, waits=()):
        return self.p.emit("vector", lambda e, o=out, i=in_:
                           e.tensor_copy(o, i), waits, "Sv")




def build_p1(l=L, dbg=False, barrier=False):
    nch = l // TC
    nc = bass.Bass("TRN2", target_bir_lowering=False,
                   dynamic_dma_scratch_size=2048)
    p = Prog(nc)
    for s in ("Sd", "St", "Ss", "Sv", "Sg", "So",
              "Si", "Swc", "Sx0", "Sx1", "Sgx0", "Sgx1"):
        p.sem(s)
    C = Common(p)
    dma, mm, act, sccp, vtt, vcp = C.dma, C.mm, C.act, C.sccp, C.vtt, C.vcp

    DT = nc.dram_tensor
    xt = DT("xt", [DM, l], F32, kind="ExternalInput").ap()
    wx = DT("wx", [128, 2 * 16 * 128], F32, kind="ExternalInput").ap()
    wbm = DT("wbm", [128, 2 * 16 * 128], F32, kind="ExternalInput").ap()
    wc = DT("wc", [128, 8 * 16 * 128], F32, kind="ExternalInput").ap()
    wdtd = DT("wdtd", [128, 16 * 128], F32, kind="ExternalInput").ap()
    wdt = DT("wdt", [128, 8 * 128], F32, kind="ExternalInput").ap()
    alog = DT("alog", [128, 4096], F32, kind="ExternalInput").ap()
    e8i = DT("e8", [128, 512], F32R, kind="ExternalInput").ap()
    e16i = DT("e16", [128, 256], F32R, kind="ExternalInput").ap()
    e16bi = DT("e16b", [128, 256], BF16, kind="ExternalInput").ap()
    smati = DT("smat", [128, 128], BF16, kind="ExternalInput").ap()
    cwi = DT("cw", [128, 32], F32, kind="ExternalInput").ap()
    cbi = DT("cb", [128, 8], F32, kind="ExternalInput").ap()
    bxi = DT("bx", [128, 2], F32, kind="ExternalInput").ap()
    bdti = DT("bdt", [128, 8], F32, kind="ExternalInput").ap()
    dvi = DT("dvec", [128, 8], F32, kind="ExternalInput").ap()
    csti = DT("cst", [128, 4], F32, kind="ExternalInput").ap()
    onesdi = DT("onesd", [128, 1], BF16, kind="ExternalInput").ap()
    ones1i = DT("ones1", [1, 128], BF16, kind="ExternalInput").ap()
    yp = DT("yp", [DSL, l], F32, kind="ExternalOutput").ap()
    if dbg:
        xn_d = DT("xn_d", [128, 16 * l], BF16, kind="ExternalOutput").ap()
        dl_d = DT("dl_d", [128, 8 * l], F32, kind="ExternalOutput").ap()
        u_d = DT("u_d", [128, 8 * l], F32, kind="ExternalOutput").ap()
        b_d = DT("b_d", [128, 2 * l], F32, kind="ExternalOutput").ap()
        c_d = DT("c_d", [128, 8 * l], BF16, kind="ExternalOutput").ap()
        h_d = DT("h_d", [128, 4 * l], F32, kind="ExternalOutput").ap()

    A = nc.alloc_sbuf_tensor
    wx_sb = A("wx_sb", [128, 2 * 16 * 128], BF16)
    wb_sb = A("wb_sb", [128, 2 * 16 * 128], BF16)
    wdtd_sb = A("wdtd_sb", [128, 16 * 128], BF16)
    wdt_sb = A("wdt_sb", [128, 8 * 128], F32R)
    wcb_sb = A("wcb_sb", [128, 2 * 2048], BF16)
    ea_sb = A("ea_sb", [128, 4096], F32R)
    e8_sb = A("e8_sb", [128, 512], F32R)
    e16_sb = A("e16_sb", [128, 256], F32R)
    e16b_sb = A("e16b_sb", [128, 256], BF16)
    s_sb = A("s_sb", [128, 128], BF16)
    cw_sb = A("cw_sb", [128, 32], F32)
    cb_sb = A("cb_sb", [128, 8], F32)
    bx_sb = A("bx_sb", [128, 2], F32)
    bdt2_sb = A("bdt2_sb", [128, 8], F32)
    d_sb = A("d_sb", [128, 8], F32)
    cst_sb = A("cst_sb", [128, 4], F32)
    onesd_sb = A("onesd_sb", [128, 1], BF16)
    ones1_sb = A("ones1_sb", [1, 128], BF16)
    x_ld = A("x_ld", [128, 2 * 512], F32)
    xw = A("xw", [128, 16 * 512], BF16)
    xsq = A("xsq", [128, 512], BF16)
    mean_f = A("mean_f", [1, 512], F32)
    msq_f = A("msq_f", [1, 512], F32)
    m2_f = A("m2_f", [1, 512], F32)
    sd_f = A("sd_f", [1, 512], F32)
    rr_f = A("rr_f", [1, 512], F32)
    mean_bf = A("mean_bf", [1, 512], BF16)
    rr_bf = A("rr_bf", [1, 512], BF16)
    mbc_sb = A("mbc_sb", [128, 512], F32)
    rbc_sb = A("rbc_sb", [128, 512], F32)
    dt1_sb = A("dt1_sb", [128, 512], F32R)
    dl_sb = A("dl_sb", [128, 8 * 512], F32R)
    u_sb = A("u_sb", [128, 8 * 512], F32R)
    wcs_sb = A("wcs_sb", [128, 1024], F32)
    w_sb = A("w_sb", [128, 8 * 512], F32R)
    c_sb = A("c_sb", [128, 8 * 512], BF16)
    b_sb = A("b_sb", [128, 2 * 512], F32R)
    xb_sb = A("xb_sb", [128, 2 * 515], F32)
    xbc_sb = A("xbc_sb", [128, 515], F32)
    tmpa_sb = A("tmpa_sb", [128, 512], F32)
    tmpb_sb = A("tmpb_sb", [128, 512], F32)
    acc_sb = A("acc_sb", [128, 2 * 512], F32)
    da_sb = A("da_sb", [128, 2 * 1024], F32)
    dbu_sb = A("dbu_sb", [128, 1024], F32)
    hp_sb = A("hp_sb", [128, 4 * 512], F32)
    st_sb = A("st_sb", [128, 128], F32)
    be_sb = A("be_sb", [128, 2 * 512], F32)
    ce_sb = A("ce_sb", [128, 2 * 512], F32)
    ch_sb = A("ch_sb", [128, 2 * 1024], BF16)
    yps_sb = A("yps_sb", [128, 512], F32)
    fence_sb = A("fence_sb", [128, 16], F32)

    AP_ = nc.alloc_psum_tensor
    P_AD = AP_("P_AD", [128, 2 * 512], F32)
    P_WE = AP_("P_WE", [128, 2 * 512], F32)
    P_CE = AP_("P_CE", [128, 512], F32)
    P_Y = AP_("P_Y", [128, 512], F32)
    P_PR = AP_("P_PR", [128, 512], F32)
    P_XE = AP_("P_XE", [128, 512], F32)

    NEG1 = cst_sb[:, 0:1]
    ONE = cst_sb[:, 1:2]
    EPS1 = cst_sb[0:1, 2:3]
    NEGH = cst_sb[0:1, 3:4]

    for dst, src in ((cw_sb, cwi), (cb_sb, cbi), (bx_sb, bxi),
                     (bdt2_sb, bdti), (d_sb, dvi), (cst_sb, csti),
                     (e8_sb, e8i), (e16_sb, e16i), (e16b_sb, e16bi),
                     (s_sb, smati), (onesd_sb, onesdi), (ones1_sb, ones1i)):
        dma(dst[:], src, sem="Si", throttle=False)
    v0 = p.v("Si")
    bdt2v = p.emit("vector",
                   lambda e: e.tensor_scalar_mul(bdt2_sb[:], bdt2_sb[:], 2.0),
                   [("Si", v0)], "Sv")
    for (src, dst, ncols) in ((wx, wx_sb, 2 * 16), (wbm, wb_sb, 2 * 16),
                              (wdtd, wdtd_sb, 16), (wdt, wdt_sb, 8)):
        total = ncols * 128
        off = 0
        while off < total:
            n = min(1024, total - off)
            stg = wcs_sb[:, 0:n]
            ld = dma(stg, src[:, off:off + n], [("Sv", p.v("Sv"))], sem="Swc")
            p.emit("vector",
                   lambda e, o=dst[:, off:off + n], i=stg: e.tensor_copy(o, i),
                   [("Swc", ld)], "Sv")
            off += n
    for kk in range(8):
        ld = dma(da_sb[:, 0:512], alog[:, kk * 512:(kk + 1) * 512],
                 [("Sv", p.v("Sv")), ("Ss", p.v("Ss"))], sem="Swc")
        a1 = act(da_sb[:, 512:1024], da_sb[:, 0:512], AF.Exp,
                 waits=[("Swc", ld)])
        p.emit("vector",
               lambda e, kk=kk: e.tensor_scalar_mul(
                   _t(ea_sb, kk, 512), da_sb[:, 512:1024], -1.0),
               [("Ss", a1)], "Sv")
    p.emit("vector", lambda e: e.memset(st_sb[:], 0.0), [], "Sv")
    p.emit("vector", lambda e: e.memset(xb_sb[:, 0:3], 0.0), [], "Sv")
    p.emit("vector", lambda e: e.memset(xb_sb[:, 515:518], 0.0), [], "Sv")

    bank = {k: ("Ss", 0) for k in ("AD", "WE", "CE", "Y", "PR", "XE")}
    xw_free = ("St", 0)
    dl_free = [("St", 0)] * 8
    w_free = [("St", 0)] * 8
    c_free = [("St", 0)] * 8
    b_free = [("St", 0)] * 2
    u_freev = [("Sv", 0)] * 8
    xld_cast = [("Sv", 0), ("Sv", 0)]
    yps_free = ("So", 0)
    da_free = [("Sv", 0), ("Sv", 0)]
    we_freed = ("Sv", 0)
    ce_freed = [("Sg", 0), ("Sg", 0)]
    ch_free = [("St", 0), ("St", 0)]
    be_freed = [("Sv", 0), ("Sv", 0)]
    xbc_free = ("Sv", 0)
    tmp_free = ("Sv", 0)
    acc_free = ("Ss", 0)
    xb_read = [("Sgx0", 0), ("Sgx1", 0)]
    halo_done = ("Ss", 0)
    wcb_free = [("St", 0), ("St", 0)]
    wcs_free = ("Sv", 0)
    slot_free = {0: ("Sg", 0), 1: ("Sg", 0), 2: ("Sg", 0), 3: ("Sg", 0)}
    projb = [("PR", P_PR[:]), ("XE", P_XE[:])]
    pb_i = [0]

    def nb():
        nm, pt = projb[pb_i[0] % 2]
        pb_i[0] += 1
        return nm, pt

    for c in range(nch):
        if barrier and c > 0:
            bvals = [(sn, p.v(sn)) for sn in p.sems]
            for en in ENGINES:
                p.emit(en, lambda e: e.nop(), bvals, None)
        col = slice(c * TC, (c + 1) * TC)
        last_q = 0
        for t in range(16):
            par = t % 2
            sx = "Sx%d" % par
            ld = dma(_t(x_ld, par, 512), xt[128 * t:128 * (t + 1), col],
                     [xld_cast[par]], sem=sx)
            cwv = p.emit("vector",
                         lambda e, t=t, par=par: e.tensor_copy(
                             _t(xw, t, 512), _t(x_ld, par, 512)),
                         [(sx, ld), xw_free] if t == 0 else [(sx, ld)],
                         "Sv")
            xld_cast[par] = ("Sv", cwv)
            sq = p.emit("vector",
                        lambda e, t=t: e.tensor_tensor(
                            xsq[:], _t(xw, t, 512), _t(xw, t, 512), AL.mult),
                        [("St", last_q)], "Sv")
            mm(P_PR[0:1, :], onesd_sb[:], _t(xw, t, 512), t == 0, t == 15,
               [("Sv", cwv)] + ([bank["PR"], ("Si", v0)] if t == 0 else []))
            last_q = mm(P_XE[0:1, :], onesd_sb[:], xsq[:], t == 0, t == 15,
                        [("Sv", sq)] + ([bank["XE"]] if t == 0 else []))
        stats_done = p.v("St")
        c1 = vcp(mean_f[:], P_PR[0:1, :], [("St", stats_done)])
        vcp(mean_bf[:], P_PR[0:1, :])
        c2 = vcp(msq_f[:], P_XE[0:1, :])
        vtt(m2_f[:], mean_f[:], mean_f[:], AL.mult, [])
        v_var = vtt(sd_f[:], msq_f[:], m2_f[:], AL.subtract, [])
        a_ln = act(rr_f[:], sd_f[:], AF.Ln, bias=EPS1,
                   waits=[("Sv", v_var)])
        a_rs = act(sd_f[:], rr_f[:], AF.Exp, scale=NEGH)
        v_rrb = vcp(rr_bf[:], sd_f[:], [("Ss", a_rs)])
        mm(P_PR[:], ones1_sb[:], mean_bf[:], True, True,
           [("Sv", v_rrb), bank["PR"]])
        rb = mm(P_XE[:], ones1_sb[:], rr_bf[:], True, True, [bank["XE"]])
        e_mb = sccp(mbc_sb[:], P_PR[:], [("St", rb)])
        e_rb = sccp(rbc_sb[:], P_XE[:], [])
        bank["PR"] = ("Ss", e_mb)
        bank["XE"] = ("Ss", e_rb)
        lnt = 0
        for t in range(16):
            par = t % 2
            sx = "Sx%d" % par
            ld2 = dma(_t(x_ld, par, 512), xt[128 * t:128 * (t + 1), col],
                      [xld_cast[par]], sem=sx)
            w1 = [(sx, ld2), ("Ss", e_rb), tmp_free]
            if t == 0:
                w1.append(("St", stats_done))
            lnt = vtt(tmpa_sb[:], _t(x_ld, par, 512), mbc_sb[:], AL.subtract,
                      w1 if t == 0 else [(sx, ld2)])
            lnt = vtt(_t(xw, t, 512), tmpa_sb[:], rbc_sb[:], AL.mult, [])
            xld_cast[par] = ("Sv", lnt)
        tmp_free = ("Sv", lnt)
        xn_done = p.v("Sv")

        nm, pt = nb()
        lastm = 0
        for k in range(16):
            lastm = mm(pt, wdtd_sb[:, k * 128:k * 128 + 128], _t(xw, k, 512),
                       k == 0, k == 15,
                       [bank[nm], ("Sv", xn_done)] if k == 0 else [])
        e1 = sccp(dt1_sb[:], pt, [("St", lastm)])
        bank[nm] = ("Ss", e1)

        def proj16(w_handle, mt, extra, nk=16):
            nm_, pt_ = nb()
            last_ = 0
            for k in range(nk):
                lh = w_handle[:, (mt * nk + k) * 128:(mt * nk + k) * 128 + 128]
                last_ = mm(pt_, lh, _t(xw, k, 512), k == 0, k == nk - 1,
                           ([bank[nm_]] + extra) if k == 0 else [])
            return nm_, pt_, last_

        def emit_delta(m):
            nonlocal tmp_free
            nm_, pt_ = nb()
            mv = mm(pt_, wdt_sb[:, 128 * m:128 * m + 128], dt1_sb[:],
                    True, True, [bank[nm_], ("Ss", e1), dl_free[m]])
            bias = bdt2_sb[:, m:m + 1]
            act(tmpa_sb[:], pt_, AF.Abs, bias=bias,
                waits=[("St", mv), tmp_free])
            act(tmpb_sb[:], tmpa_sb[:], AF.Exp, scale=NEG1)
            act(tmpa_sb[:], tmpb_sb[:], AF.Ln, bias=ONE)
            a4 = act(tmpb_sb[:], pt_, AF.Relu, bias=bias)
            bank[nm_] = ("Ss", a4)
            dv_ = vtt(_t(dl_sb, m, 512), tmpa_sb[:], tmpb_sb[:], AL.add,
                      [("Ss", a4)])
            tmp_free = ("Sv", dv_)
            return dv_

        def emit_cproj(m):
            nonlocal wcs_free
            wpar = m % 2
            for hh in range(2):
                ldw = dma(wcs_sb[:], wc[:, m * 2048 + 1024 * hh:
                                        m * 2048 + 1024 * hh + 1024],
                          [wcs_free], sem="Swc")
                cv = p.emit("vector",
                            lambda e, wpar=wpar, hh=hh: e.tensor_copy(
                                wcb_sb[:, wpar * 2048 + 1024 * hh:
                                       wpar * 2048 + 1024 * hh + 1024],
                                wcs_sb[:]),
                            [("Swc", ldw)] +
                            ([wcb_free[wpar]] if hh == 0 else []), "Sv")
                wcs_free = ("Sv", cv)
            nm_, pt_ = nb()
            last_ = 0
            for k in range(16):
                last_ = mm(pt_, wcb_sb[:, wpar * 2048 + k * 128:
                                       wpar * 2048 + k * 128 + 128],
                           _t(xw, k, 512), k == 0, k == 15,
                           [bank[nm_], ("Sv", cv), c_free[m]] if k == 0
                           else [])
            wcb_free[wpar] = ("St", last_)
            ev_ = sccp(_t(c_sb, m, 512), pt_, [("St", last_)])
            bank[nm_] = ("Ss", ev_)
            return ev_

        def emit_xb_b(m):
            nm_, pt_, lastm_ = proj16(wx_sb, m, [halo_done] if m == 0 else [])
            ev_ = act(xb_sb[:, 515 * m + 3:515 * m + 515], pt_, AF.Identity,
                      bias=bx_sb[:, m:m + 1],
                      waits=[("St", lastm_)] + xb_read)
            bank[nm_] = ("Ss", ev_)
            nm_, pt_, lastm_ = proj16(wb_sb, m, [b_free[m]])
            ev_ = sccp(_t(b_sb, m, 512), pt_, [("St", lastm_)])
            bank[nm_] = ("Ss", ev_)
            return ev_

        def emit_conv(k):
            nonlocal xbc_free, acc_free
            bt, o = k // 4, (32 * k) % 128
            sgx = "Sgx%d" % (k % 2)
            g_ = 0
            for hb in range(2):
                for rr in range(4):
                    first = hb == 0 and rr == 0
                    g_ = dma(xbc_sb[hb * 64 + rr * 16:hb * 64 + rr * 16 + 16,
                                    :],
                             xb_sb[o + 16 * hb:o + 16 * hb + 16,
                                   515 * bt:515 * bt + 515],
                             [("Ss", p.v("Ss")), xbc_free] if first else [],
                             sem=sgx, throttle=first)
            t0 = p.emit("vector",
                        lambda e, k=k: e.tensor_scalar(
                            _t(acc_sb, 0, 512), xbc_sb[:, 0:512],
                            cw_sb[:, 4 * k:4 * k + 1], None, AL.mult),
                        [(sgx, g_), acc_free], "Sv")
            for tp_ in range(1, 4):
                t0 = p.emit("vector",
                            lambda e, k=k, tp_=tp_: e.scalar_tensor_tensor(
                                _t(acc_sb, tp_ % 2, 512),
                                xbc_sb[:, tp_:tp_ + 512],
                                cw_sb[:, 4 * k + tp_:4 * k + tp_ + 1],
                                _t(acc_sb, (tp_ + 1) % 2, 512),
                                AL.mult, AL.add),
                            [], "Sv")
            xbc_free = ("Sv", t0)
            su = act(_t(u_sb, k, 512), _t(acc_sb, 1, 512), AF.Silu,
                     bias=cb_sb[:, k:k + 1],
                     waits=[("Sv", t0), u_freev[k]])
            acc_free = ("Ss", su)
            wv = vtt(_t(w_sb, k, 512), _t(dl_sb, k, 512), _t(u_sb, k, 512),
                     AL.mult, [("Ss", su), w_freetk[k]])
            return wv

        w_freetk = w_free

        dlv = [0] * 8
        cev = [0] * 8
        wvv = [0] * 8
        bev = [0, 0]
        dlv[0] = emit_delta(0)
        cev[0] = emit_cproj(0)
        bev[0] = emit_xb_b(0)
        dlv[1] = emit_delta(1)
        cev[1] = emit_cproj(1)
        wvv[0] = emit_conv(0)

        pend = [None]
        pend_evac = [None]

        def emit_evac(kb, ysb):
            nonlocal yps_free
            ev = p.emit("vector",
                        lambda e, kb=kb: e.scalar_tensor_tensor(
                            yps_sb[:], _t(u_sb, kb, 512),
                            d_sb[:, kb:kb + 1], P_Y[:], AL.mult, AL.add),
                        [("St", ysb), yps_free], "Sv")
            bank["Y"] = ("Sv", ev)
            dma(yp[128 * kb:128 * kb + 128, col], yps_sb[:],
                [("Sv", ev)], sem="So")
            yps_free = ("So", p.v("So"))
            u_freev[kb] = ("Sv", ev)

        def drain(fence_w):
            if pend[0] is None:
                return
            Pq, cparq, ecq = pend[0]
            pend[0] = None
            sl0q = 2 * (Pq % 2)
            ch = p.emit("gpsimd",
                        lambda e, cparq=cparq, sl0q=sl0q: e.tensor_tensor(
                            _t(ch_sb, cparq, 1024),
                            hp_sb[:, sl0q * 512:sl0q * 512 + 1024],
                            ce_sb[:, cparq * 512:cparq * 512 + 512]
                            .unsqueeze(1).broadcast_to([128, 2, 512]),
                            AL.mult),
                        [fence_w, ("Ss", ecq), ch_free[cparq]], "Sg")
            ce_freed[cparq] = ("Sg", ch)
            slot_free[sl0q] = ("Sg", ch)
            slot_free[sl0q + 1] = ("Sg", ch)
            if pend_evac[0] is not None:
                kb, ysb = pend_evac[0]
                pend_evac[0] = None
                emit_evac(kb, ysb)
            ys = 0
            for jj, half in ((2 * Pq, 0), (2 * Pq + 1, 1)):
                cc = (jj % 16) // 4
                tpp = (0, 96) if cc == 3 else None
                ys = mm(P_Y[32 * cc:32 * cc + 32, :],
                        s_sb[:, 32 * (jj % 4):32 * (jj % 4) + 32],
                        ch_sb[:, cparq * 1024 + half * 512:
                              cparq * 1024 + half * 512 + 512],
                        jj % 4 == 0, jj % 4 == 3,
                        [("Sg", ch)] + ([bank["Y"]] if jj % 16 == 0
                                        else []),
                        tp=tpp, skip=True)
            ch_free[cparq] = ("St", ys)
            if Pq % 8 == 7:
                pend_evac[0] = (Pq // 8, ys)

        for k in range(8):
            if k + 2 < 8:
                dlv[k + 2] = emit_delta(k + 2)
                cev[k + 2] = emit_cproj(k + 2)
            if k == 2:
                bev[1] = emit_xb_b(1)
            if k + 1 < 8:
                wvv[k + 1] = emit_conv(k + 1)
            if k == 7:
                xbr = [("Sgx0", p.v("Sgx0")), ("Sgx1", p.v("Sgx1"))]
                h1 = sccp(xb_sb[:, 0:3], xb_sb[:, 512:515], xbr)
                h2 = sccp(xb_sb[:, 515:518], xb_sb[:, 1027:1030], [])
                xb_read = xbr

            for pr in range(8):
                P = 8 * k + pr
                j0, j1 = 2 * P, 2 * P + 1
                ppar = P % 2
                cpar = P % 2
                base = 32 * ((j0 % 16) // 4)
                kt = k
                be_w = None
                if P % 4 == 0:
                    g = P // 4
                    gp = g % 2
                    bb = 32 * ((g % 8) // 2)
                    mb_ = mm(P_CE[:],
                             e16_sb[bb:bb + 32, 128 * (g % 2):
                                    128 * (g % 2) + 128],
                             b_sb[bb:bb + 32,
                                  (g // 8) * 512:(g // 8) * 512 + 512],
                             True, True,
                             [bank["CE"], ("Ss", bev[g // 8])],
                             tp=(bb, 0))
                    be_w = mb_
                for jj, half in ((j0, 0), (j1, 1)):
                    eidx = 4 * (jj // 16) + (jj % 4)
                    mm(P_AD[:, half * 512:half * 512 + 512],
                       ea_sb[base:base + 32, 128 * eidx:128 * eidx + 128],
                       dl_sb[base:base + 32, kt * 512:kt * 512 + 512],
                       True, True,
                       ([bank["AD"], ("Sv", dlv[k])] if half == 0 else []),
                       tp=(base, 0))
                ma = p.v("St")
                da = act(_t(da_sb, ppar, 1024), P_AD[:], AF.Exp,
                         waits=[("St", ma), da_free[ppar]])
                bank["AD"] = ("Ss", da)
                if be_w is not None:
                    gp = (P // 4) % 2
                    eb = sccp(be_sb[:, gp * 512:gp * 512 + 512], P_CE[:],
                              [("St", be_w), be_freed[gp]])
                    bank["CE"] = ("Ss", eb)
                    be_ready = eb
                for jj, half in ((j0, 0), (j1, 1)):
                    mm(P_WE[:, half * 512:half * 512 + 512],
                       e8_sb[base:base + 32,
                             128 * (jj % 4):128 * (jj % 4) + 128],
                       w_sb[base:base + 32, kt * 512:kt * 512 + 512],
                       True, True,
                       ([bank["WE"], we_freed, ("Sv", wvv[k])]
                        if half == 0 else []),
                       tp=(base, 0))
                mw = p.v("St")
                bc_ = 32 * ((P % 8) // 2)
                mc_ = mm(P_CE[:],
                         e16b_sb[bc_:bc_ + 32, 128 * (P % 2):
                                 128 * (P % 2) + 128],
                         c_sb[bc_:bc_ + 32, kt * 512:kt * 512 + 512],
                         True, True, [bank["CE"], ("Ss", cev[k])],
                         tp=(bc_, 0))
                ec = sccp(ce_sb[:, cpar * 512:cpar * 512 + 512], P_CE[:],
                          [("St", mc_), ce_freed[cpar]])
                bank["CE"] = ("Ss", ec)
                gp = (P // 4) % 2
                db = vtt(dbu_sb[:], P_WE[:],
                         be_sb[:, gp * 512:gp * 512 + 512]
                         .unsqueeze(1).broadcast_to([128, 2, 512]),
                         AL.mult,
                         [("St", mw), ("Ss", be_ready), ("Ss", da)])
                we_freed = ("Sv", db)
                if P % 4 == 3:
                    be_freed[gp] = ("Sv", db)
                if j0 % 4 == 0 and P > 0:
                    p.emit("vector",
                           lambda e, j0=j0: e.tensor_copy(
                               st_sb[:, j0 - 4:j0].unsqueeze(2),
                               hp_sb[:].rearrange("p (s t) -> p s t", s=4)
                               [:, :, 511:512]),
                           [], "Sv")
                drain(("Sv", db))
                sc = 0
                for jj, half in ((j0, 0), (j1, 1)):
                    slot = jj % 4
                    sc = p.emit("vector",
                                lambda e, slot=slot, ppar=ppar, jj=jj,
                                half=half:
                                e.tensor_tensor_scan(
                                    _t(hp_sb, slot, 512),
                                    da_sb[:, ppar * 1024 + half * 512:
                                          ppar * 1024 + half * 512 + 512],
                                    dbu_sb[:, half * 512:half * 512 + 512],
                                    st_sb[:, jj:jj + 1], AL.mult, AL.add),
                                [slot_free[slot]], "Sv")
                da_free[ppar] = ("Sv", sc)
                pend[0] = (P, cpar, ec)
            dl_free[k] = ("St", p.v("St"))
            w_free[k] = ("St", p.v("St"))
            c_free[k] = ("St", p.v("St"))
        flet = p.emit("vector",
                      lambda e: e.tensor_copy(fence_sb[:], da_sb[:, 0:16]),
                      [], "Sv")
        drain(("Sv", flet))
        kb_, ysb_ = pend_evac[0]
        pend_evac[0] = None
        emit_evac(kb_, ysb_)
        p.emit("vector",
               lambda e: e.tensor_copy(
                   st_sb[:, NT - 4:NT].unsqueeze(2),
                   hp_sb[:].rearrange("p (s t) -> p s t", s=4)
                   [:, :, 511:512]),
               [], "Sv")
        halo_done = ("Ss", p.v("Ss"))
        b_free = [("St", p.v("St"))] * 2
        xw_free = ("St", p.v("St"))
        if dbg:
            nchk = l // TC
            for bufs, dst, nt_, cast in ((xw, xn_d, 16, False),
                                         (dl_sb, dl_d, 8, True),
                                         (u_sb, u_d, 8, True),
                                         (b_sb, b_d, 2, True),
                                         (c_sb, c_d, 8, False)):
                for tt in range(nt_):
                    srcap = _t(bufs, tt, 512)
                    if cast:
                        srcap = srcap.bitcast(F32)
                    dma(dst[:, (tt * nchk + c) * 512:
                            (tt * nchk + c) * 512 + 512],
                        srcap, [("Sv", p.v("Sv")), ("Sg", p.v("Sg"))],
                        sem="So", throttle=False)
            for tt in range(4):
                dma(h_d[:, (tt * nchk + c) * 512:(tt * nchk + c) * 512 + 512],
                    _t(hp_sb, tt, 512), [("Sv", p.v("Sv"))], sem="So",
                    throttle=False)
            p.emit("sync", lambda e: e.nop(),
                   [(sn, p.v(sn)) for sn in p.sems if sn != "Sd"],
                   "Sd", 16)
            bar = ("Sd", p.v("Sd"))
            xld_cast = [bar, bar]
            dl_free = [bar] * 8
            w_free = [bar] * 8
            c_free = [bar] * 8
            u_freev = [bar] * 8
            b_free = [bar] * 2
            tmp_free = bar
            for _sl in range(4):
                slot_free[_sl] = bar

    std_o = DT("st_d", [128, 128], F32, kind="ExternalOutput").ap()
    dma(std_o, st_sb[:], [("Sv", p.v("Sv"))], sem="So")
    p.emit("sync", lambda e: e.nop(), [("So", p.v("So"))], None)
    p.build()
    return nc



def build_p2(t2=T2, dbg=False):
    nc = bass.Bass("TRN2", target_bir_lowering=False,
                   dynamic_dma_scratch_size=4096)
    p = Prog(nc)
    for s in ("Sd", "St", "Ss", "Sv", "Sg", "So", "Si",
              "Sx0", "Sx1", "Sw0", "Sw1", "Sy0", "Sy1",
              "Sst0", "Sst1", "Sp0", "Sp1", "Swd0", "Swd1",
              "Spl", "Sl0", "Sl1", "So0", "So1"):
        p.sem(s)
    C = Common(p)
    dma, mm, act, sccp, vtt, vcp = C.dma, C.mm, C.act, C.sccp, C.vtt, C.vcp
    ncc = t2 // 512

    DT = nc.dram_tensor
    xt2 = DT("xt2", [DM, t2], F32, kind="ExternalInput").ap()
    yp2 = DT("yp2", [DIN, t2], F32, kind="ExternalInput").ap()
    wz = DT("wz", [128, 32 * 16 * 128], F32, kind="ExternalInput").ap()
    wo = DT("wo", [128, 16 * 32 * 128], F32, kind="ExternalInput").ap()
    wg = DT("wg", [128, 44 * 16 * 128], I32, kind="ExternalInput").ap()
    wu = DT("wu", [128, 44 * 16 * 128], I32, kind="ExternalInput").ap()
    wd = DT("wd", [128, 16 * 44 * 128], I32, kind="ExternalInput").ap()
    bzi = DT("bz", [128, 32], F32, kind="ExternalInput").ap()
    boi = DT("bo", [128, 16], F32, kind="ExternalInput").ap()
    sgi = DT("sg", [128, 44], F32, kind="ExternalInput").ap()
    sui = DT("su", [128, 44], F32, kind="ExternalInput").ap()
    sdi = DT("sdn", [128, 16], F32, kind="ExternalInput").ap()
    onesdi = DT("onesd", [128, 1], BF16, kind="ExternalInput").ap()
    ones1i = DT("ones1", [1, 128], BF16, kind="ExternalInput").ap()
    csti = DT("cst", [128, 4], F32, kind="ExternalInput").ap()
    outt = DT("outt", [DM, t2], F32, kind="ExternalOutput").ap()
    s_dr = DT("s_dr", [DM, t2], F32,
              kind="ExternalOutput" if dbg else "Internal").ap()
    if dbg:
        g_d = DT("g_d", [DM, t2], F32, kind="ExternalOutput").ap()
        sp_d = DT("sp_d", [DIN, t2], BF16, kind="ExternalOutput").ap()
    p_dr = DT("p_dr", [HID, t2], BF16, kind="Internal").ap()
    wd_dr = DT("wd_dr", [128, 16 * 44 * 128], BF16, kind="Internal").ap()

    A = nc.alloc_sbuf_tensor
    arena = A("arena", [128, 96 * 1024], mybir.dt.uint8)
    ARE = arena[:].bitcast(BF16)
    xn2 = ARE[:, 0:16 * t2]
    spc = ARE[:, 16 * t2:16 * t2 + 32 * 512]
    pch = ARE[:, 0:44 * 1024]
    x_ld = A("x_ld", [128, 2 * 512], F32)
    xsq = A("xsq", [128, 512], BF16)
    mean_f = A("mean_f", [1, 512], F32)
    msq_f = A("msq_f", [1, 512], F32)
    m2_f = A("m2_f", [1, 512], F32)
    sd_f = A("sd_f", [1, 512], F32)
    rr_f = A("rr_f", [1, 512], F32)
    mean_bf = A("mean_bf", [1, 512], BF16)
    rr_bf = A("rr_bf", [1, 512], BF16)
    mbc_sb = A("mbc_sb", [128, 512], F32)
    rbc_sb = A("rbc_sb", [128, 512], F32)
    tmpa_sb = A("tmpa_sb", [128, 512], F32)
    onesd_sb = A("onesd_sb", [128, 1], BF16)
    ones1_sb = A("ones1_sb", [1, 128], BF16)
    cst_sb = A("cst_sb", [128, 4], F32)
    bz_sb = A("bz_sb", [128, 32], F32)
    bo_sb = A("bo_sb", [128, 16], F32)
    sg_sb = A("sg_sb", [128, 44], F32)
    su_sb = A("su_sb", [128, 44], F32)
    sd2_sb = A("sd2_sb", [128, 16], F32)
    wst = A("wst", [128, 2 * 5632], F32)
    wbf = A("wbf", [128, 2 * 5632], BF16)
    ypt = A("ypt", [128, 2 * 512], F32)
    szt = A("szt", [128, 512], F32)
    gt_sb = A("gt_sb", [128, 512], F32)
    ut_sb = A("ut_sb", [128, 512], F32)
    pt_sb = A("pt_sb", [128, 2 * 512], BF16)
    st2_sb = A("st2_sb", [128, 2 * 512], F32)
    g2_sb = A("g2_sb", [128, 512], F32)
    sgl_sb = A("sgl_sb", [128, 2 * 512], F32)
    xtl_sb = A("xtl_sb", [128, 2 * 512], F32)
    ot_sb = A("ot_sb", [128, 2 * 512], F32)

    AP_ = nc.alloc_psum_tensor
    P_A = AP_("P_A", [128, 512], F32)
    P_B = AP_("P_B", [128, 512], F32)
    P_C = AP_("P_C", [128, 512], F32)
    P_D = AP_("P_D", [128, 512], F32)
    banks = [("A", P_A[:]), ("B", P_B[:]), ("C", P_C[:]), ("D", P_D[:])]

    EPS1 = cst_sb[0:1, 2:3]
    NEGH = cst_sb[0:1, 3:4]

    for dst, src in ((bz_sb, bzi), (bo_sb, boi), (sg_sb, sgi), (su_sb, sui),
                     (sd2_sb, sdi), (onesd_sb, onesdi), (ones1_sb, ones1i),
                     (cst_sb, csti)):
        dma(dst[:], src, sem="Si", throttle=False)
    v0 = p.v("Si")

    bank = {k: ("Ss", 0) for k in ("A", "B", "C", "D")}
    xld_cast = [("Sv", 0), ("Sv", 0)]
    wst_free = [("Sv", 0), ("Sv", 0)]
    wbf_free = [("St", 0), ("St", 0)]

    def xslice(t, c):
        off = (t * ncc + c) * 512
        return xn2[:, off:off + 512]

    for c in range(ncc):
        col = slice(c * 512, (c + 1) * 512)
        last_q = 0
        for t in range(16):
            par = t % 2
            sx = "Sx%d" % par
            ld = dma(_t(x_ld, par, 512), xt2[128 * t:128 * (t + 1), col],
                     [xld_cast[par]], sem=sx)
            cwv = p.emit("vector",
                         lambda e, t=t, par=par, c=c: e.tensor_copy(
                             xslice(t, c), _t(x_ld, par, 512)),
                         [(sx, ld)], "Sv")
            xld_cast[par] = ("Sv", cwv)
            sq = p.emit("vector",
                        lambda e, t=t, c=c: e.tensor_tensor(
                            xsq[:], xslice(t, c), xslice(t, c), AL.mult),
                        [("St", last_q)], "Sv")
            mm(P_A[0:1, :], onesd_sb[:], xslice(t, c), t == 0, t == 15,
               [("Sv", cwv)] + ([bank["A"], ("Si", v0)] if t == 0 else []))
            last_q = mm(P_B[0:1, :], onesd_sb[:], xsq[:], t == 0, t == 15,
                        [("Sv", sq)] + ([bank["B"]] if t == 0 else []))
        stats_done = p.v("St")
        c1 = vcp(mean_f[:], P_A[0:1, :], [("St", stats_done)])
        vcp(mean_bf[:], P_A[0:1, :])
        c2 = vcp(msq_f[:], P_B[0:1, :])
        vtt(m2_f[:], mean_f[:], mean_f[:], AL.mult, [])
        v_var = vtt(sd_f[:], msq_f[:], m2_f[:], AL.subtract, [])
        a_ln = act(rr_f[:], sd_f[:], AF.Ln, bias=EPS1,
                   waits=[("Sv", v_var)])
        a_rs = act(sd_f[:], rr_f[:], AF.Exp, scale=NEGH)
        v_rrb = vcp(rr_bf[:], sd_f[:], [("Ss", a_rs)])
        mm(P_A[:], ones1_sb[:], mean_bf[:], True, True,
           [("Sv", v_rrb), bank["A"]])
        rb = mm(P_B[:], ones1_sb[:], rr_bf[:], True, True, [bank["B"]])
        e_mb = sccp(mbc_sb[:], P_A[:], [("St", rb)])
        e_rb = sccp(rbc_sb[:], P_B[:], [])
        bank["A"] = ("Ss", e_mb)
        bank["B"] = ("Ss", e_rb)
        lnt = 0
        for t in range(16):
            par = t % 2
            sx = "Sx%d" % par
            ld2 = dma(_t(x_ld, par, 512), xt2[128 * t:128 * (t + 1), col],
                      [xld_cast[par]], sem=sx)
            w1 = [(sx, ld2), ("Ss", e_rb)]
            if t == 0:
                w1.append(("St", stats_done))
            lnt = vtt(tmpa_sb[:], _t(x_ld, par, 512), mbc_sb[:], AL.subtract,
                      w1)
            lnt = vtt(xslice(t, c), tmpa_sb[:], rbc_sb[:], AL.mult, [])
            xld_cast[par] = ("Sv", lnt)
    xn_done = p.v("Sv")

    def wload(src, mt, width, dt_in, par, waits=()):
        sw = "Sw%d" % par
        if dt_in == F32:
            stg = wst[:, par * 5632:par * 5632 + width]
        else:
            stg = wst[:, par * 5632:par * 5632 + width].bitcast(I32)
        ld = dma(stg, src[:, mt * width:mt * width + width],
                 [wst_free[par]] + list(waits), sem=sw)
        dstb = wbf[:, par * 5632:par * 5632 + width]
        if dt_in == F32:
            cv = p.emit("vector",
                        lambda e, o=dstb, i=stg: e.tensor_copy(o, i),
                        [(sw, ld), wbf_free[par]], "Sv")
        else:
            cv = p.emit("vector",
                        lambda e, o=dstb, i=stg: e.tensor_scalar(
                            o, i, 128.0, None, AL.subtract),
                        [(sw, ld), wbf_free[par]], "Sv")
        wst_free[par] = ("Sv", cv)
        return dstb, cv

    spc_free = ("St", 0)
    ypt_free = [("Sv", 0), ("Sv", 0)]
    szt_free = ("Sv", 0)
    st2_free = [("Sst0", 0), ("Sst1", 0)]
    for c in range(ncc):
        for m in range(32):
            par = m % 2
            dstb, cv = wload(wz, m, 2048, F32, par,
                             [("Sv", xn_done)] if (c == 0 and m < 2) else ())
            nm, pt_ = banks[m % 4]
            last = 0
            for k in range(16):
                last = mm(pt_, dstb[:, k * 128:k * 128 + 128], xslice(k, c),
                          k == 0, k == 15,
                          [("Sv", cv), bank[nm]] if k == 0 else [])
            wbf_free[par] = ("St", last)
            a_sz = act(szt[:], pt_, AF.Silu, bias=bz_sb[:, m:m + 1],
                       waits=[("St", last), szt_free])
            bank[nm] = ("Ss", a_sz)
            ldy = dma(_t(ypt, par, 512),
                      yp2[128 * m:128 * m + 128, c * 512:c * 512 + 512],
                      [ypt_free[par]], sem="Sy%d" % par)
            sv = vtt(_t(spc, m, 512), _t(ypt, par, 512), szt[:], AL.mult,
                     [("Ss", a_sz), ("Sy%d" % par, ldy), spc_free] if m == 0
                     else [("Ss", a_sz), ("Sy%d" % par, ldy)])
            ypt_free[par] = ("Sv", sv)
            szt_free = ("Sv", sv)
        sp_done = p.v("Sv")
        if dbg:
            for mm_ in range(32):
                dma(sp_d[128 * mm_:128 * mm_ + 128, c * 512:c * 512 + 512],
                    _t(spc, mm_, 512), [("Sv", sp_done)], sem="So",
                    throttle=False)
        for m in range(16):
            par = m % 2
            dstb, cv = wload(wo, m, 4096, F32, par)
            nm, pt_ = banks[m % 4]
            last = 0
            for k in range(32):
                last = mm(pt_, dstb[:, k * 128:k * 128 + 128],
                          _t(spc, k, 512), k == 0, k == 31,
                          [("Sv", max(cv, sp_done)), bank[nm]] if k == 0
                          else [])
            wbf_free[par] = ("St", last)
            ev = act(_t(st2_sb, par, 512), pt_, AF.Identity,
                     bias=bo_sb[:, m:m + 1],
                     waits=[("St", last), st2_free[par]])
            bank[nm] = ("Ss", ev)
            sd_ = dma(s_dr[128 * m:128 * m + 128, c * 512:c * 512 + 512],
                      _t(st2_sb, par, 512), [("Ss", ev)], sem="Sst%d" % par)
            st2_free[par] = ("Sst%d" % par, sd_)
        spc_free = ("St", p.v("St"))

    pt_free = [("Sp0", 0), ("Sp1", 0)]
    gu_free = ("Sv", 0)
    for hm in range(44):
        dg, cg = wload(wg, hm, 2048, I32, 0)
        du, cu = wload(wu, hm, 2048, I32, 1)
        lg = lu = 0
        for c in range(ncc):
            for k in range(16):
                lg = mm(P_A[:], dg[:, k * 128:k * 128 + 128], xslice(k, c),
                        k == 0, k == 15,
                        [("Sv", cg), bank["A"]] if k == 0 else [])
            for k in range(16):
                lu = mm(P_B[:], du[:, k * 128:k * 128 + 128], xslice(k, c),
                        k == 0, k == 15,
                        [("Sv", cu), bank["B"]] if k == 0 else [])
            ag = act(gt_sb[:], P_A[:], AF.Silu, scale=sg_sb[:, hm:hm + 1],
                     waits=[("St", lg), gu_free])
            au = act(ut_sb[:], P_B[:], AF.Identity, scale=su_sb[:, hm:hm + 1],
                     waits=[("St", lu)])
            bank["A"] = ("Ss", ag)
            bank["B"] = ("Ss", au)
            ppar = c % 2
            pv = vtt(_t(pt_sb, ppar, 512), gt_sb[:], ut_sb[:], AL.mult,
                     [("Ss", au), pt_free[ppar]])
            gu_free = ("Sv", pv)
            pd = dma(p_dr[128 * hm:128 * hm + 128, c * 512:c * 512 + 512],
                     _t(pt_sb, ppar, 512), [("Sv", pv)], sem="Sp%d" % ppar)
            pt_free[ppar] = ("Sp%d" % ppar, pd)
        wbf_free[0] = ("St", lg)
        wbf_free[1] = ("St", lu)

    for m in range(16):
        par = m % 2
        dstb, cv = wload(wd, m, 44 * 128, I32, par)
        wds = dma(wd_dr[:, m * 44 * 128:(m + 1) * 44 * 128], dstb,
                  [("Sv", cv)], sem="Swd%d" % par)
        wbf_free[par] = ("Swd%d" % par, wds)

    p_stored = [("Sp0", p.v("Sp0")), ("Sp1", p.v("Sp1"))]
    wd_stored = [("Swd0", p.v("Swd0")), ("Swd1", p.v("Swd1"))]
    s_stored = [("Sst0", p.v("Sst0")), ("Sst1", p.v("Sst1"))]
    pch_free = ("St", p.v("St"))
    g_done = ("Sv", 0)
    sgl_free = [("Sv", 0), ("Sv", 0)]
    xtl_free = [("Sv", 0), ("Sv", 0)]
    ot_free = [("So0", 0), ("So1", 0)]
    for c3 in range(t2 // 1024):
        ldp = 0
        for hh in range(44):
            first = hh == 0
            ldp = dma(_t(pch, hh, 1024),
                      p_dr[128 * hh:128 * hh + 128,
                           c3 * 1024:c3 * 1024 + 1024],
                      ([pch_free] + (p_stored if c3 == 0 else []))
                      if first else [],
                      sem="Spl", throttle=first)
        for m in range(16):
            par = m % 2
            stg = wst[:, par * 5632:par * 5632 + 2816].bitcast(BF16)
            ldw = dma(stg, wd_dr[:, m * 44 * 128:(m + 1) * 44 * 128],
                      [wst_free[par]] +
                      (wd_stored if (c3 == 0 and m < 2) else []),
                      sem="Sw%d" % par)
            for half in range(2):
                nm, pt_ = banks[(m * 2 + half) % 4]
                last = 0
                for k in range(44):
                    last = mm(pt_, stg[:, k * 128:k * 128 + 128],
                              pch[:, (k * 2 + half) * 512:
                                  (k * 2 + half) * 512 + 512],
                              k == 0, k == 43,
                              [("Sw%d" % par, ldw), ("Spl", ldp), bank[nm]]
                              if k == 0 else [])
                ag = act(g2_sb[:], pt_, AF.Silu, scale=sd2_sb[:, m:m + 1],
                         waits=[("St", last), g_done])
                bank[nm] = ("Ss", ag)
                colh = slice(c3 * 1024 + half * 512,
                             c3 * 1024 + half * 512 + 512)
                if dbg:
                    dma(g_d[128 * m:128 * m + 128, colh], g2_sb[:],
                        [("Ss", ag)], sem="So", throttle=False)
                lds = dma(_t(sgl_sb, half, 512),
                          s_dr[128 * m:128 * m + 128, colh],
                          [sgl_free[half]] +
                          (s_stored if (c3 == 0 and m == 0) else []),
                          sem="Sl%d" % half)
                ldx = dma(_t(xtl_sb, half, 512),
                          xt2[128 * m:128 * m + 128, colh],
                          [xtl_free[half]], sem="Sx%d" % half)
                v1 = vtt(_t(ot_sb, half, 512), g2_sb[:],
                         _t(sgl_sb, half, 512), AL.mult,
                         [("Ss", ag), ("Sl%d" % half, lds),
                          ("Sx%d" % half, ldx), ot_free[half]])
                g_done = ("Sv", v1)
                sgl_free[half] = ("Sv", v1)
                v2 = vtt(_t(ot_sb, half, 512), _t(ot_sb, half, 512),
                         _t(xtl_sb, half, 512), AL.add, [])
                xtl_free[half] = ("Sv", v2)
                od = dma(outt[128 * m:128 * m + 128, colh],
                         _t(ot_sb, half, 512), [("Sv", v2)],
                         sem="So%d" % half)
                ot_free[half] = ("So%d" % half, od)
            wst_free[par] = ("St", p.v("St"))
        pch_free = ("St", p.v("St"))

    p.emit("sync", lambda e: e.nop(),
           [("So0", p.v("So0")), ("So1", p.v("So1")), ("So", p.v("So"))],
           None)
    p.build()
    return nc



def _tile_w(wT):
    K, M = wT.shape
    nk, nm = K // 128, M // 128
    a = wT.reshape(nk, 128, nm, 128).transpose(1, 2, 0, 3).reshape(128, -1)
    return np.ascontiguousarray(a)


def _pp(vec):
    n = vec.shape[0] // 128
    return np.ascontiguousarray(vec.reshape(n, 128).T)


def _mk_e8():
    e = np.zeros((128, 512), np.float32)
    for g4 in range(4):
        for v in range(4):
            for dd in range(8):
                for s in range(16):
                    e[32 * g4 + 8 * v + dd, 128 * v + 16 * dd + s] = 1.0
    return e


def _mk_e16():
    e = np.zeros((128, 256), np.float32)
    for g4 in range(4):
        for v in range(2):
            for dd in range(8):
                for s in range(16):
                    e[32 * g4 + 16 * v + s, 128 * v + 16 * dd + s] = 1.0
    return e


def _mk_smat():
    s = np.zeros((128, 128), np.float32)
    for m in range(4):
        for pp_ in range(128):
            s[pp_, 32 * m + 8 * m + pp_ // 16] = 1.0
    return s


def _mk_alog_ea(alog_slice):
    out = np.full((128, 4096), -1e30, np.float32)
    for j in range(128):
        g4 = (j % 16) // 4
        eidx = 4 * (j // 16) + (j % 4)
        r8 = 8 * (j % 4)
        for dd in range(8):
            out[32 * g4 + r8 + dd,
                128 * eidx + 16 * dd:128 * eidx + 16 * dd + 16] = \
                alog_slice[8 * j + dd, :]
    return out


_CACHE = {}


def kernel(**inp):
    x = np.ascontiguousarray(np.asarray(inp["L0_attn_hidden_states"],
                                        np.float32))
    Wx = np.asarray(inp["L0_attn_Wx"], np.float32)
    bx = np.asarray(inp["L0_attn_bx"], np.float32)
    Wz = np.asarray(inp["L0_attn_Wz"], np.float32)
    bz = np.asarray(inp["L0_attn_bz"], np.float32)
    conv_w = np.asarray(inp["L0_attn_conv_w"], np.float32)
    conv_b = np.asarray(inp["L0_attn_conv_b"], np.float32)
    WB = np.asarray(inp["L0_attn_WB"], np.float32)
    WC = np.asarray(inp["L0_attn_WC"], np.float32)
    Wdtd = np.asarray(inp["L0_attn_Wdt_down"], np.float32)
    Wdt = np.asarray(inp["L0_attn_Wdt"], np.float32)
    bdt = np.asarray(inp["L0_attn_bdt"], np.float32)
    A_log = np.asarray(inp["L0_attn_A_log"], np.float32)
    Dv = np.asarray(inp["L0_attn_D"], np.float32)
    Wout = np.asarray(inp["L0_attn_Wout"], np.float32)
    bout = np.asarray(inp["L0_attn_bout"], np.float32)
    wqg = np.asarray(inp["L0_ffn_wq_gate"], np.int32)
    sg = np.asarray(inp["L0_ffn_s_gate"], np.float32)
    wqu = np.asarray(inp["L0_ffn_wq_up"], np.int32)
    su = np.asarray(inp["L0_ffn_s_up"], np.float32)
    wqd = np.asarray(inp["L0_ffn_wq_down"], np.int32)
    sdn = np.asarray(inp["L0_ffn_s_down"], np.float32)

    if "p1" not in _CACHE:
        _CACHE["p1"] = build_p1()
        _CACHE["p2"] = build_p2()
    nc1, nc2_ = _CACHE["p1"], _CACHE["p2"]

    e8 = _mk_e8()
    e16 = _mk_e16()
    e16b = e16.astype(BFNP)
    smat = _mk_smat().astype(BFNP)
    cst = np.tile(np.array([[-1.0, 1.0, 1e-5, -0.5]], np.float32), (128, 1))
    onesd = np.full((128, 1), 1.0 / 2048.0, BFNP)
    ones1 = np.ones((1, 128), BFNP)

    xts = [np.ascontiguousarray(x[b].T) for b in range(2)]

    in1 = []
    for core in range(8):
        b, q = core // 4, core % 4
        sl1 = slice(DSL * q, DSL * (q + 1))
        sl4 = slice(256 * q, 256 * (q + 1))
        in1.append({
            "xt": xts[b],
            "wx": _tile_w(np.ascontiguousarray(Wx[sl4].T)),
            "wbm": _tile_w(np.ascontiguousarray(WB[sl4].T)),
            "wc": _tile_w(np.ascontiguousarray(WC[sl1].T)),
            "wdtd": _tile_w(np.ascontiguousarray(Wdtd.T)),
            "wdt": _tile_w(np.ascontiguousarray(Wdt[sl1].T)),
            "alog": _mk_alog_ea(A_log[sl1]),
            "e8": e8, "e16": e16, "e16b": e16b, "smat": smat,
            "cw": np.ascontiguousarray(
                conv_w[sl1, 0, :].reshape(8, 128, 4).transpose(1, 0, 2)
                .reshape(128, 32)),
            "cb": _pp(conv_b[sl1]),
            "bx": _pp(bx[sl4]),
            "bdt": _pp(bdt[sl1]),
            "dvec": _pp(Dv[sl1]),
            "cst": cst, "onesd": onesd, "ones1": ones1,
        })
    r1 = run_bass_kernel_spmd(nc1, in1, core_ids=list(range(8)))
    ypl = [np.asarray(r1.results[i]["yp"]) for i in range(8)]

    wz_t = _tile_w(np.ascontiguousarray(Wz.T))
    wo_t = _tile_w(np.ascontiguousarray(Wout.T))
    wg_t = _tile_w(np.ascontiguousarray(wqg.T))
    wu_t = _tile_w(np.ascontiguousarray(wqu.T))
    wd_t = _tile_w(np.ascontiguousarray(wqd.T))
    bz_p = _pp(bz)
    bo_p = _pp(bout)
    sg_p = _pp(sg)
    su_p = _pp(su)
    sd_p = _pp(sdn)

    in2 = []
    for core in range(8):
        b, q = core // 4, core % 4
        ybf = np.concatenate([ypl[4 * b + qq] for qq in range(4)], axis=0)
        in2.append({
            "xt2": np.ascontiguousarray(xts[b][:, T2 * q:T2 * (q + 1)]),
            "yp2": np.ascontiguousarray(ybf[:, T2 * q:T2 * (q + 1)]),
            "wz": wz_t, "wo": wo_t, "wg": wg_t, "wu": wu_t, "wd": wd_t,
            "bz": bz_p, "bo": bo_p, "sg": sg_p, "su": su_p, "sdn": sd_p,
            "onesd": onesd, "ones1": ones1, "cst": cst,
        })
    r2 = run_bass_kernel_spmd(nc2_, in2, core_ids=list(range(8)))

    out = np.empty((2, L, DM), np.float32)
    for core in range(8):
        b, q = core // 4, core % 4
        out[b, T2 * q:T2 * (q + 1), :] = np.asarray(
            r2.results[core]["outt"]).T
    return out


# version 07
# speedup vs baseline: 1.3418x; 1.1628x over previous
import numpy as np
import ml_dtypes
import concourse.bass as bass
import concourse.mybir as mybir
from concourse.bass_utils import run_bass_kernel_spmd

F32 = mybir.dt.float32
F32R = mybir.dt.float32r
BF16 = mybir.dt.bfloat16
I32 = mybir.dt.int32
AL = mybir.AluOpType
AF = mybir.ActivationFunctionType
BFNP = ml_dtypes.bfloat16

DM = 2048
DIN = 4096
DSL = 1024
L = 8192
TC = 512
T2 = 2048
NT = 128
HID = 5632

ENGINES = ("sync", "tensor", "scalar", "vector", "gpsimd")


class Prog:

    def __init__(self, nc):
        self.nc = nc
        self.ops = {e: [] for e in ENGINES}
        self.cnt = {}
        self.sems = {}

    def sem(self, name):
        if name not in self.sems:
            self.sems[name] = self.nc.alloc_semaphore(name)
            self.cnt[name] = 0
        return self.sems[name]

    def emit(self, eng, fn, waits=(), inc=None, incn=1):
        waits = tuple((s, v) for (s, v) in waits if v > 0)
        for s, v in waits:
            assert v <= self.cnt[s], (eng, s, v, self.cnt[s])
        self.ops[eng].append((fn, waits, inc, incn))
        if inc is not None:
            self.cnt[inc] += incn
            return self.cnt[inc]
        return None

    def v(self, name):
        return self.cnt.get(name, 0)

    def build(self):
        sems = self.sems
        with self.nc.Block() as blk:
            for ename in ENGINES:
                ops = self.ops[ename]
                if not ops:
                    continue

                def make(ops=ops):
                    def body(eng):
                        for fn, waits, inc, incn in ops:
                            for s, v in waits:
                                eng.wait_ge(sems[s], v)
                            inst = fn(eng)
                            if inc is not None:
                                inst.then_inc(sems[inc], incn)
                    return body

                getattr(blk, ename)(make())


def _t(h, i, w):
    return h[:, i * w:(i + 1) * w]


class Common:
    def __init__(self, p):
        self.p = p

    def dma(self, dst, src, waits=(), sem="Sd", throttle=True):
        w = list(waits)
        if throttle:
            w.append((sem, self.p.v(sem)))
        return self.p.emit("sync", lambda e, d=dst, s=src: e.dma_start(d, s),
                           w, sem, 16)

    def mm(self, pt, lh, rh, st, sp, waits=(), tp=None, skip=False):
        def f(e, pt=pt, lh=lh, rh=rh, st=st, sp=sp, tp=tp, skip=skip):
            return e.matmul(pt, lh, rh, start=st, stop=sp, tile_position=tp,
                            skip_group_check=skip)
        return self.p.emit("tensor", f, waits, "St")

    def act(self, out, in_, fn, bias=0.0, scale=1.0, waits=()):
        def f(e, out=out, in_=in_, fn=fn, bias=bias, scale=scale):
            return e.activation(out, in_, fn, bias=bias, scale=scale)
        return self.p.emit("scalar", f, waits, "Ss")

    def sccp(self, out, in_, waits=()):
        return self.p.emit("scalar", lambda e, o=out, i=in_: e.copy(o, i),
                           waits, "Ss")

    def vtt(self, out, a, b, op, waits=()):
        return self.p.emit("vector", lambda e, o=out, a=a, b=b, op=op:
                           e.tensor_tensor(o, a, b, op), waits, "Sv")

    def vcp(self, out, in_, waits=()):
        return self.p.emit("vector", lambda e, o=out, i=in_:
                           e.tensor_copy(o, i), waits, "Sv")




def build_p1(l=L, dbg=False, barrier=False):
    nch = l // TC
    nc = bass.Bass("TRN2", target_bir_lowering=False,
                   dynamic_dma_scratch_size=2048)
    p = Prog(nc)
    for s in ("Sd", "St", "Ss", "Sv", "Sg", "So",
              "Si", "Swc", "Sx0", "Sx1", "Sgx0", "Sgx1"):
        p.sem(s)
    C = Common(p)
    dma, mm, act, sccp, vtt, vcp = C.dma, C.mm, C.act, C.sccp, C.vtt, C.vcp

    DT = nc.dram_tensor
    xt = DT("xt", [DM, l], F32, kind="ExternalInput").ap()
    wx = DT("wx", [128, 2 * 16 * 128], F32, kind="ExternalInput").ap()
    wbm = DT("wbm", [128, 2 * 16 * 128], F32, kind="ExternalInput").ap()
    wc = DT("wc", [128, 8 * 16 * 128], F32, kind="ExternalInput").ap()
    wdtd = DT("wdtd", [128, 16 * 128], F32, kind="ExternalInput").ap()
    wdt = DT("wdt", [128, 8 * 128], F32, kind="ExternalInput").ap()
    alog = DT("alog", [128, 4096], F32, kind="ExternalInput").ap()
    e8i = DT("e8", [128, 512], F32R, kind="ExternalInput").ap()
    e16i = DT("e16", [128, 256], F32R, kind="ExternalInput").ap()
    e16bi = DT("e16b", [128, 256], BF16, kind="ExternalInput").ap()
    smati = DT("smat", [128, 128], BF16, kind="ExternalInput").ap()
    cwi = DT("cw", [128, 32], F32, kind="ExternalInput").ap()
    cbi = DT("cb", [128, 8], F32, kind="ExternalInput").ap()
    bxi = DT("bx", [128, 2], F32, kind="ExternalInput").ap()
    bdti = DT("bdt", [128, 8], F32, kind="ExternalInput").ap()
    dvi = DT("dvec", [128, 8], F32, kind="ExternalInput").ap()
    csti = DT("cst", [128, 4], F32, kind="ExternalInput").ap()
    onesdi = DT("onesd", [128, 1], BF16, kind="ExternalInput").ap()
    ones1i = DT("ones1", [1, 128], BF16, kind="ExternalInput").ap()
    yp = DT("yp", [DSL, l], F32, kind="ExternalOutput").ap()
    if dbg:
        xn_d = DT("xn_d", [128, 16 * l], BF16, kind="ExternalOutput").ap()
        dl_d = DT("dl_d", [128, 8 * l], F32, kind="ExternalOutput").ap()
        u_d = DT("u_d", [128, 8 * l], F32, kind="ExternalOutput").ap()
        b_d = DT("b_d", [128, 2 * l], F32, kind="ExternalOutput").ap()
        c_d = DT("c_d", [128, 8 * l], BF16, kind="ExternalOutput").ap()
        h_d = DT("h_d", [128, 4 * l], F32, kind="ExternalOutput").ap()

    A = nc.alloc_sbuf_tensor
    wx_sb = A("wx_sb", [128, 2 * 16 * 128], BF16)
    wb_sb = A("wb_sb", [128, 2 * 16 * 128], BF16)
    wdtd_sb = A("wdtd_sb", [128, 16 * 128], BF16)
    wdt_sb = A("wdt_sb", [128, 8 * 128], F32R)
    wcb_sb = A("wcb_sb", [128, 2 * 2048], BF16)
    ea_sb = A("ea_sb", [128, 4096], F32R)
    e8_sb = A("e8_sb", [128, 512], F32R)
    e16_sb = A("e16_sb", [128, 256], F32R)
    e16b_sb = A("e16b_sb", [128, 256], BF16)
    s_sb = A("s_sb", [128, 128], BF16)
    cw_sb = A("cw_sb", [128, 32], F32)
    cb_sb = A("cb_sb", [128, 8], F32)
    bx_sb = A("bx_sb", [128, 2], F32)
    bdt2_sb = A("bdt2_sb", [128, 8], F32)
    d_sb = A("d_sb", [128, 8], F32)
    cst_sb = A("cst_sb", [128, 4], F32)
    onesd_sb = A("onesd_sb", [128, 1], BF16)
    ones1_sb = A("ones1_sb", [1, 128], BF16)
    x_ld = A("x_ld", [128, 2 * 512], F32)
    xw = A("xw", [128, 16 * 512], BF16)
    xsq = A("xsq", [128, 512], BF16)
    mean_f = A("mean_f", [1, 512], F32)
    msq_f = A("msq_f", [1, 512], F32)
    m2_f = A("m2_f", [1, 512], F32)
    sd_f = A("sd_f", [1, 512], F32)
    rr_f = A("rr_f", [1, 512], F32)
    mean_bf = A("mean_bf", [1, 512], BF16)
    rr_bf = A("rr_bf", [1, 512], BF16)
    mbc_sb = A("mbc_sb", [128, 512], F32)
    rbc_sb = A("rbc_sb", [128, 512], F32)
    dt1_sb = A("dt1_sb", [128, 512], F32R)
    dl_sb = A("dl_sb", [128, 8 * 512], F32R)
    u_sb = A("u_sb", [128, 8 * 512], F32R)
    wcs_sb = A("wcs_sb", [128, 1024], F32)
    w_sb = A("w_sb", [128, 8 * 512], F32R)
    c_sb = A("c_sb", [128, 8 * 512], BF16)
    b_sb = A("b_sb", [128, 2 * 512], F32R)
    xb_sb = A("xb_sb", [128, 2 * 515], F32)
    xbc_sb = A("xbc_sb", [128, 515], F32)
    tmpa_sb = A("tmpa_sb", [128, 512], F32)
    tmpb_sb = A("tmpb_sb", [128, 512], F32)
    acc_sb = A("acc_sb", [128, 2 * 512], F32)
    da_sb = A("da_sb", [128, 2 * 1024], F32)
    dbu_sb = A("dbu_sb", [128, 1024], F32)
    hp_sb = A("hp_sb", [128, 4 * 512], F32)
    st_sb = A("st_sb", [128, 128], F32)
    be_sb = A("be_sb", [128, 2 * 512], F32)
    ce_sb = A("ce_sb", [128, 2 * 512], F32)
    ch_sb = A("ch_sb", [128, 3 * 1024], BF16)
    yps_sb = A("yps_sb", [128, 512], F32)
    fence_sb = A("fence_sb", [128, 16], F32)

    AP_ = nc.alloc_psum_tensor
    P_AD = AP_("P_AD", [128, 2 * 512], F32)
    P_WE = AP_("P_WE", [128, 2 * 512], F32)
    P_CE = AP_("P_CE", [128, 512], F32)
    P_Y = AP_("P_Y", [128, 512], F32)
    P_PR = AP_("P_PR", [128, 512], F32)
    P_XE = AP_("P_XE", [128, 512], F32)

    NEG1 = cst_sb[:, 0:1]
    ONE = cst_sb[:, 1:2]
    EPS1 = cst_sb[0:1, 2:3]
    NEGH = cst_sb[0:1, 3:4]

    for dst, src in ((cw_sb, cwi), (cb_sb, cbi), (bx_sb, bxi),
                     (bdt2_sb, bdti), (d_sb, dvi), (cst_sb, csti),
                     (e8_sb, e8i), (e16_sb, e16i), (e16b_sb, e16bi),
                     (s_sb, smati), (onesd_sb, onesdi), (ones1_sb, ones1i)):
        dma(dst[:], src, sem="Si", throttle=False)
    v0 = p.v("Si")
    bdt2v = p.emit("vector",
                   lambda e: e.tensor_scalar_mul(bdt2_sb[:], bdt2_sb[:], 2.0),
                   [("Si", v0)], "Sv")
    for (src, dst, ncols) in ((wx, wx_sb, 2 * 16), (wbm, wb_sb, 2 * 16),
                              (wdtd, wdtd_sb, 16), (wdt, wdt_sb, 8)):
        total = ncols * 128
        off = 0
        while off < total:
            n = min(1024, total - off)
            stg = wcs_sb[:, 0:n]
            ld = dma(stg, src[:, off:off + n], [("Sv", p.v("Sv"))], sem="Swc")
            p.emit("vector",
                   lambda e, o=dst[:, off:off + n], i=stg: e.tensor_copy(o, i),
                   [("Swc", ld)], "Sv")
            off += n
    for kk in range(8):
        ld = dma(da_sb[:, 0:512], alog[:, kk * 512:(kk + 1) * 512],
                 [("Sv", p.v("Sv")), ("Ss", p.v("Ss"))], sem="Swc")
        a1 = act(da_sb[:, 512:1024], da_sb[:, 0:512], AF.Exp,
                 waits=[("Swc", ld)])
        p.emit("vector",
               lambda e, kk=kk: e.tensor_scalar_mul(
                   _t(ea_sb, kk, 512), da_sb[:, 512:1024], -1.0),
               [("Ss", a1)], "Sv")
    p.emit("vector", lambda e: e.memset(st_sb[:], 0.0), [], "Sv")
    p.emit("vector", lambda e: e.memset(xb_sb[:, 0:3], 0.0), [], "Sv")
    p.emit("vector", lambda e: e.memset(xb_sb[:, 515:518], 0.0), [], "Sv")

    bank = {k: ("Ss", 0) for k in ("AD", "WE", "CE", "Y", "PR", "XE")}
    xw_free = ("St", 0)
    dl_free = [("St", 0)] * 8
    w_free = [("St", 0)] * 8
    c_free = [("St", 0)] * 8
    b_free = [("St", 0)] * 2
    u_freev = [("Sv", 0)] * 8
    xld_cast = [("Sv", 0), ("Sv", 0)]
    yps_free = ("So", 0)
    da_free = [("Sv", 0), ("Sv", 0)]
    we_freed = ("Sv", 0)
    ce_freed = [("Sg", 0), ("Sg", 0)]
    ch_free = [("St", 0), ("St", 0), ("St", 0)]
    be_freed = [("Sv", 0), ("Sv", 0)]
    xbc_free = ("Sv", 0)
    tmp_free = ("Sv", 0)
    acc_free = ("Ss", 0)
    xb_read = [("Sgx0", 0), ("Sgx1", 0)]
    halo_done = ("Ss", 0)
    wcb_free = [("St", 0), ("St", 0)]
    wcs_free = ("Sv", 0)
    slot_free = {0: ("Sg", 0), 1: ("Sg", 0), 2: ("Sg", 0), 3: ("Sg", 0)}
    projb = [("PR", P_PR[:]), ("XE", P_XE[:])]
    pb_i = [0]

    def nb():
        nm, pt = projb[pb_i[0] % 2]
        pb_i[0] += 1
        return nm, pt

    for c in range(nch):
        if barrier and c > 0:
            bvals = [(sn, p.v(sn)) for sn in p.sems]
            for en in ENGINES:
                p.emit(en, lambda e: e.nop(), bvals, None)
        col = slice(c * TC, (c + 1) * TC)
        last_q = 0
        for t in range(16):
            par = t % 2
            sx = "Sx%d" % par
            ld = dma(_t(x_ld, par, 512), xt[128 * t:128 * (t + 1), col],
                     [xld_cast[par]], sem=sx)
            cwv = p.emit("vector",
                         lambda e, t=t, par=par: e.tensor_copy(
                             _t(xw, t, 512), _t(x_ld, par, 512)),
                         [(sx, ld), xw_free] if t == 0 else [(sx, ld)],
                         "Sv")
            xld_cast[par] = ("Sv", cwv)
            sq = p.emit("vector",
                        lambda e, t=t: e.tensor_tensor(
                            xsq[:], _t(xw, t, 512), _t(xw, t, 512), AL.mult),
                        [("St", last_q)], "Sv")
            mm(P_PR[0:1, :], onesd_sb[:], _t(xw, t, 512), t == 0, t == 15,
               [("Sv", cwv)] + ([bank["PR"], ("Si", v0)] if t == 0 else []))
            last_q = mm(P_XE[0:1, :], onesd_sb[:], xsq[:], t == 0, t == 15,
                        [("Sv", sq)] + ([bank["XE"]] if t == 0 else []))
        stats_done = p.v("St")
        c1 = vcp(mean_f[:], P_PR[0:1, :], [("St", stats_done)])
        vcp(mean_bf[:], P_PR[0:1, :])
        c2 = vcp(msq_f[:], P_XE[0:1, :])
        vtt(m2_f[:], mean_f[:], mean_f[:], AL.mult, [])
        v_var = vtt(sd_f[:], msq_f[:], m2_f[:], AL.subtract, [])
        a_ln = act(rr_f[:], sd_f[:], AF.Ln, bias=EPS1,
                   waits=[("Sv", v_var)])
        a_rs = act(sd_f[:], rr_f[:], AF.Exp, scale=NEGH)
        v_rrb = vcp(rr_bf[:], sd_f[:], [("Ss", a_rs)])
        mm(P_PR[:], ones1_sb[:], mean_bf[:], True, True,
           [("Sv", v_rrb), bank["PR"]])
        rb = mm(P_XE[:], ones1_sb[:], rr_bf[:], True, True, [bank["XE"]])
        e_mb = sccp(mbc_sb[:], P_PR[:], [("St", rb)])
        e_rb = sccp(rbc_sb[:], P_XE[:], [])
        bank["PR"] = ("Ss", e_mb)
        bank["XE"] = ("Ss", e_rb)
        lnt = 0
        tfree = [0, 0]
        for t in range(16):
            par = t % 2
            sx = "Sx%d" % par
            ld2 = dma(_t(x_ld, par, 512), xt[128 * t:128 * (t + 1), col],
                      [xld_cast[par]], sem=sx)
            tb = tmpa_sb if par == 0 else tmpb_sb
            w1 = [(sx, ld2)]
            if t == 0:
                w1 += [("Ss", e_rb), tmp_free, ("St", stats_done)]
            elif t == 1:
                w1 += [tmp_free]
            else:
                w1 += [("Sv", tfree[par])]
            sub = p.emit("gpsimd",
                         lambda e, par=par, tb=tb: e.tensor_tensor(
                             tb[:], _t(x_ld, par, 512), mbc_sb[:],
                             AL.subtract),
                         w1, "Sg")
            lnt = vtt(_t(xw, t, 512), tb[:], rbc_sb[:], AL.mult,
                      [("Sg", sub)])
            tfree[par] = lnt
            xld_cast[par] = ("Sg", sub)
        tmp_free = ("Sv", lnt)
        xn_done = p.v("Sv")

        nm, pt = nb()
        lastm = 0
        for k in range(16):
            lastm = mm(pt, wdtd_sb[:, k * 128:k * 128 + 128], _t(xw, k, 512),
                       k == 0, k == 15,
                       [bank[nm], ("Sv", xn_done)] if k == 0 else [])
        e1 = sccp(dt1_sb[:], pt, [("St", lastm)])
        bank[nm] = ("Ss", e1)

        def proj16(w_handle, mt, extra, nk=16):
            nm_, pt_ = nb()
            last_ = 0
            for k in range(nk):
                lh = w_handle[:, (mt * nk + k) * 128:(mt * nk + k) * 128 + 128]
                last_ = mm(pt_, lh, _t(xw, k, 512), k == 0, k == nk - 1,
                           ([bank[nm_]] + extra) if k == 0 else [])
            return nm_, pt_, last_

        def emit_delta(m):
            nonlocal tmp_free
            nm_, pt_ = nb()
            mv = mm(pt_, wdt_sb[:, 128 * m:128 * m + 128], dt1_sb[:],
                    True, True, [bank[nm_], ("Ss", e1), dl_free[m]])
            bias = bdt2_sb[:, m:m + 1]
            act(tmpa_sb[:], pt_, AF.Abs, bias=bias,
                waits=[("St", mv), tmp_free])
            act(tmpb_sb[:], tmpa_sb[:], AF.Exp, scale=NEG1)
            act(tmpa_sb[:], tmpb_sb[:], AF.Ln, bias=ONE)
            a4 = act(tmpb_sb[:], pt_, AF.Relu, bias=bias)
            bank[nm_] = ("Ss", a4)
            dv_ = vtt(_t(dl_sb, m, 512), tmpa_sb[:], tmpb_sb[:], AL.add,
                      [("Ss", a4)])
            tmp_free = ("Sv", dv_)
            return dv_

        def emit_cproj(m):
            nonlocal wcs_free
            wpar = m % 2
            for hh in range(2):
                ldw = dma(wcs_sb[:], wc[:, m * 2048 + 1024 * hh:
                                        m * 2048 + 1024 * hh + 1024],
                          [wcs_free], sem="Swc")
                cv = p.emit("vector",
                            lambda e, wpar=wpar, hh=hh: e.tensor_copy(
                                wcb_sb[:, wpar * 2048 + 1024 * hh:
                                       wpar * 2048 + 1024 * hh + 1024],
                                wcs_sb[:]),
                            [("Swc", ldw)] +
                            ([wcb_free[wpar]] if hh == 0 else []), "Sv")
                wcs_free = ("Sv", cv)
            nm_, pt_ = nb()
            last_ = 0
            for k in range(16):
                last_ = mm(pt_, wcb_sb[:, wpar * 2048 + k * 128:
                                       wpar * 2048 + k * 128 + 128],
                           _t(xw, k, 512), k == 0, k == 15,
                           [bank[nm_], ("Sv", cv), c_free[m]] if k == 0
                           else [])
            wcb_free[wpar] = ("St", last_)
            ev_ = sccp(_t(c_sb, m, 512), pt_, [("St", last_)])
            bank[nm_] = ("Ss", ev_)
            return ev_

        def emit_xb_b(m):
            nm_, pt_, lastm_ = proj16(wx_sb, m, [halo_done] if m == 0 else [])
            ev_ = act(xb_sb[:, 515 * m + 3:515 * m + 515], pt_, AF.Identity,
                      bias=bx_sb[:, m:m + 1],
                      waits=[("St", lastm_)] + xb_read)
            bank[nm_] = ("Ss", ev_)
            nm_, pt_, lastm_ = proj16(wb_sb, m, [b_free[m]])
            ev_ = sccp(_t(b_sb, m, 512), pt_, [("St", lastm_)])
            bank[nm_] = ("Ss", ev_)
            return ev_

        def emit_conv(k):
            nonlocal xbc_free, acc_free
            bt, o = k // 4, (32 * k) % 128
            sgx = "Sgx%d" % (k % 2)
            g_ = 0
            for hb in range(2):
                for rr in range(4):
                    first = hb == 0 and rr == 0
                    g_ = dma(xbc_sb[hb * 64 + rr * 16:hb * 64 + rr * 16 + 16,
                                    :],
                             xb_sb[o + 16 * hb:o + 16 * hb + 16,
                                   515 * bt:515 * bt + 515],
                             [("Ss", p.v("Ss")), xbc_free] if first else [],
                             sem=sgx, throttle=first)
            t0 = p.emit("vector",
                        lambda e, k=k: e.tensor_scalar(
                            _t(acc_sb, 0, 512), xbc_sb[:, 0:512],
                            cw_sb[:, 4 * k:4 * k + 1], None, AL.mult),
                        [(sgx, g_), acc_free], "Sv")
            for tp_ in range(1, 4):
                t0 = p.emit("vector",
                            lambda e, k=k, tp_=tp_: e.scalar_tensor_tensor(
                                _t(acc_sb, tp_ % 2, 512),
                                xbc_sb[:, tp_:tp_ + 512],
                                cw_sb[:, 4 * k + tp_:4 * k + tp_ + 1],
                                _t(acc_sb, (tp_ + 1) % 2, 512),
                                AL.mult, AL.add),
                            [], "Sv")
            xbc_free = ("Sv", t0)
            su = act(_t(u_sb, k, 512), _t(acc_sb, 1, 512), AF.Silu,
                     bias=cb_sb[:, k:k + 1],
                     waits=[("Sv", t0), u_freev[k]])
            acc_free = ("Ss", su)
            wv = vtt(_t(w_sb, k, 512), _t(dl_sb, k, 512), _t(u_sb, k, 512),
                     AL.mult, [("Ss", su), w_freetk[k]])
            return wv

        w_freetk = w_free

        dlv = [0] * 8
        cev = [0] * 8
        wvv = [0] * 8
        bev = [0, 0]
        dlv[0] = emit_delta(0)
        cev[0] = emit_cproj(0)
        bev[0] = emit_xb_b(0)
        dlv[1] = emit_delta(1)
        cev[1] = emit_cproj(1)
        wvv[0] = emit_conv(0)

        pend_ch = [None]
        pend_ys = [None]
        pend_evac = [None]

        def emit_evac(kb, ysb):
            nonlocal yps_free
            ev = p.emit("vector",
                        lambda e, kb=kb: e.scalar_tensor_tensor(
                            yps_sb[:], _t(u_sb, kb, 512),
                            d_sb[:, kb:kb + 1], P_Y[:], AL.mult, AL.add),
                        [("St", ysb), yps_free], "Sv")
            bank["Y"] = ("Sv", ev)
            dma(yp[128 * kb:128 * kb + 128, col], yps_sb[:],
                [("Sv", ev)], sem="So")
            yps_free = ("So", p.v("So"))
            u_freev[kb] = ("Sv", ev)

        def drain_ch(fence_w):
            if pend_ch[0] is None:
                return
            Pq, ecq = pend_ch[0]
            pend_ch[0] = None
            c2, c3 = Pq % 2, Pq % 3
            sl0q = 2 * (Pq % 2)
            ch = p.emit("gpsimd",
                        lambda e, c2=c2, c3=c3, sl0q=sl0q: e.tensor_tensor(
                            _t(ch_sb, c3, 1024),
                            hp_sb[:, sl0q * 512:sl0q * 512 + 1024],
                            ce_sb[:, c2 * 512:c2 * 512 + 512]
                            .unsqueeze(1).broadcast_to([128, 2, 512]),
                            AL.mult),
                        [fence_w, ("Ss", ecq), ch_free[c3]], "Sg")
            ce_freed[c2] = ("Sg", ch)
            slot_free[sl0q] = ("Sg", ch)
            slot_free[sl0q + 1] = ("Sg", ch)
            pend_ys[0] = (Pq, ch)

        def drain_ysum():
            if pend_ys[0] is None:
                return
            Pq, ch = pend_ys[0]
            pend_ys[0] = None
            c3 = Pq % 3
            if pend_evac[0] is not None:
                kb, ysb = pend_evac[0]
                pend_evac[0] = None
                emit_evac(kb, ysb)
            ys = 0
            for jj, half in ((2 * Pq, 0), (2 * Pq + 1, 1)):
                cc = (jj % 16) // 4
                tpp = (0, 96) if cc == 3 else None
                ys = mm(P_Y[32 * cc:32 * cc + 32, :],
                        s_sb[:, 32 * (jj % 4):32 * (jj % 4) + 32],
                        ch_sb[:, c3 * 1024 + half * 512:
                              c3 * 1024 + half * 512 + 512],
                        jj % 4 == 0, jj % 4 == 3,
                        [("Sg", ch)] + ([bank["Y"]] if jj % 16 == 0
                                        else []),
                        tp=tpp, skip=True)
            ch_free[c3] = ("St", ys)
            if Pq % 8 == 7:
                pend_evac[0] = (Pq // 8, ys)

        for k in range(8):
            if k + 2 < 8:
                dlv[k + 2] = emit_delta(k + 2)
                cev[k + 2] = emit_cproj(k + 2)
            if k == 2:
                bev[1] = emit_xb_b(1)
            if k + 1 < 8:
                wvv[k + 1] = emit_conv(k + 1)
            if k == 7:
                xbr = [("Sgx0", p.v("Sgx0")), ("Sgx1", p.v("Sgx1"))]
                h1 = sccp(xb_sb[:, 0:3], xb_sb[:, 512:515], xbr)
                h2 = sccp(xb_sb[:, 515:518], xb_sb[:, 1027:1030], [])
                xb_read = xbr

            for pr in range(8):
                P = 8 * k + pr
                j0, j1 = 2 * P, 2 * P + 1
                ppar = P % 2
                cpar = P % 2
                base = 32 * ((j0 % 16) // 4)
                kt = k
                be_w = None
                if P % 4 == 0:
                    g = P // 4
                    gp = g % 2
                    bb = 32 * ((g % 8) // 2)
                    mb_ = mm(P_CE[:],
                             e16_sb[bb:bb + 32, 128 * (g % 2):
                                    128 * (g % 2) + 128],
                             b_sb[bb:bb + 32,
                                  (g // 8) * 512:(g // 8) * 512 + 512],
                             True, True,
                             [bank["CE"], ("Ss", bev[g // 8])],
                             tp=(bb, 0))
                    be_w = mb_
                for jj, half in ((j0, 0), (j1, 1)):
                    eidx = 4 * (jj // 16) + (jj % 4)
                    mm(P_AD[:, half * 512:half * 512 + 512],
                       ea_sb[base:base + 32, 128 * eidx:128 * eidx + 128],
                       dl_sb[base:base + 32, kt * 512:kt * 512 + 512],
                       True, True,
                       ([bank["AD"], ("Sv", dlv[k])] if half == 0 else []),
                       tp=(base, 0))
                ma = p.v("St")
                da = act(_t(da_sb, ppar, 1024), P_AD[:], AF.Exp,
                         waits=[("St", ma), da_free[ppar]])
                bank["AD"] = ("Ss", da)
                if be_w is not None:
                    gp = (P // 4) % 2
                    eb = sccp(be_sb[:, gp * 512:gp * 512 + 512], P_CE[:],
                              [("St", be_w), be_freed[gp]])
                    bank["CE"] = ("Ss", eb)
                    be_ready = eb
                for jj, half in ((j0, 0), (j1, 1)):
                    mm(P_WE[:, half * 512:half * 512 + 512],
                       e8_sb[base:base + 32,
                             128 * (jj % 4):128 * (jj % 4) + 128],
                       w_sb[base:base + 32, kt * 512:kt * 512 + 512],
                       True, True,
                       ([bank["WE"], we_freed, ("Sv", wvv[k])]
                        if half == 0 else []),
                       tp=(base, 0))
                mw = p.v("St")
                bc_ = 32 * ((P % 8) // 2)
                mc_ = mm(P_CE[:],
                         e16b_sb[bc_:bc_ + 32, 128 * (P % 2):
                                 128 * (P % 2) + 128],
                         c_sb[bc_:bc_ + 32, kt * 512:kt * 512 + 512],
                         True, True, [bank["CE"], ("Ss", cev[k])],
                         tp=(bc_, 0))
                ec = sccp(ce_sb[:, cpar * 512:cpar * 512 + 512], P_CE[:],
                          [("St", mc_), ce_freed[cpar]])
                bank["CE"] = ("Ss", ec)
                gp = (P // 4) % 2
                db = vtt(dbu_sb[:], P_WE[:],
                         be_sb[:, gp * 512:gp * 512 + 512]
                         .unsqueeze(1).broadcast_to([128, 2, 512]),
                         AL.mult,
                         [("St", mw), ("Ss", be_ready), ("Ss", da)])
                we_freed = ("Sv", db)
                if P % 4 == 3:
                    be_freed[gp] = ("Sv", db)
                if j0 % 4 == 0 and P > 0:
                    p.emit("vector",
                           lambda e, j0=j0: e.tensor_copy(
                               st_sb[:, j0 - 4:j0].unsqueeze(2),
                               hp_sb[:].rearrange("p (s t) -> p s t", s=4)
                               [:, :, 511:512]),
                           [], "Sv")
                drain_ysum()
                drain_ch(("Sv", db))
                sc = 0
                for jj, half in ((j0, 0), (j1, 1)):
                    slot = jj % 4
                    sc = p.emit("vector",
                                lambda e, slot=slot, ppar=ppar, jj=jj,
                                half=half:
                                e.tensor_tensor_scan(
                                    _t(hp_sb, slot, 512),
                                    da_sb[:, ppar * 1024 + half * 512:
                                          ppar * 1024 + half * 512 + 512],
                                    dbu_sb[:, half * 512:half * 512 + 512],
                                    st_sb[:, jj:jj + 1], AL.mult, AL.add),
                                [slot_free[slot]], "Sv")
                da_free[ppar] = ("Sv", sc)
                pend_ch[0] = (P, ec)
            dl_free[k] = ("St", p.v("St"))
            w_free[k] = ("St", p.v("St"))
            c_free[k] = ("St", p.v("St"))
        flet = p.emit("vector",
                      lambda e: e.tensor_copy(fence_sb[:], da_sb[:, 0:16]),
                      [], "Sv")
        drain_ysum()
        drain_ch(("Sv", flet))
        drain_ysum()
        kb_, ysb_ = pend_evac[0]
        pend_evac[0] = None
        emit_evac(kb_, ysb_)
        p.emit("vector",
               lambda e: e.tensor_copy(
                   st_sb[:, NT - 4:NT].unsqueeze(2),
                   hp_sb[:].rearrange("p (s t) -> p s t", s=4)
                   [:, :, 511:512]),
               [], "Sv")
        halo_done = ("Ss", p.v("Ss"))
        b_free = [("St", p.v("St"))] * 2
        xw_free = ("St", p.v("St"))
        if dbg:
            nchk = l // TC
            for bufs, dst, nt_, cast in ((xw, xn_d, 16, False),
                                         (dl_sb, dl_d, 8, True),
                                         (u_sb, u_d, 8, True),
                                         (b_sb, b_d, 2, True),
                                         (c_sb, c_d, 8, False)):
                for tt in range(nt_):
                    srcap = _t(bufs, tt, 512)
                    if cast:
                        srcap = srcap.bitcast(F32)
                    dma(dst[:, (tt * nchk + c) * 512:
                            (tt * nchk + c) * 512 + 512],
                        srcap, [("Sv", p.v("Sv")), ("Sg", p.v("Sg"))],
                        sem="So", throttle=False)
            for tt in range(4):
                dma(h_d[:, (tt * nchk + c) * 512:(tt * nchk + c) * 512 + 512],
                    _t(hp_sb, tt, 512), [("Sv", p.v("Sv"))], sem="So",
                    throttle=False)
            p.emit("sync", lambda e: e.nop(),
                   [(sn, p.v(sn)) for sn in p.sems if sn != "Sd"],
                   "Sd", 16)
            bar = ("Sd", p.v("Sd"))
            xld_cast = [bar, bar]
            dl_free = [bar] * 8
            w_free = [bar] * 8
            c_free = [bar] * 8
            u_freev = [bar] * 8
            b_free = [bar] * 2
            tmp_free = bar
            for _sl in range(4):
                slot_free[_sl] = bar

    std_o = DT("st_d", [128, 128], F32, kind="ExternalOutput").ap()
    dma(std_o, st_sb[:], [("Sv", p.v("Sv"))], sem="So")
    p.emit("sync", lambda e: e.nop(), [("So", p.v("So"))], None)
    p.build()
    return nc



def build_p2(t2=T2, dbg=False):
    nc = bass.Bass("TRN2", target_bir_lowering=False,
                   dynamic_dma_scratch_size=4096)
    p = Prog(nc)
    for s in ("Sd", "St", "Ss", "Sv", "Sg", "So", "Si",
              "Sx0", "Sx1", "Sw0", "Sw1", "Sy0", "Sy1",
              "Sst0", "Sst1", "Sp0", "Sp1", "Swd0", "Swd1",
              "Spl", "Sl0", "Sl1", "So0", "So1"):
        p.sem(s)
    C = Common(p)
    dma, mm, act, sccp, vtt, vcp = C.dma, C.mm, C.act, C.sccp, C.vtt, C.vcp
    ncc = t2 // 512

    DT = nc.dram_tensor
    xt2 = DT("xt2", [DM, t2], F32, kind="ExternalInput").ap()
    yp2 = DT("yp2", [DIN, t2], F32, kind="ExternalInput").ap()
    wz = DT("wz", [128, 32 * 16 * 128], F32, kind="ExternalInput").ap()
    wo = DT("wo", [128, 16 * 32 * 128], F32, kind="ExternalInput").ap()
    wg = DT("wg", [128, 44 * 16 * 128], I32, kind="ExternalInput").ap()
    wu = DT("wu", [128, 44 * 16 * 128], I32, kind="ExternalInput").ap()
    wd = DT("wd", [128, 16 * 44 * 128], I32, kind="ExternalInput").ap()
    bzi = DT("bz", [128, 32], F32, kind="ExternalInput").ap()
    boi = DT("bo", [128, 16], F32, kind="ExternalInput").ap()
    sgi = DT("sg", [128, 44], F32, kind="ExternalInput").ap()
    sui = DT("su", [128, 44], F32, kind="ExternalInput").ap()
    sdi = DT("sdn", [128, 16], F32, kind="ExternalInput").ap()
    onesdi = DT("onesd", [128, 1], BF16, kind="ExternalInput").ap()
    ones1i = DT("ones1", [1, 128], BF16, kind="ExternalInput").ap()
    csti = DT("cst", [128, 4], F32, kind="ExternalInput").ap()
    outt = DT("outt", [DM, t2], F32, kind="ExternalOutput").ap()
    s_dr = DT("s_dr", [DM, t2], F32,
              kind="ExternalOutput" if dbg else "Internal").ap()
    if dbg:
        g_d = DT("g_d", [DM, t2], F32, kind="ExternalOutput").ap()
        sp_d = DT("sp_d", [DIN, t2], BF16, kind="ExternalOutput").ap()
    p_dr = DT("p_dr", [HID, t2], BF16, kind="Internal").ap()
    wd_dr = DT("wd_dr", [128, 16 * 44 * 128], BF16, kind="Internal").ap()

    A = nc.alloc_sbuf_tensor
    arena = A("arena", [128, 96 * 1024], mybir.dt.uint8)
    ARE = arena[:].bitcast(BF16)
    xn2 = ARE[:, 0:16 * t2]
    spc = ARE[:, 16 * t2:16 * t2 + 32 * 512]
    pch = ARE[:, 0:44 * 1024]
    x_ld = A("x_ld", [128, 2 * 512], F32)
    xsq = A("xsq", [128, 512], BF16)
    mean_f = A("mean_f", [1, 512], F32)
    msq_f = A("msq_f", [1, 512], F32)
    m2_f = A("m2_f", [1, 512], F32)
    sd_f = A("sd_f", [1, 512], F32)
    rr_f = A("rr_f", [1, 512], F32)
    mean_bf = A("mean_bf", [1, 512], BF16)
    rr_bf = A("rr_bf", [1, 512], BF16)
    mbc_sb = A("mbc_sb", [128, 512], F32)
    rbc_sb = A("rbc_sb", [128, 512], F32)
    tmpa_sb = A("tmpa_sb", [128, 512], F32)
    onesd_sb = A("onesd_sb", [128, 1], BF16)
    ones1_sb = A("ones1_sb", [1, 128], BF16)
    cst_sb = A("cst_sb", [128, 4], F32)
    bz_sb = A("bz_sb", [128, 32], F32)
    bo_sb = A("bo_sb", [128, 16], F32)
    sg_sb = A("sg_sb", [128, 44], F32)
    su_sb = A("su_sb", [128, 44], F32)
    sd2_sb = A("sd2_sb", [128, 16], F32)
    wst = A("wst", [128, 2 * 5632], F32)
    wbf = A("wbf", [128, 2 * 5632], BF16)
    ypt = A("ypt", [128, 2 * 512], F32)
    szt = A("szt", [128, 512], F32)
    gt_sb = A("gt_sb", [128, 512], F32)
    ut_sb = A("ut_sb", [128, 512], F32)
    pt_sb = A("pt_sb", [128, 2 * 512], BF16)
    st2_sb = A("st2_sb", [128, 2 * 512], F32)
    g2_sb = A("g2_sb", [128, 512], F32)
    sgl_sb = A("sgl_sb", [128, 2 * 512], F32)
    xtl_sb = A("xtl_sb", [128, 2 * 512], F32)
    ot_sb = A("ot_sb", [128, 2 * 512], F32)

    AP_ = nc.alloc_psum_tensor
    P_A = AP_("P_A", [128, 512], F32)
    P_B = AP_("P_B", [128, 512], F32)
    P_C = AP_("P_C", [128, 512], F32)
    P_D = AP_("P_D", [128, 512], F32)
    banks = [("A", P_A[:]), ("B", P_B[:]), ("C", P_C[:]), ("D", P_D[:])]

    EPS1 = cst_sb[0:1, 2:3]
    NEGH = cst_sb[0:1, 3:4]

    for dst, src in ((bz_sb, bzi), (bo_sb, boi), (sg_sb, sgi), (su_sb, sui),
                     (sd2_sb, sdi), (onesd_sb, onesdi), (ones1_sb, ones1i),
                     (cst_sb, csti)):
        dma(dst[:], src, sem="Si", throttle=False)
    v0 = p.v("Si")

    bank = {k: ("Ss", 0) for k in ("A", "B", "C", "D")}
    xld_cast = [("Sv", 0), ("Sv", 0)]
    wst_free = [("Sv", 0), ("Sv", 0)]
    wbf_free = [("St", 0), ("St", 0)]

    def xslice(t, c):
        off = (t * ncc + c) * 512
        return xn2[:, off:off + 512]

    for c in range(ncc):
        col = slice(c * 512, (c + 1) * 512)
        last_q = 0
        for t in range(16):
            par = t % 2
            sx = "Sx%d" % par
            ld = dma(_t(x_ld, par, 512), xt2[128 * t:128 * (t + 1), col],
                     [xld_cast[par]], sem=sx)
            cwv = p.emit("vector",
                         lambda e, t=t, par=par, c=c: e.tensor_copy(
                             xslice(t, c), _t(x_ld, par, 512)),
                         [(sx, ld)], "Sv")
            xld_cast[par] = ("Sv", cwv)
            sq = p.emit("vector",
                        lambda e, t=t, c=c: e.tensor_tensor(
                            xsq[:], xslice(t, c), xslice(t, c), AL.mult),
                        [("St", last_q)], "Sv")
            mm(P_A[0:1, :], onesd_sb[:], xslice(t, c), t == 0, t == 15,
               [("Sv", cwv)] + ([bank["A"], ("Si", v0)] if t == 0 else []))
            last_q = mm(P_B[0:1, :], onesd_sb[:], xsq[:], t == 0, t == 15,
                        [("Sv", sq)] + ([bank["B"]] if t == 0 else []))
        stats_done = p.v("St")
        c1 = vcp(mean_f[:], P_A[0:1, :], [("St", stats_done)])
        vcp(mean_bf[:], P_A[0:1, :])
        c2 = vcp(msq_f[:], P_B[0:1, :])
        vtt(m2_f[:], mean_f[:], mean_f[:], AL.mult, [])
        v_var = vtt(sd_f[:], msq_f[:], m2_f[:], AL.subtract, [])
        a_ln = act(rr_f[:], sd_f[:], AF.Ln, bias=EPS1,
                   waits=[("Sv", v_var)])
        a_rs = act(sd_f[:], rr_f[:], AF.Exp, scale=NEGH)
        v_rrb = vcp(rr_bf[:], sd_f[:], [("Ss", a_rs)])
        mm(P_A[:], ones1_sb[:], mean_bf[:], True, True,
           [("Sv", v_rrb), bank["A"]])
        rb = mm(P_B[:], ones1_sb[:], rr_bf[:], True, True, [bank["B"]])
        e_mb = sccp(mbc_sb[:], P_A[:], [("St", rb)])
        e_rb = sccp(rbc_sb[:], P_B[:], [])
        bank["A"] = ("Ss", e_mb)
        bank["B"] = ("Ss", e_rb)
        lnt = 0
        for t in range(16):
            par = t % 2
            sx = "Sx%d" % par
            ld2 = dma(_t(x_ld, par, 512), xt2[128 * t:128 * (t + 1), col],
                      [xld_cast[par]], sem=sx)
            w1 = [(sx, ld2), ("Ss", e_rb)]
            if t == 0:
                w1.append(("St", stats_done))
            lnt = vtt(tmpa_sb[:], _t(x_ld, par, 512), mbc_sb[:], AL.subtract,
                      w1)
            lnt = vtt(xslice(t, c), tmpa_sb[:], rbc_sb[:], AL.mult, [])
            xld_cast[par] = ("Sv", lnt)
    xn_done = p.v("Sv")

    def wload(src, mt, width, dt_in, par, waits=()):
        sw = "Sw%d" % par
        if dt_in == F32:
            stg = wst[:, par * 5632:par * 5632 + width]
        else:
            stg = wst[:, par * 5632:par * 5632 + width].bitcast(I32)
        ld = dma(stg, src[:, mt * width:mt * width + width],
                 [wst_free[par]] + list(waits), sem=sw)
        dstb = wbf[:, par * 5632:par * 5632 + width]
        if dt_in == F32:
            cv = p.emit("vector",
                        lambda e, o=dstb, i=stg: e.tensor_copy(o, i),
                        [(sw, ld), wbf_free[par]], "Sv")
        else:
            cv = p.emit("vector",
                        lambda e, o=dstb, i=stg: e.tensor_scalar(
                            o, i, 128.0, None, AL.subtract),
                        [(sw, ld), wbf_free[par]], "Sv")
        wst_free[par] = ("Sv", cv)
        return dstb, cv

    spc_free = ("St", 0)
    ypt_free = [("Sv", 0), ("Sv", 0)]
    szt_free = ("Sv", 0)
    st2_free = [("Sst0", 0), ("Sst1", 0)]
    for c in range(ncc):
        for m in range(32):
            par = m % 2
            dstb, cv = wload(wz, m, 2048, F32, par,
                             [("Sv", xn_done)] if (c == 0 and m < 2) else ())
            nm, pt_ = banks[m % 4]
            last = 0
            for k in range(16):
                last = mm(pt_, dstb[:, k * 128:k * 128 + 128], xslice(k, c),
                          k == 0, k == 15,
                          [("Sv", cv), bank[nm]] if k == 0 else [])
            wbf_free[par] = ("St", last)
            a_sz = act(szt[:], pt_, AF.Silu, bias=bz_sb[:, m:m + 1],
                       waits=[("St", last), szt_free])
            bank[nm] = ("Ss", a_sz)
            ldy = dma(_t(ypt, par, 512),
                      yp2[128 * m:128 * m + 128, c * 512:c * 512 + 512],
                      [ypt_free[par]], sem="Sy%d" % par)
            sv = vtt(_t(spc, m, 512), _t(ypt, par, 512), szt[:], AL.mult,
                     [("Ss", a_sz), ("Sy%d" % par, ldy), spc_free] if m == 0
                     else [("Ss", a_sz), ("Sy%d" % par, ldy)])
            ypt_free[par] = ("Sv", sv)
            szt_free = ("Sv", sv)
        sp_done = p.v("Sv")
        if dbg:
            for mm_ in range(32):
                dma(sp_d[128 * mm_:128 * mm_ + 128, c * 512:c * 512 + 512],
                    _t(spc, mm_, 512), [("Sv", sp_done)], sem="So",
                    throttle=False)
        for m in range(16):
            par = m % 2
            dstb, cv = wload(wo, m, 4096, F32, par)
            nm, pt_ = banks[m % 4]
            last = 0
            for k in range(32):
                last = mm(pt_, dstb[:, k * 128:k * 128 + 128],
                          _t(spc, k, 512), k == 0, k == 31,
                          [("Sv", max(cv, sp_done)), bank[nm]] if k == 0
                          else [])
            wbf_free[par] = ("St", last)
            ev = act(_t(st2_sb, par, 512), pt_, AF.Identity,
                     bias=bo_sb[:, m:m + 1],
                     waits=[("St", last), st2_free[par]])
            bank[nm] = ("Ss", ev)
            sd_ = dma(s_dr[128 * m:128 * m + 128, c * 512:c * 512 + 512],
                      _t(st2_sb, par, 512), [("Ss", ev)], sem="Sst%d" % par)
            st2_free[par] = ("Sst%d" % par, sd_)
        spc_free = ("St", p.v("St"))

    pt_free = [("Sp0", 0), ("Sp1", 0)]
    gu_free = ("Sv", 0)
    for hm in range(44):
        dg, cg = wload(wg, hm, 2048, I32, 0)
        du, cu = wload(wu, hm, 2048, I32, 1)
        lg = lu = 0
        for c in range(ncc):
            for k in range(16):
                lg = mm(P_A[:], dg[:, k * 128:k * 128 + 128], xslice(k, c),
                        k == 0, k == 15,
                        [("Sv", cg), bank["A"]] if k == 0 else [])
            for k in range(16):
                lu = mm(P_B[:], du[:, k * 128:k * 128 + 128], xslice(k, c),
                        k == 0, k == 15,
                        [("Sv", cu), bank["B"]] if k == 0 else [])
            ag = act(gt_sb[:], P_A[:], AF.Silu, scale=sg_sb[:, hm:hm + 1],
                     waits=[("St", lg), gu_free])
            au = act(ut_sb[:], P_B[:], AF.Identity, scale=su_sb[:, hm:hm + 1],
                     waits=[("St", lu)])
            bank["A"] = ("Ss", ag)
            bank["B"] = ("Ss", au)
            ppar = c % 2
            pv = vtt(_t(pt_sb, ppar, 512), gt_sb[:], ut_sb[:], AL.mult,
                     [("Ss", au), pt_free[ppar]])
            gu_free = ("Sv", pv)
            pd = dma(p_dr[128 * hm:128 * hm + 128, c * 512:c * 512 + 512],
                     _t(pt_sb, ppar, 512), [("Sv", pv)], sem="Sp%d" % ppar)
            pt_free[ppar] = ("Sp%d" % ppar, pd)
        wbf_free[0] = ("St", lg)
        wbf_free[1] = ("St", lu)

    for m in range(16):
        par = m % 2
        dstb, cv = wload(wd, m, 44 * 128, I32, par)
        wds = dma(wd_dr[:, m * 44 * 128:(m + 1) * 44 * 128], dstb,
                  [("Sv", cv)], sem="Swd%d" % par)
        wbf_free[par] = ("Swd%d" % par, wds)

    p_stored = [("Sp0", p.v("Sp0")), ("Sp1", p.v("Sp1"))]
    wd_stored = [("Swd0", p.v("Swd0")), ("Swd1", p.v("Swd1"))]
    s_stored = [("Sst0", p.v("Sst0")), ("Sst1", p.v("Sst1"))]
    pch_free = ("St", p.v("St"))
    g_done = ("Sv", 0)
    sgl_free = [("Sv", 0), ("Sv", 0)]
    xtl_free = [("Sv", 0), ("Sv", 0)]
    ot_free = [("So0", 0), ("So1", 0)]
    for c3 in range(t2 // 1024):
        ldp = 0
        for hh in range(44):
            first = hh == 0
            ldp = dma(_t(pch, hh, 1024),
                      p_dr[128 * hh:128 * hh + 128,
                           c3 * 1024:c3 * 1024 + 1024],
                      ([pch_free] + (p_stored if c3 == 0 else []))
                      if first else [],
                      sem="Spl", throttle=first)
        for m in range(16):
            par = m % 2
            stg = wst[:, par * 5632:par * 5632 + 2816].bitcast(BF16)
            ldw = dma(stg, wd_dr[:, m * 44 * 128:(m + 1) * 44 * 128],
                      [wst_free[par]] +
                      (wd_stored if (c3 == 0 and m < 2) else []),
                      sem="Sw%d" % par)
            for half in range(2):
                nm, pt_ = banks[(m * 2 + half) % 4]
                last = 0
                for k in range(44):
                    last = mm(pt_, stg[:, k * 128:k * 128 + 128],
                              pch[:, (k * 2 + half) * 512:
                                  (k * 2 + half) * 512 + 512],
                              k == 0, k == 43,
                              [("Sw%d" % par, ldw), ("Spl", ldp), bank[nm]]
                              if k == 0 else [])
                ag = act(g2_sb[:], pt_, AF.Silu, scale=sd2_sb[:, m:m + 1],
                         waits=[("St", last), g_done])
                bank[nm] = ("Ss", ag)
                colh = slice(c3 * 1024 + half * 512,
                             c3 * 1024 + half * 512 + 512)
                if dbg:
                    dma(g_d[128 * m:128 * m + 128, colh], g2_sb[:],
                        [("Ss", ag)], sem="So", throttle=False)
                lds = dma(_t(sgl_sb, half, 512),
                          s_dr[128 * m:128 * m + 128, colh],
                          [sgl_free[half]] +
                          (s_stored if (c3 == 0 and m == 0) else []),
                          sem="Sl%d" % half)
                ldx = dma(_t(xtl_sb, half, 512),
                          xt2[128 * m:128 * m + 128, colh],
                          [xtl_free[half]], sem="Sx%d" % half)
                v1 = vtt(_t(ot_sb, half, 512), g2_sb[:],
                         _t(sgl_sb, half, 512), AL.mult,
                         [("Ss", ag), ("Sl%d" % half, lds),
                          ("Sx%d" % half, ldx), ot_free[half]])
                g_done = ("Sv", v1)
                sgl_free[half] = ("Sv", v1)
                v2 = vtt(_t(ot_sb, half, 512), _t(ot_sb, half, 512),
                         _t(xtl_sb, half, 512), AL.add, [])
                xtl_free[half] = ("Sv", v2)
                od = dma(outt[128 * m:128 * m + 128, colh],
                         _t(ot_sb, half, 512), [("Sv", v2)],
                         sem="So%d" % half)
                ot_free[half] = ("So%d" % half, od)
            wst_free[par] = ("St", p.v("St"))
        pch_free = ("St", p.v("St"))

    p.emit("sync", lambda e: e.nop(),
           [("So0", p.v("So0")), ("So1", p.v("So1")), ("So", p.v("So"))],
           None)
    p.build()
    return nc



def _tile_w(wT):
    K, M = wT.shape
    nk, nm = K // 128, M // 128
    a = wT.reshape(nk, 128, nm, 128).transpose(1, 2, 0, 3).reshape(128, -1)
    return np.ascontiguousarray(a)


def _pp(vec):
    n = vec.shape[0] // 128
    return np.ascontiguousarray(vec.reshape(n, 128).T)


def _mk_e8():
    e = np.zeros((128, 512), np.float32)
    for g4 in range(4):
        for v in range(4):
            for dd in range(8):
                for s in range(16):
                    e[32 * g4 + 8 * v + dd, 128 * v + 16 * dd + s] = 1.0
    return e


def _mk_e16():
    e = np.zeros((128, 256), np.float32)
    for g4 in range(4):
        for v in range(2):
            for dd in range(8):
                for s in range(16):
                    e[32 * g4 + 16 * v + s, 128 * v + 16 * dd + s] = 1.0
    return e


def _mk_smat():
    s = np.zeros((128, 128), np.float32)
    for m in range(4):
        for pp_ in range(128):
            s[pp_, 32 * m + 8 * m + pp_ // 16] = 1.0
    return s


def _mk_alog_ea(alog_slice):
    out = np.full((128, 4096), -1e30, np.float32)
    for j in range(128):
        g4 = (j % 16) // 4
        eidx = 4 * (j // 16) + (j % 4)
        r8 = 8 * (j % 4)
        for dd in range(8):
            out[32 * g4 + r8 + dd,
                128 * eidx + 16 * dd:128 * eidx + 16 * dd + 16] = \
                alog_slice[8 * j + dd, :]
    return out


_CACHE = {}


def kernel(**inp):
    x = np.ascontiguousarray(np.asarray(inp["L0_attn_hidden_states"],
                                        np.float32))
    Wx = np.asarray(inp["L0_attn_Wx"], np.float32)
    bx = np.asarray(inp["L0_attn_bx"], np.float32)
    Wz = np.asarray(inp["L0_attn_Wz"], np.float32)
    bz = np.asarray(inp["L0_attn_bz"], np.float32)
    conv_w = np.asarray(inp["L0_attn_conv_w"], np.float32)
    conv_b = np.asarray(inp["L0_attn_conv_b"], np.float32)
    WB = np.asarray(inp["L0_attn_WB"], np.float32)
    WC = np.asarray(inp["L0_attn_WC"], np.float32)
    Wdtd = np.asarray(inp["L0_attn_Wdt_down"], np.float32)
    Wdt = np.asarray(inp["L0_attn_Wdt"], np.float32)
    bdt = np.asarray(inp["L0_attn_bdt"], np.float32)
    A_log = np.asarray(inp["L0_attn_A_log"], np.float32)
    Dv = np.asarray(inp["L0_attn_D"], np.float32)
    Wout = np.asarray(inp["L0_attn_Wout"], np.float32)
    bout = np.asarray(inp["L0_attn_bout"], np.float32)
    wqg = np.asarray(inp["L0_ffn_wq_gate"], np.int32)
    sg = np.asarray(inp["L0_ffn_s_gate"], np.float32)
    wqu = np.asarray(inp["L0_ffn_wq_up"], np.int32)
    su = np.asarray(inp["L0_ffn_s_up"], np.float32)
    wqd = np.asarray(inp["L0_ffn_wq_down"], np.int32)
    sdn = np.asarray(inp["L0_ffn_s_down"], np.float32)

    if "p1" not in _CACHE:
        _CACHE["p1"] = build_p1()
        _CACHE["p2"] = build_p2()
    nc1, nc2_ = _CACHE["p1"], _CACHE["p2"]

    e8 = _mk_e8()
    e16 = _mk_e16()
    e16b = e16.astype(BFNP)
    smat = _mk_smat().astype(BFNP)
    cst = np.tile(np.array([[-1.0, 1.0, 1e-5, -0.5]], np.float32), (128, 1))
    onesd = np.full((128, 1), 1.0 / 2048.0, BFNP)
    ones1 = np.ones((1, 128), BFNP)

    xts = [np.ascontiguousarray(x[b].T) for b in range(2)]

    in1 = []
    for core in range(8):
        b, q = core // 4, core % 4
        sl1 = slice(DSL * q, DSL * (q + 1))
        sl4 = slice(256 * q, 256 * (q + 1))
        in1.append({
            "xt": xts[b],
            "wx": _tile_w(np.ascontiguousarray(Wx[sl4].T)),
            "wbm": _tile_w(np.ascontiguousarray(WB[sl4].T)),
            "wc": _tile_w(np.ascontiguousarray(WC[sl1].T)),
            "wdtd": _tile_w(np.ascontiguousarray(Wdtd.T)),
            "wdt": _tile_w(np.ascontiguousarray(Wdt[sl1].T)),
            "alog": _mk_alog_ea(A_log[sl1]),
            "e8": e8, "e16": e16, "e16b": e16b, "smat": smat,
            "cw": np.ascontiguousarray(
                conv_w[sl1, 0, :].reshape(8, 128, 4).transpose(1, 0, 2)
                .reshape(128, 32)),
            "cb": _pp(conv_b[sl1]),
            "bx": _pp(bx[sl4]),
            "bdt": _pp(bdt[sl1]),
            "dvec": _pp(Dv[sl1]),
            "cst": cst, "onesd": onesd, "ones1": ones1,
        })
    r1 = run_bass_kernel_spmd(nc1, in1, core_ids=list(range(8)))
    ypl = [np.asarray(r1.results[i]["yp"]) for i in range(8)]

    wz_t = _tile_w(np.ascontiguousarray(Wz.T))
    wo_t = _tile_w(np.ascontiguousarray(Wout.T))
    wg_t = _tile_w(np.ascontiguousarray(wqg.T))
    wu_t = _tile_w(np.ascontiguousarray(wqu.T))
    wd_t = _tile_w(np.ascontiguousarray(wqd.T))
    bz_p = _pp(bz)
    bo_p = _pp(bout)
    sg_p = _pp(sg)
    su_p = _pp(su)
    sd_p = _pp(sdn)

    in2 = []
    for core in range(8):
        b, q = core // 4, core % 4
        ybf = np.concatenate([ypl[4 * b + qq] for qq in range(4)], axis=0)
        in2.append({
            "xt2": np.ascontiguousarray(xts[b][:, T2 * q:T2 * (q + 1)]),
            "yp2": np.ascontiguousarray(ybf[:, T2 * q:T2 * (q + 1)]),
            "wz": wz_t, "wo": wo_t, "wg": wg_t, "wu": wu_t, "wd": wd_t,
            "bz": bz_p, "bo": bo_p, "sg": sg_p, "su": su_p, "sdn": sd_p,
            "onesd": onesd, "ones1": ones1, "cst": cst,
        })
    r2 = run_bass_kernel_spmd(nc2_, in2, core_ids=list(range(8)))

    out = np.empty((2, L, DM), np.float32)
    for core in range(8):
        b, q = core // 4, core % 4
        out[b, T2 * q:T2 * (q + 1), :] = np.asarray(
            r2.results[core]["outt"]).T
    return out


# version 08
# speedup vs baseline: 1.4619x; 1.0895x over previous
import numpy as np
import ml_dtypes
import concourse.bass as bass
import concourse.mybir as mybir
from concourse.bass_utils import run_bass_kernel_spmd

F32 = mybir.dt.float32
F32R = mybir.dt.float32r
BF16 = mybir.dt.bfloat16
I32 = mybir.dt.int32
AL = mybir.AluOpType
AF = mybir.ActivationFunctionType
BFNP = ml_dtypes.bfloat16

DM = 2048
DIN = 4096
DSL = 1024
L = 8192
TC = 512
T2 = 2048
NT = 128
HID = 5632

ENGINES = ("sync", "tensor", "scalar", "vector", "gpsimd")


class Prog:

    def __init__(self, nc):
        self.nc = nc
        self.ops = {e: [] for e in ENGINES}
        self.cnt = {}
        self.sems = {}

    def sem(self, name):
        if name not in self.sems:
            self.sems[name] = self.nc.alloc_semaphore(name)
            self.cnt[name] = 0
        return self.sems[name]

    def emit(self, eng, fn, waits=(), inc=None, incn=1):
        waits = tuple((s, v) for (s, v) in waits if v > 0)
        for s, v in waits:
            assert v <= self.cnt[s], (eng, s, v, self.cnt[s])
        self.ops[eng].append((fn, waits, inc, incn))
        if inc is not None:
            self.cnt[inc] += incn
            return self.cnt[inc]
        return None

    def v(self, name):
        return self.cnt.get(name, 0)

    def build(self):
        sems = self.sems
        with self.nc.Block() as blk:
            for ename in ENGINES:
                ops = self.ops[ename]
                if not ops:
                    continue

                def make(ops=ops):
                    def body(eng):
                        for fn, waits, inc, incn in ops:
                            for s, v in waits:
                                eng.wait_ge(sems[s], v)
                            inst = fn(eng)
                            if inc is not None:
                                inst.then_inc(sems[inc], incn)
                    return body

                getattr(blk, ename)(make())


def _t(h, i, w):
    return h[:, i * w:(i + 1) * w]


class Common:
    def __init__(self, p):
        self.p = p

    def dma(self, dst, src, waits=(), sem="Sd", throttle=True):
        w = list(waits)
        if throttle:
            w.append((sem, self.p.v(sem)))
        return self.p.emit("sync", lambda e, d=dst, s=src: e.dma_start(d, s),
                           w, sem, 16)

    def mm(self, pt, lh, rh, st, sp, waits=(), tp=None, skip=False):
        def f(e, pt=pt, lh=lh, rh=rh, st=st, sp=sp, tp=tp, skip=skip):
            return e.matmul(pt, lh, rh, start=st, stop=sp, tile_position=tp,
                            skip_group_check=skip)
        return self.p.emit("tensor", f, waits, "St")

    def act(self, out, in_, fn, bias=0.0, scale=1.0, waits=()):
        def f(e, out=out, in_=in_, fn=fn, bias=bias, scale=scale):
            return e.activation(out, in_, fn, bias=bias, scale=scale)
        return self.p.emit("scalar", f, waits, "Ss")

    def sccp(self, out, in_, waits=()):
        return self.p.emit("scalar", lambda e, o=out, i=in_: e.copy(o, i),
                           waits, "Ss")

    def vtt(self, out, a, b, op, waits=()):
        return self.p.emit("vector", lambda e, o=out, a=a, b=b, op=op:
                           e.tensor_tensor(o, a, b, op), waits, "Sv")

    def vcp(self, out, in_, waits=()):
        return self.p.emit("vector", lambda e, o=out, i=in_:
                           e.tensor_copy(o, i), waits, "Sv")




def build_p1(l=L, dbg=False, barrier=False):
    nch = l // TC
    nc = bass.Bass("TRN2", target_bir_lowering=False,
                   dynamic_dma_scratch_size=2048)
    p = Prog(nc)
    for s in ("Sd", "St", "Ss", "Sv", "Sg", "So",
              "Si", "Swc", "Sx0", "Sx1", "Sgx0", "Sgx1"):
        p.sem(s)
    C = Common(p)
    dma, mm, act, sccp, vtt, vcp = C.dma, C.mm, C.act, C.sccp, C.vtt, C.vcp

    DT = nc.dram_tensor
    xt = DT("xt", [DM, l], F32, kind="ExternalInput").ap()
    wx = DT("wx", [128, 2 * 16 * 128], F32, kind="ExternalInput").ap()
    wbm = DT("wbm", [128, 2 * 16 * 128], F32, kind="ExternalInput").ap()
    wc = DT("wc", [128, 8 * 16 * 128], F32, kind="ExternalInput").ap()
    wdtd = DT("wdtd", [128, 16 * 128], F32, kind="ExternalInput").ap()
    wdt = DT("wdt", [128, 8 * 128], F32, kind="ExternalInput").ap()
    alog = DT("alog", [128, 4096], F32, kind="ExternalInput").ap()
    e8i = DT("e8", [128, 512], F32R, kind="ExternalInput").ap()
    e16i = DT("e16", [128, 256], F32R, kind="ExternalInput").ap()
    e16bi = DT("e16b", [128, 256], BF16, kind="ExternalInput").ap()
    smati = DT("smat", [128, 128], BF16, kind="ExternalInput").ap()
    cwi = DT("cw", [128, 32], F32, kind="ExternalInput").ap()
    cbi = DT("cb", [128, 8], F32, kind="ExternalInput").ap()
    bxi = DT("bx", [128, 2], F32, kind="ExternalInput").ap()
    bdti = DT("bdt", [128, 8], F32, kind="ExternalInput").ap()
    dvi = DT("dvec", [128, 8], F32, kind="ExternalInput").ap()
    csti = DT("cst", [128, 4], F32, kind="ExternalInput").ap()
    onesdi = DT("onesd", [128, 1], BF16, kind="ExternalInput").ap()
    ones1i = DT("ones1", [1, 128], BF16, kind="ExternalInput").ap()
    yp = DT("yp", [DSL, l], F32, kind="ExternalOutput").ap()
    if dbg:
        xn_d = DT("xn_d", [128, 16 * l], BF16, kind="ExternalOutput").ap()
        dl_d = DT("dl_d", [128, 8 * l], F32, kind="ExternalOutput").ap()
        u_d = DT("u_d", [128, 8 * l], F32, kind="ExternalOutput").ap()
        b_d = DT("b_d", [128, 2 * l], F32, kind="ExternalOutput").ap()
        c_d = DT("c_d", [128, 8 * l], BF16, kind="ExternalOutput").ap()
        h_d = DT("h_d", [128, 4 * l], F32, kind="ExternalOutput").ap()

    A = nc.alloc_sbuf_tensor
    wx_sb = A("wx_sb", [128, 2 * 16 * 128], BF16)
    wb_sb = A("wb_sb", [128, 2 * 16 * 128], BF16)
    wdtd_sb = A("wdtd_sb", [128, 16 * 128], BF16)
    wdt_sb = A("wdt_sb", [128, 8 * 128], F32R)
    wcb_sb = A("wcb_sb", [128, 2 * 2048], BF16)
    ea_sb = A("ea_sb", [128, 4096], F32R)
    e8_sb = A("e8_sb", [128, 512], F32R)
    e16_sb = A("e16_sb", [128, 256], F32R)
    e16b_sb = A("e16b_sb", [128, 256], BF16)
    s_sb = A("s_sb", [128, 128], BF16)
    cw_sb = A("cw_sb", [128, 32], F32)
    cb_sb = A("cb_sb", [128, 8], F32)
    bx_sb = A("bx_sb", [128, 2], F32)
    bdt2_sb = A("bdt2_sb", [128, 8], F32)
    d_sb = A("d_sb", [128, 8], F32)
    cst_sb = A("cst_sb", [128, 4], F32)
    onesd_sb = A("onesd_sb", [128, 1], BF16)
    ones1_sb = A("ones1_sb", [1, 128], BF16)
    x_ld = A("x_ld", [128, 2 * 512], F32)
    xw = A("xw", [128, 16 * 512], BF16)
    xsq = A("xsq", [128, 512], BF16)
    mean_f = A("mean_f", [1, 512], F32)
    msq_f = A("msq_f", [1, 512], F32)
    m2_f = A("m2_f", [1, 512], F32)
    sd_f = A("sd_f", [1, 512], F32)
    rr_f = A("rr_f", [1, 512], F32)
    mean_bf = A("mean_bf", [1, 512], BF16)
    rr_bf = A("rr_bf", [1, 512], BF16)
    mbc_sb = A("mbc_sb", [128, 512], F32)
    rbc_sb = A("rbc_sb", [128, 512], F32)
    dt1_sb = A("dt1_sb", [128, 512], F32R)
    dl_sb = A("dl_sb", [128, 8 * 512], F32R)
    u_sb = A("u_sb", [128, 8 * 512], F32R)
    wcs_sb = A("wcs_sb", [128, 1024], F32)
    w_sb = A("w_sb", [128, 8 * 512], F32R)
    c_sb = A("c_sb", [128, 8 * 512], BF16)
    b_sb = A("b_sb", [128, 2 * 512], F32R)
    xb_sb = A("xb_sb", [128, 2 * 515], F32)
    xbc_sb = A("xbc_sb", [128, 515], F32)
    tmpa_sb = A("tmpa_sb", [128, 512], F32)
    tmpb_sb = A("tmpb_sb", [128, 512], F32)
    acc_sb = A("acc_sb", [128, 2 * 512], F32)
    da_sb = A("da_sb", [128, 2 * 1024], F32)
    dbu_sb = A("dbu_sb", [128, 1024], F32)
    hp_sb = A("hp_sb", [128, 4 * 512], F32)
    st_sb = A("st_sb", [128, 128], F32)
    be_sb = A("be_sb", [128, 2 * 512], F32)
    ce_sb = A("ce_sb", [128, 2 * 512], F32)
    ch_sb = A("ch_sb", [128, 3 * 1024], BF16)
    yps_sb = A("yps_sb", [128, 512], F32)
    fence_sb = A("fence_sb", [128, 16], F32)

    AP_ = nc.alloc_psum_tensor
    P_AD = AP_("P_AD", [128, 2 * 512], F32)
    P_WE = AP_("P_WE", [128, 2 * 512], F32)
    P_CE = AP_("P_CE", [128, 512], F32)
    P_Y = AP_("P_Y", [128, 512], F32)
    P_PR = AP_("P_PR", [128, 512], F32)
    P_XE = AP_("P_XE", [128, 512], F32)

    NEG1 = cst_sb[:, 0:1]
    ONE = cst_sb[:, 1:2]
    EPS1 = cst_sb[0:1, 2:3]
    NEGH = cst_sb[0:1, 3:4]

    for dst, src in ((cw_sb, cwi), (cb_sb, cbi), (bx_sb, bxi),
                     (bdt2_sb, bdti), (d_sb, dvi), (cst_sb, csti),
                     (e8_sb, e8i), (e16_sb, e16i), (e16b_sb, e16bi),
                     (s_sb, smati), (onesd_sb, onesdi), (ones1_sb, ones1i)):
        dma(dst[:], src, sem="Si", throttle=False)
    v0 = p.v("Si")
    bdt2v = p.emit("vector",
                   lambda e: e.tensor_scalar_mul(bdt2_sb[:], bdt2_sb[:], 2.0),
                   [("Si", v0)], "Sv")
    for (src, dst, ncols) in ((wx, wx_sb, 2 * 16), (wbm, wb_sb, 2 * 16),
                              (wdtd, wdtd_sb, 16), (wdt, wdt_sb, 8)):
        total = ncols * 128
        off = 0
        while off < total:
            n = min(1024, total - off)
            stg = wcs_sb[:, 0:n]
            ld = dma(stg, src[:, off:off + n], [("Sv", p.v("Sv"))], sem="Swc")
            p.emit("vector",
                   lambda e, o=dst[:, off:off + n], i=stg: e.tensor_copy(o, i),
                   [("Swc", ld)], "Sv")
            off += n
    for kk in range(8):
        ld = dma(da_sb[:, 0:512], alog[:, kk * 512:(kk + 1) * 512],
                 [("Sv", p.v("Sv")), ("Ss", p.v("Ss"))], sem="Swc")
        a1 = act(da_sb[:, 512:1024], da_sb[:, 0:512], AF.Exp,
                 waits=[("Swc", ld)])
        p.emit("vector",
               lambda e, kk=kk: e.tensor_scalar_mul(
                   _t(ea_sb, kk, 512), da_sb[:, 512:1024], -1.0),
               [("Ss", a1)], "Sv")
    p.emit("vector", lambda e: e.memset(st_sb[:], 0.0), [], "Sv")
    p.emit("vector", lambda e: e.memset(xb_sb[:, 0:3], 0.0), [], "Sv")
    p.emit("vector", lambda e: e.memset(xb_sb[:, 515:518], 0.0), [], "Sv")

    bank = {k: ("Ss", 0) for k in ("AD", "WE", "CE", "Y", "PR", "XE")}
    xw_free = ("St", 0)
    dl_free = [("St", 0)] * 8
    w_free = [("St", 0)] * 8
    c_free = [("St", 0)] * 8
    b_free = [("St", 0)] * 2
    u_freev = [("Sv", 0)] * 8
    xld_cast = [("Sv", 0), ("Sv", 0)]
    yps_free = ("So", 0)
    da_free = [("Sv", 0), ("Sv", 0)]
    we_freed = ("Sv", 0)
    ce_freed = [("Sg", 0), ("Sg", 0)]
    ch_free = [("St", 0), ("St", 0), ("St", 0)]
    be_freed = [("Sv", 0), ("Sv", 0)]
    xbc_free = ("Sv", 0)
    tmp_free = ("Sv", 0)
    acc_free = ("Ss", 0)
    xb_read = [("Sgx0", 0), ("Sgx1", 0)]
    halo_done = ("Ss", 0)
    wcb_free = [("St", 0), ("St", 0)]
    wcs_free = ("Sv", 0)
    slot_free = {0: ("Sg", 0), 1: ("Sg", 0), 2: ("Sg", 0), 3: ("Sg", 0)}
    projb = [("PR", P_PR[:]), ("XE", P_XE[:])]
    pb_i = [0]

    def nb():
        nm, pt = projb[pb_i[0] % 2]
        pb_i[0] += 1
        return nm, pt

    for c in range(nch):
        if barrier and c > 0:
            bvals = [(sn, p.v(sn)) for sn in p.sems]
            for en in ENGINES:
                p.emit(en, lambda e: e.nop(), bvals, None)
        col = slice(c * TC, (c + 1) * TC)
        last_q = 0
        for t in range(16):
            par = t % 2
            sx = "Sx%d" % par
            ld = dma(_t(x_ld, par, 512), xt[128 * t:128 * (t + 1), col],
                     [xld_cast[par]], sem=sx)
            cwv = p.emit("vector",
                         lambda e, t=t, par=par: e.tensor_copy(
                             _t(xw, t, 512), _t(x_ld, par, 512)),
                         [(sx, ld), xw_free] if t == 0 else [(sx, ld)],
                         "Sv")
            xld_cast[par] = ("Sv", cwv)
            sq = p.emit("vector",
                        lambda e, t=t: e.tensor_tensor(
                            xsq[:], _t(xw, t, 512), _t(xw, t, 512), AL.mult),
                        [("St", last_q)], "Sv")
            mm(P_PR[0:1, :], onesd_sb[:], _t(xw, t, 512), t == 0, t == 15,
               [("Sv", cwv)] + ([bank["PR"], ("Si", v0)] if t == 0 else []))
            last_q = mm(P_XE[0:1, :], onesd_sb[:], xsq[:], t == 0, t == 15,
                        [("Sv", sq)] + ([bank["XE"]] if t == 0 else []))
        stats_done = p.v("St")
        c1 = vcp(mean_f[:], P_PR[0:1, :], [("St", stats_done)])
        vcp(mean_bf[:], P_PR[0:1, :])
        c2 = vcp(msq_f[:], P_XE[0:1, :])
        vtt(m2_f[:], mean_f[:], mean_f[:], AL.mult, [])
        v_var = vtt(sd_f[:], msq_f[:], m2_f[:], AL.subtract, [])
        a_ln = act(rr_f[:], sd_f[:], AF.Ln, bias=EPS1,
                   waits=[("Sv", v_var)])
        a_rs = act(sd_f[:], rr_f[:], AF.Exp, scale=NEGH)
        v_rrb = vcp(rr_bf[:], sd_f[:], [("Ss", a_rs)])
        mm(P_PR[:], ones1_sb[:], mean_bf[:], True, True,
           [("Sv", v_rrb), bank["PR"]])
        rb = mm(P_XE[:], ones1_sb[:], rr_bf[:], True, True, [bank["XE"]])
        e_mb = sccp(mbc_sb[:], P_PR[:], [("St", rb)])
        e_rb = sccp(rbc_sb[:], P_XE[:], [])
        bank["PR"] = ("Ss", e_mb)
        bank["XE"] = ("Ss", e_rb)
        lnt = 0
        tfree = [0, 0]
        for t in range(16):
            par = t % 2
            sx = "Sx%d" % par
            ld2 = dma(_t(x_ld, par, 512), xt[128 * t:128 * (t + 1), col],
                      [xld_cast[par]], sem=sx)
            tb = tmpa_sb if par == 0 else tmpb_sb
            w1 = [(sx, ld2)]
            if t == 0:
                w1 += [("Ss", e_rb), tmp_free, ("St", stats_done)]
            elif t == 1:
                w1 += [tmp_free]
            else:
                w1 += [("Sv", tfree[par])]
            sub = p.emit("gpsimd",
                         lambda e, par=par, tb=tb: e.tensor_tensor(
                             tb[:], _t(x_ld, par, 512), mbc_sb[:],
                             AL.subtract),
                         w1, "Sg")
            lnt = vtt(_t(xw, t, 512), tb[:], rbc_sb[:], AL.mult,
                      [("Sg", sub)])
            tfree[par] = lnt
            xld_cast[par] = ("Sg", sub)
        tmp_free = ("Sv", lnt)
        xn_done = p.v("Sv")

        nm, pt = nb()
        lastm = 0
        for k in range(16):
            lastm = mm(pt, wdtd_sb[:, k * 128:k * 128 + 128], _t(xw, k, 512),
                       k == 0, k == 15,
                       [bank[nm], ("Sv", xn_done)] if k == 0 else [])
        e1 = sccp(dt1_sb[:], pt, [("St", lastm)])
        bank[nm] = ("Ss", e1)

        def proj16(w_handle, mt, extra, nk=16):
            nm_, pt_ = nb()
            last_ = 0
            for k in range(nk):
                lh = w_handle[:, (mt * nk + k) * 128:(mt * nk + k) * 128 + 128]
                last_ = mm(pt_, lh, _t(xw, k, 512), k == 0, k == nk - 1,
                           ([bank[nm_]] + extra) if k == 0 else [])
            return nm_, pt_, last_

        def emit_delta(m):
            nonlocal tmp_free
            nm_, pt_ = nb()
            mv = mm(pt_, wdt_sb[:, 128 * m:128 * m + 128], dt1_sb[:],
                    True, True, [bank[nm_], ("Ss", e1), dl_free[m]])
            bias = bdt2_sb[:, m:m + 1]
            act(tmpa_sb[:], pt_, AF.Abs, bias=bias,
                waits=[("St", mv), tmp_free])
            act(tmpb_sb[:], tmpa_sb[:], AF.Exp, scale=NEG1)
            act(tmpa_sb[:], tmpb_sb[:], AF.Ln, bias=ONE)
            a4 = act(tmpb_sb[:], pt_, AF.Relu, bias=bias)
            bank[nm_] = ("Ss", a4)
            dv_ = vtt(_t(dl_sb, m, 512), tmpa_sb[:], tmpb_sb[:], AL.add,
                      [("Ss", a4)])
            tmp_free = ("Sv", dv_)
            return dv_

        def emit_cproj(m):
            nonlocal wcs_free
            wpar = m % 2
            for hh in range(2):
                ldw = dma(wcs_sb[:], wc[:, m * 2048 + 1024 * hh:
                                        m * 2048 + 1024 * hh + 1024],
                          [wcs_free], sem="Swc")
                cv = p.emit("vector",
                            lambda e, wpar=wpar, hh=hh: e.tensor_copy(
                                wcb_sb[:, wpar * 2048 + 1024 * hh:
                                       wpar * 2048 + 1024 * hh + 1024],
                                wcs_sb[:]),
                            [("Swc", ldw)] +
                            ([wcb_free[wpar]] if hh == 0 else []), "Sv")
                wcs_free = ("Sv", cv)
            nm_, pt_ = nb()
            last_ = 0
            for k in range(16):
                last_ = mm(pt_, wcb_sb[:, wpar * 2048 + k * 128:
                                       wpar * 2048 + k * 128 + 128],
                           _t(xw, k, 512), k == 0, k == 15,
                           [bank[nm_], ("Sv", cv), c_free[m]] if k == 0
                           else [])
            wcb_free[wpar] = ("St", last_)
            ev_ = sccp(_t(c_sb, m, 512), pt_, [("St", last_)])
            bank[nm_] = ("Ss", ev_)
            return ev_

        def emit_xb_b(m):
            nm_, pt_, lastm_ = proj16(wx_sb, m, [halo_done] if m == 0 else [])
            ev_ = act(xb_sb[:, 515 * m + 3:515 * m + 515], pt_, AF.Identity,
                      bias=bx_sb[:, m:m + 1],
                      waits=[("St", lastm_)] + xb_read)
            bank[nm_] = ("Ss", ev_)
            nm_, pt_, lastm_ = proj16(wb_sb, m, [b_free[m]])
            ev_ = sccp(_t(b_sb, m, 512), pt_, [("St", lastm_)])
            bank[nm_] = ("Ss", ev_)
            return ev_

        def emit_conv(k):
            nonlocal xbc_free, acc_free
            bt, o = k // 4, (32 * k) % 128
            sgx = "Sgx%d" % (k % 2)
            g_ = 0
            for hb in range(2):
                for rr in range(4):
                    first = hb == 0 and rr == 0
                    g_ = dma(xbc_sb[hb * 64 + rr * 16:hb * 64 + rr * 16 + 16,
                                    :],
                             xb_sb[o + 16 * hb:o + 16 * hb + 16,
                                   515 * bt:515 * bt + 515],
                             [("Ss", p.v("Ss")), xbc_free] if first else [],
                             sem=sgx, throttle=first)
            t0 = p.emit("vector",
                        lambda e, k=k: e.tensor_scalar(
                            _t(acc_sb, 0, 512), xbc_sb[:, 0:512],
                            cw_sb[:, 4 * k:4 * k + 1], None, AL.mult),
                        [(sgx, g_), acc_free], "Sv")
            for tp_ in range(1, 4):
                t0 = p.emit("vector",
                            lambda e, k=k, tp_=tp_: e.scalar_tensor_tensor(
                                _t(acc_sb, tp_ % 2, 512),
                                xbc_sb[:, tp_:tp_ + 512],
                                cw_sb[:, 4 * k + tp_:4 * k + tp_ + 1],
                                _t(acc_sb, (tp_ + 1) % 2, 512),
                                AL.mult, AL.add),
                            [], "Sv")
            xbc_free = ("Sv", t0)
            su = act(_t(u_sb, k, 512), _t(acc_sb, 1, 512), AF.Silu,
                     bias=cb_sb[:, k:k + 1],
                     waits=[("Sv", t0), u_freev[k]])
            acc_free = ("Ss", su)
            wv = vtt(_t(w_sb, k, 512), _t(dl_sb, k, 512), _t(u_sb, k, 512),
                     AL.mult, [("Ss", su), w_freetk[k]])
            return wv

        w_freetk = w_free

        cpst = {}

        def emit_cproj_stage(m, hh):
            nonlocal wcs_free
            wpar = m % 2
            ldw = dma(wcs_sb[:], wc[:, m * 2048 + 1024 * hh:
                                    m * 2048 + 1024 * hh + 1024],
                      [wcs_free], sem="Swc")
            cv = p.emit("vector",
                        lambda e, wpar=wpar, hh=hh: e.tensor_copy(
                            wcb_sb[:, wpar * 2048 + 1024 * hh:
                                   wpar * 2048 + 1024 * hh + 1024],
                            wcs_sb[:]),
                        [("Swc", ldw)] +
                        ([wcb_free[wpar]] if hh == 0 else []), "Sv")
            wcs_free = ("Sv", cv)
            if hh == 0:
                cpst[m] = [wpar, cv, 0]
            else:
                cpst[m][2] = cv

        def emit_cproj_mms(m, ks, ke):
            wpar, cv0, cv1 = cpst[m]
            if ks == 0:
                cpst[(m, "b")] = nb()
            nm_, pt_ = cpst[(m, "b")]
            last_ = 0
            for kk in range(ks, ke):
                w0 = []
                if kk == 0:
                    w0 = [bank[nm_], ("Sv", cv0), c_free[m]]
                elif kk == 8:
                    w0 = [("Sv", cv1)]
                last_ = mm(pt_, wcb_sb[:, wpar * 2048 + kk * 128:
                                       wpar * 2048 + kk * 128 + 128],
                           _t(xw, kk, 512), kk == 0, kk == 15, w0)
            if ke == 16:
                wcb_free[wpar] = ("St", last_)
                ev_ = sccp(_t(c_sb, m, 512), pt_, [("St", last_)])
                bank[nm_] = ("Ss", ev_)
                cev[m] = ev_

        def emit_xbproj(which, m, ks, ke):
            key = (which, m)
            if ks == 0:
                cpst[key] = nb()
            nm_, pt_ = cpst[key]
            wh = wx_sb if which == "x" else wb_sb
            last_ = 0
            for kk in range(ks, ke):
                w0 = []
                if kk == 0:
                    w0 = [bank[nm_]]
                    if which == "x" and m == 0:
                        w0.append(halo_done)
                    if which == "b":
                        w0.append(b_free[m])
                last_ = mm(pt_, wh[:, (m * 16 + kk) * 128:
                                  (m * 16 + kk) * 128 + 128],
                           _t(xw, kk, 512), kk == 0, kk == 15, w0)
            if ke == 16:
                if which == "x":
                    ev_ = act(xb_sb[:, 515 * m + 3:515 * m + 515], pt_,
                              AF.Identity, bias=bx_sb[:, m:m + 1],
                              waits=[("St", last_)] + xb_read)
                else:
                    ev_ = sccp(_t(b_sb, m, 512), pt_, [("St", last_)])
                    bev[m] = ev_
                bank[nm_] = ("Ss", ev_)

        dlv = [0] * 8
        cev = [0] * 8
        wvv = [0] * 8
        bev = [0, 0]
        dlv[0] = emit_delta(0)
        cev[0] = emit_cproj(0)
        bev[0] = emit_xb_b(0)
        dlv[1] = emit_delta(1)
        cev[1] = emit_cproj(1)
        wvv[0] = emit_conv(0)

        def filler(k, pr):
            nonlocal xb_read
            if pr == 0 and k + 2 < 8:
                dlv[k + 2] = emit_delta(k + 2)
            elif pr == 1 and k + 2 < 8:
                emit_cproj_stage(k + 2, 0)
            elif pr == 2 and k + 2 < 8:
                emit_cproj_stage(k + 2, 1)
                emit_cproj_mms(k + 2, 0, 4)
            elif pr == 3 and k + 2 < 8:
                emit_cproj_mms(k + 2, 4, 10)
            elif pr == 4 and k + 2 < 8:
                emit_cproj_mms(k + 2, 10, 16)
            elif pr == 5 and k + 1 < 8:
                wvv[k + 1] = emit_conv(k + 1)
            elif pr == 6:
                if k == 2:
                    emit_xbproj("x", 1, 0, 8)
                elif k == 3:
                    emit_xbproj("b", 1, 0, 8)
                elif k == 7:
                    xbr = [("Sgx0", p.v("Sgx0")), ("Sgx1", p.v("Sgx1"))]
                    sccp(xb_sb[:, 0:3], xb_sb[:, 512:515], xbr)
                    sccp(xb_sb[:, 515:518], xb_sb[:, 1027:1030], [])
                    xb_read = xbr
            elif pr == 7:
                if k == 2:
                    emit_xbproj("x", 1, 8, 16)
                elif k == 3:
                    emit_xbproj("b", 1, 8, 16)

        pend_ch = [None]
        pend_ys = [None]
        pend_evac = [None]

        def emit_evac(kb, ysb):
            nonlocal yps_free
            ev = p.emit("vector",
                        lambda e, kb=kb: e.scalar_tensor_tensor(
                            yps_sb[:], _t(u_sb, kb, 512),
                            d_sb[:, kb:kb + 1], P_Y[:], AL.mult, AL.add),
                        [("St", ysb), yps_free], "Sv")
            bank["Y"] = ("Sv", ev)
            dma(yp[128 * kb:128 * kb + 128, col], yps_sb[:],
                [("Sv", ev)], sem="So")
            yps_free = ("So", p.v("So"))
            u_freev[kb] = ("Sv", ev)

        def drain_ch(fence_w):
            if pend_ch[0] is None:
                return
            Pq, ecq = pend_ch[0]
            pend_ch[0] = None
            c2, c3 = Pq % 2, Pq % 3
            sl0q = 2 * (Pq % 2)
            ch = p.emit("gpsimd",
                        lambda e, c2=c2, c3=c3, sl0q=sl0q: e.tensor_tensor(
                            _t(ch_sb, c3, 1024),
                            hp_sb[:, sl0q * 512:sl0q * 512 + 1024],
                            ce_sb[:, c2 * 512:c2 * 512 + 512]
                            .unsqueeze(1).broadcast_to([128, 2, 512]),
                            AL.mult),
                        [fence_w, ("Ss", ecq), ch_free[c3]], "Sg")
            ce_freed[c2] = ("Sg", ch)
            slot_free[sl0q] = ("Sg", ch)
            slot_free[sl0q + 1] = ("Sg", ch)
            pend_ys[0] = (Pq, ch)

        def drain_ysum():
            if pend_ys[0] is None:
                return
            Pq, ch = pend_ys[0]
            pend_ys[0] = None
            c3 = Pq % 3
            if pend_evac[0] is not None:
                kb, ysb = pend_evac[0]
                pend_evac[0] = None
                emit_evac(kb, ysb)
            ys = 0
            for jj, half in ((2 * Pq, 0), (2 * Pq + 1, 1)):
                cc = (jj % 16) // 4
                tpp = (0, 96) if cc == 3 else None
                ys = mm(P_Y[32 * cc:32 * cc + 32, :],
                        s_sb[:, 32 * (jj % 4):32 * (jj % 4) + 32],
                        ch_sb[:, c3 * 1024 + half * 512:
                              c3 * 1024 + half * 512 + 512],
                        jj % 4 == 0, jj % 4 == 3,
                        [("Sg", ch)] + ([bank["Y"]] if jj % 16 == 0
                                        else []),
                        tp=tpp, skip=True)
            ch_free[c3] = ("St", ys)
            if Pq % 8 == 7:
                pend_evac[0] = (Pq // 8, ys)

        for k in range(8):
            for pr in range(8):
                P = 8 * k + pr
                j0, j1 = 2 * P, 2 * P + 1
                ppar = P % 2
                cpar = P % 2
                base = 32 * ((j0 % 16) // 4)
                kt = k
                be_w = None
                if P % 4 == 0:
                    g = P // 4
                    gp = g % 2
                    bb = 32 * ((g % 8) // 2)
                    mb_ = mm(P_CE[:],
                             e16_sb[bb:bb + 32, 128 * (g % 2):
                                    128 * (g % 2) + 128],
                             b_sb[bb:bb + 32,
                                  (g // 8) * 512:(g // 8) * 512 + 512],
                             True, True,
                             [bank["CE"], ("Ss", bev[g // 8])],
                             tp=(bb, 0))
                    be_w = mb_
                for jj, half in ((j0, 0), (j1, 1)):
                    eidx = 4 * (jj // 16) + (jj % 4)
                    mm(P_AD[:, half * 512:half * 512 + 512],
                       ea_sb[base:base + 32, 128 * eidx:128 * eidx + 128],
                       dl_sb[base:base + 32, kt * 512:kt * 512 + 512],
                       True, True,
                       ([bank["AD"], ("Sv", dlv[k])] if half == 0 else []),
                       tp=(base, 0))
                ma = p.v("St")
                da = act(_t(da_sb, ppar, 1024), P_AD[:], AF.Exp,
                         waits=[("St", ma), da_free[ppar]])
                bank["AD"] = ("Ss", da)
                if be_w is not None:
                    gp = (P // 4) % 2
                    eb = sccp(be_sb[:, gp * 512:gp * 512 + 512], P_CE[:],
                              [("St", be_w), be_freed[gp]])
                    bank["CE"] = ("Ss", eb)
                    be_ready = eb
                for jj, half in ((j0, 0), (j1, 1)):
                    mm(P_WE[:, half * 512:half * 512 + 512],
                       e8_sb[base:base + 32,
                             128 * (jj % 4):128 * (jj % 4) + 128],
                       w_sb[base:base + 32, kt * 512:kt * 512 + 512],
                       True, True,
                       ([bank["WE"], we_freed, ("Sv", wvv[k])]
                        if half == 0 else []),
                       tp=(base, 0))
                mw = p.v("St")
                bc_ = 32 * ((P % 8) // 2)
                mc_ = mm(P_CE[:],
                         e16b_sb[bc_:bc_ + 32, 128 * (P % 2):
                                 128 * (P % 2) + 128],
                         c_sb[bc_:bc_ + 32, kt * 512:kt * 512 + 512],
                         True, True, [bank["CE"], ("Ss", cev[k])],
                         tp=(bc_, 0))
                ec = sccp(ce_sb[:, cpar * 512:cpar * 512 + 512], P_CE[:],
                          [("St", mc_), ce_freed[cpar]])
                bank["CE"] = ("Ss", ec)
                gp = (P // 4) % 2
                db = vtt(dbu_sb[:], P_WE[:],
                         be_sb[:, gp * 512:gp * 512 + 512]
                         .unsqueeze(1).broadcast_to([128, 2, 512]),
                         AL.mult,
                         [("St", mw), ("Ss", be_ready), ("Ss", da)])
                we_freed = ("Sv", db)
                if P % 4 == 3:
                    be_freed[gp] = ("Sv", db)
                if j0 % 4 == 0 and P > 0:
                    p.emit("vector",
                           lambda e, j0=j0: e.tensor_copy(
                               st_sb[:, j0 - 4:j0].unsqueeze(2),
                               hp_sb[:].rearrange("p (s t) -> p s t", s=4)
                               [:, :, 511:512]),
                           [], "Sv")
                drain_ysum()
                drain_ch(("Sv", db))
                sc = 0
                for jj, half in ((j0, 0), (j1, 1)):
                    slot = jj % 4
                    sc = p.emit("vector",
                                lambda e, slot=slot, ppar=ppar, jj=jj,
                                half=half:
                                e.tensor_tensor_scan(
                                    _t(hp_sb, slot, 512),
                                    da_sb[:, ppar * 1024 + half * 512:
                                          ppar * 1024 + half * 512 + 512],
                                    dbu_sb[:, half * 512:half * 512 + 512],
                                    st_sb[:, jj:jj + 1], AL.mult, AL.add),
                                [slot_free[slot]], "Sv")
                da_free[ppar] = ("Sv", sc)
                pend_ch[0] = (P, ec)
                filler(k, pr)
            dl_free[k] = ("St", p.v("St"))
            w_free[k] = ("St", p.v("St"))
            c_free[k] = ("St", p.v("St"))
        flet = p.emit("vector",
                      lambda e: e.tensor_copy(fence_sb[:], da_sb[:, 0:16]),
                      [], "Sv")
        drain_ysum()
        drain_ch(("Sv", flet))
        drain_ysum()
        kb_, ysb_ = pend_evac[0]
        pend_evac[0] = None
        emit_evac(kb_, ysb_)
        p.emit("vector",
               lambda e: e.tensor_copy(
                   st_sb[:, NT - 4:NT].unsqueeze(2),
                   hp_sb[:].rearrange("p (s t) -> p s t", s=4)
                   [:, :, 511:512]),
               [], "Sv")
        halo_done = ("Ss", p.v("Ss"))
        b_free = [("St", p.v("St"))] * 2
        xw_free = ("St", p.v("St"))
        if dbg:
            nchk = l // TC
            for bufs, dst, nt_, cast in ((xw, xn_d, 16, False),
                                         (dl_sb, dl_d, 8, True),
                                         (u_sb, u_d, 8, True),
                                         (b_sb, b_d, 2, True),
                                         (c_sb, c_d, 8, False)):
                for tt in range(nt_):
                    srcap = _t(bufs, tt, 512)
                    if cast:
                        srcap = srcap.bitcast(F32)
                    dma(dst[:, (tt * nchk + c) * 512:
                            (tt * nchk + c) * 512 + 512],
                        srcap, [("Sv", p.v("Sv")), ("Sg", p.v("Sg"))],
                        sem="So", throttle=False)
            for tt in range(4):
                dma(h_d[:, (tt * nchk + c) * 512:(tt * nchk + c) * 512 + 512],
                    _t(hp_sb, tt, 512), [("Sv", p.v("Sv"))], sem="So",
                    throttle=False)
            p.emit("sync", lambda e: e.nop(),
                   [(sn, p.v(sn)) for sn in p.sems if sn != "Sd"],
                   "Sd", 16)
            bar = ("Sd", p.v("Sd"))
            xld_cast = [bar, bar]
            dl_free = [bar] * 8
            w_free = [bar] * 8
            c_free = [bar] * 8
            u_freev = [bar] * 8
            b_free = [bar] * 2
            tmp_free = bar
            for _sl in range(4):
                slot_free[_sl] = bar

    std_o = DT("st_d", [128, 128], F32, kind="ExternalOutput").ap()
    dma(std_o, st_sb[:], [("Sv", p.v("Sv"))], sem="So")
    p.emit("sync", lambda e: e.nop(), [("So", p.v("So"))], None)
    p.build()
    return nc



def build_p2(t2=T2, dbg=False):
    nc = bass.Bass("TRN2", target_bir_lowering=False,
                   dynamic_dma_scratch_size=4096)
    p = Prog(nc)
    for s in ("Sd", "St", "Ss", "Sv", "Sg", "So", "Si",
              "Sx0", "Sx1", "Sw0", "Sw1", "Sy0", "Sy1",
              "Sst0", "Sst1", "Sp0", "Sp1", "Swd0", "Swd1",
              "Spl", "Sl0", "Sl1", "So0", "So1"):
        p.sem(s)
    C = Common(p)
    dma, mm, act, sccp, vtt, vcp = C.dma, C.mm, C.act, C.sccp, C.vtt, C.vcp
    ncc = t2 // 512

    DT = nc.dram_tensor
    xt2 = DT("xt2", [DM, t2], F32, kind="ExternalInput").ap()
    yp2 = DT("yp2", [DIN, t2], F32, kind="ExternalInput").ap()
    wz = DT("wz", [128, 32 * 16 * 128], F32, kind="ExternalInput").ap()
    wo = DT("wo", [128, 16 * 32 * 128], F32, kind="ExternalInput").ap()
    wg = DT("wg", [128, 44 * 16 * 128], I32, kind="ExternalInput").ap()
    wu = DT("wu", [128, 44 * 16 * 128], I32, kind="ExternalInput").ap()
    wd = DT("wd", [128, 16 * 44 * 128], I32, kind="ExternalInput").ap()
    bzi = DT("bz", [128, 32], F32, kind="ExternalInput").ap()
    boi = DT("bo", [128, 16], F32, kind="ExternalInput").ap()
    sgi = DT("sg", [128, 44], F32, kind="ExternalInput").ap()
    sui = DT("su", [128, 44], F32, kind="ExternalInput").ap()
    sdi = DT("sdn", [128, 16], F32, kind="ExternalInput").ap()
    onesdi = DT("onesd", [128, 1], BF16, kind="ExternalInput").ap()
    ones1i = DT("ones1", [1, 128], BF16, kind="ExternalInput").ap()
    csti = DT("cst", [128, 4], F32, kind="ExternalInput").ap()
    outt = DT("outt", [DM, t2], F32, kind="ExternalOutput").ap()
    s_dr = DT("s_dr", [DM, t2], F32,
              kind="ExternalOutput" if dbg else "Internal").ap()
    if dbg:
        g_d = DT("g_d", [DM, t2], F32, kind="ExternalOutput").ap()
        sp_d = DT("sp_d", [DIN, t2], BF16, kind="ExternalOutput").ap()
    p_dr = DT("p_dr", [HID, t2], BF16, kind="Internal").ap()
    wd_dr = DT("wd_dr", [128, 16 * 44 * 128], BF16, kind="Internal").ap()

    A = nc.alloc_sbuf_tensor
    arena = A("arena", [128, 96 * 1024], mybir.dt.uint8)
    ARE = arena[:].bitcast(BF16)
    xn2 = ARE[:, 0:16 * t2]
    spc = ARE[:, 16 * t2:16 * t2 + 32 * 512]
    pch = ARE[:, 0:44 * 1024]
    x_ld = A("x_ld", [128, 2 * 512], F32)
    xsq = A("xsq", [128, 512], BF16)
    mean_f = A("mean_f", [1, 512], F32)
    msq_f = A("msq_f", [1, 512], F32)
    m2_f = A("m2_f", [1, 512], F32)
    sd_f = A("sd_f", [1, 512], F32)
    rr_f = A("rr_f", [1, 512], F32)
    mean_bf = A("mean_bf", [1, 512], BF16)
    rr_bf = A("rr_bf", [1, 512], BF16)
    mbc_sb = A("mbc_sb", [128, 512], F32)
    rbc_sb = A("rbc_sb", [128, 512], F32)
    tmpa_sb = A("tmpa_sb", [128, 512], F32)
    onesd_sb = A("onesd_sb", [128, 1], BF16)
    ones1_sb = A("ones1_sb", [1, 128], BF16)
    cst_sb = A("cst_sb", [128, 4], F32)
    bz_sb = A("bz_sb", [128, 32], F32)
    bo_sb = A("bo_sb", [128, 16], F32)
    sg_sb = A("sg_sb", [128, 44], F32)
    su_sb = A("su_sb", [128, 44], F32)
    sd2_sb = A("sd2_sb", [128, 16], F32)
    wst = A("wst", [128, 2 * 5632], F32)
    wbf = A("wbf", [128, 2 * 5632], BF16)
    ypt = A("ypt", [128, 2 * 512], F32)
    szt = A("szt", [128, 512], F32)
    gt_sb = A("gt_sb", [128, 512], F32)
    ut_sb = A("ut_sb", [128, 512], F32)
    pt_sb = A("pt_sb", [128, 2 * 512], BF16)
    st2_sb = A("st2_sb", [128, 2 * 512], F32)
    g2_sb = A("g2_sb", [128, 512], F32)
    sgl_sb = A("sgl_sb", [128, 2 * 512], F32)
    xtl_sb = A("xtl_sb", [128, 2 * 512], F32)
    ot_sb = A("ot_sb", [128, 2 * 512], F32)

    AP_ = nc.alloc_psum_tensor
    P_A = AP_("P_A", [128, 512], F32)
    P_B = AP_("P_B", [128, 512], F32)
    P_C = AP_("P_C", [128, 512], F32)
    P_D = AP_("P_D", [128, 512], F32)
    banks = [("A", P_A[:]), ("B", P_B[:]), ("C", P_C[:]), ("D", P_D[:])]

    EPS1 = cst_sb[0:1, 2:3]
    NEGH = cst_sb[0:1, 3:4]

    for dst, src in ((bz_sb, bzi), (bo_sb, boi), (sg_sb, sgi), (su_sb, sui),
                     (sd2_sb, sdi), (onesd_sb, onesdi), (ones1_sb, ones1i),
                     (cst_sb, csti)):
        dma(dst[:], src, sem="Si", throttle=False)
    v0 = p.v("Si")

    bank = {k: ("Ss", 0) for k in ("A", "B", "C", "D")}
    xld_cast = [("Sv", 0), ("Sv", 0)]
    wst_free = [("Sv", 0), ("Sv", 0)]
    wbf_free = [("St", 0), ("St", 0)]

    def xslice(t, c):
        off = (t * ncc + c) * 512
        return xn2[:, off:off + 512]

    for c in range(ncc):
        col = slice(c * 512, (c + 1) * 512)
        last_q = 0
        for t in range(16):
            par = t % 2
            sx = "Sx%d" % par
            ld = dma(_t(x_ld, par, 512), xt2[128 * t:128 * (t + 1), col],
                     [xld_cast[par]], sem=sx)
            cwv = p.emit("vector",
                         lambda e, t=t, par=par, c=c: e.tensor_copy(
                             xslice(t, c), _t(x_ld, par, 512)),
                         [(sx, ld)], "Sv")
            xld_cast[par] = ("Sv", cwv)
            sq = p.emit("vector",
                        lambda e, t=t, c=c: e.tensor_tensor(
                            xsq[:], xslice(t, c), xslice(t, c), AL.mult),
                        [("St", last_q)], "Sv")
            mm(P_A[0:1, :], onesd_sb[:], xslice(t, c), t == 0, t == 15,
               [("Sv", cwv)] + ([bank["A"], ("Si", v0)] if t == 0 else []))
            last_q = mm(P_B[0:1, :], onesd_sb[:], xsq[:], t == 0, t == 15,
                        [("Sv", sq)] + ([bank["B"]] if t == 0 else []))
        stats_done = p.v("St")
        c1 = vcp(mean_f[:], P_A[0:1, :], [("St", stats_done)])
        vcp(mean_bf[:], P_A[0:1, :])
        c2 = vcp(msq_f[:], P_B[0:1, :])
        vtt(m2_f[:], mean_f[:], mean_f[:], AL.mult, [])
        v_var = vtt(sd_f[:], msq_f[:], m2_f[:], AL.subtract, [])
        a_ln = act(rr_f[:], sd_f[:], AF.Ln, bias=EPS1,
                   waits=[("Sv", v_var)])
        a_rs = act(sd_f[:], rr_f[:], AF.Exp, scale=NEGH)
        v_rrb = vcp(rr_bf[:], sd_f[:], [("Ss", a_rs)])
        mm(P_A[:], ones1_sb[:], mean_bf[:], True, True,
           [("Sv", v_rrb), bank["A"]])
        rb = mm(P_B[:], ones1_sb[:], rr_bf[:], True, True, [bank["B"]])
        e_mb = sccp(mbc_sb[:], P_A[:], [("St", rb)])
        e_rb = sccp(rbc_sb[:], P_B[:], [])
        bank["A"] = ("Ss", e_mb)
        bank["B"] = ("Ss", e_rb)
        lnt = 0
        for t in range(16):
            par = t % 2
            sx = "Sx%d" % par
            ld2 = dma(_t(x_ld, par, 512), xt2[128 * t:128 * (t + 1), col],
                      [xld_cast[par]], sem=sx)
            w1 = [(sx, ld2), ("Ss", e_rb)]
            if t == 0:
                w1.append(("St", stats_done))
            lnt = vtt(tmpa_sb[:], _t(x_ld, par, 512), mbc_sb[:], AL.subtract,
                      w1)
            lnt = vtt(xslice(t, c), tmpa_sb[:], rbc_sb[:], AL.mult, [])
            xld_cast[par] = ("Sv", lnt)
    xn_done = p.v("Sv")

    def wload(src, mt, width, dt_in, par, waits=()):
        sw = "Sw%d" % par
        if dt_in == F32:
            stg = wst[:, par * 5632:par * 5632 + width]
        else:
            stg = wst[:, par * 5632:par * 5632 + width].bitcast(I32)
        ld = dma(stg, src[:, mt * width:mt * width + width],
                 [wst_free[par]] + list(waits), sem=sw)
        dstb = wbf[:, par * 5632:par * 5632 + width]
        if dt_in == F32:
            cv = p.emit("vector",
                        lambda e, o=dstb, i=stg: e.tensor_copy(o, i),
                        [(sw, ld), wbf_free[par]], "Sv")
        else:
            cv = p.emit("vector",
                        lambda e, o=dstb, i=stg: e.tensor_scalar(
                            o, i, 128.0, None, AL.subtract),
                        [(sw, ld), wbf_free[par]], "Sv")
        wst_free[par] = ("Sv", cv)
        return dstb, cv

    spc_free = ("St", 0)
    ypt_free = [("Sv", 0), ("Sv", 0)]
    szt_free = ("Sv", 0)
    st2_free = [("Sst0", 0), ("Sst1", 0)]
    for c in range(ncc):
        for m in range(32):
            par = m % 2
            dstb, cv = wload(wz, m, 2048, F32, par,
                             [("Sv", xn_done)] if (c == 0 and m < 2) else ())
            nm, pt_ = banks[m % 4]
            last = 0
            for k in range(16):
                last = mm(pt_, dstb[:, k * 128:k * 128 + 128], xslice(k, c),
                          k == 0, k == 15,
                          [("Sv", cv), bank[nm]] if k == 0 else [])
            wbf_free[par] = ("St", last)
            a_sz = act(szt[:], pt_, AF.Silu, bias=bz_sb[:, m:m + 1],
                       waits=[("St", last), szt_free])
            bank[nm] = ("Ss", a_sz)
            ldy = dma(_t(ypt, par, 512),
                      yp2[128 * m:128 * m + 128, c * 512:c * 512 + 512],
                      [ypt_free[par]], sem="Sy%d" % par)
            sv = vtt(_t(spc, m, 512), _t(ypt, par, 512), szt[:], AL.mult,
                     [("Ss", a_sz), ("Sy%d" % par, ldy), spc_free] if m == 0
                     else [("Ss", a_sz), ("Sy%d" % par, ldy)])
            ypt_free[par] = ("Sv", sv)
            szt_free = ("Sv", sv)
        sp_done = p.v("Sv")
        if dbg:
            for mm_ in range(32):
                dma(sp_d[128 * mm_:128 * mm_ + 128, c * 512:c * 512 + 512],
                    _t(spc, mm_, 512), [("Sv", sp_done)], sem="So",
                    throttle=False)
        for m in range(16):
            par = m % 2
            dstb, cv = wload(wo, m, 4096, F32, par)
            nm, pt_ = banks[m % 4]
            last = 0
            for k in range(32):
                last = mm(pt_, dstb[:, k * 128:k * 128 + 128],
                          _t(spc, k, 512), k == 0, k == 31,
                          [("Sv", max(cv, sp_done)), bank[nm]] if k == 0
                          else [])
            wbf_free[par] = ("St", last)
            ev = act(_t(st2_sb, par, 512), pt_, AF.Identity,
                     bias=bo_sb[:, m:m + 1],
                     waits=[("St", last), st2_free[par]])
            bank[nm] = ("Ss", ev)
            sd_ = dma(s_dr[128 * m:128 * m + 128, c * 512:c * 512 + 512],
                      _t(st2_sb, par, 512), [("Ss", ev)], sem="Sst%d" % par)
            st2_free[par] = ("Sst%d" % par, sd_)
        spc_free = ("St", p.v("St"))

    pt_free = [("Sp0", 0), ("Sp1", 0)]
    gu_free = ("Sv", 0)
    for hm in range(44):
        dg, cg = wload(wg, hm, 2048, I32, 0)
        du, cu = wload(wu, hm, 2048, I32, 1)
        lg = lu = 0
        for c in range(ncc):
            for k in range(16):
                lg = mm(P_A[:], dg[:, k * 128:k * 128 + 128], xslice(k, c),
                        k == 0, k == 15,
                        [("Sv", cg), bank["A"]] if k == 0 else [])
            for k in range(16):
                lu = mm(P_B[:], du[:, k * 128:k * 128 + 128], xslice(k, c),
                        k == 0, k == 15,
                        [("Sv", cu), bank["B"]] if k == 0 else [])
            ag = act(gt_sb[:], P_A[:], AF.Silu, scale=sg_sb[:, hm:hm + 1],
                     waits=[("St", lg), gu_free])
            au = act(ut_sb[:], P_B[:], AF.Identity, scale=su_sb[:, hm:hm + 1],
                     waits=[("St", lu)])
            bank["A"] = ("Ss", ag)
            bank["B"] = ("Ss", au)
            ppar = c % 2
            pv = vtt(_t(pt_sb, ppar, 512), gt_sb[:], ut_sb[:], AL.mult,
                     [("Ss", au), pt_free[ppar]])
            gu_free = ("Sv", pv)
            pd = dma(p_dr[128 * hm:128 * hm + 128, c * 512:c * 512 + 512],
                     _t(pt_sb, ppar, 512), [("Sv", pv)], sem="Sp%d" % ppar)
            pt_free[ppar] = ("Sp%d" % ppar, pd)
        wbf_free[0] = ("St", lg)
        wbf_free[1] = ("St", lu)

    for m in range(16):
        par = m % 2
        dstb, cv = wload(wd, m, 44 * 128, I32, par)
        wds = dma(wd_dr[:, m * 44 * 128:(m + 1) * 44 * 128], dstb,
                  [("Sv", cv)], sem="Swd%d" % par)
        wbf_free[par] = ("Swd%d" % par, wds)

    p_stored = [("Sp0", p.v("Sp0")), ("Sp1", p.v("Sp1"))]
    wd_stored = [("Swd0", p.v("Swd0")), ("Swd1", p.v("Swd1"))]
    s_stored = [("Sst0", p.v("Sst0")), ("Sst1", p.v("Sst1"))]
    pch_free = ("St", p.v("St"))
    g_done = ("Sv", 0)
    sgl_free = [("Sv", 0), ("Sv", 0)]
    xtl_free = [("Sv", 0), ("Sv", 0)]
    ot_free = [("So0", 0), ("So1", 0)]
    for c3 in range(t2 // 1024):
        ldp = 0
        for hh in range(44):
            first = hh == 0
            ldp = dma(_t(pch, hh, 1024),
                      p_dr[128 * hh:128 * hh + 128,
                           c3 * 1024:c3 * 1024 + 1024],
                      ([pch_free] + (p_stored if c3 == 0 else []))
                      if first else [],
                      sem="Spl", throttle=first)
        for m in range(16):
            par = m % 2
            stg = wst[:, par * 5632:par * 5632 + 2816].bitcast(BF16)
            ldw = dma(stg, wd_dr[:, m * 44 * 128:(m + 1) * 44 * 128],
                      [wst_free[par]] +
                      (wd_stored if (c3 == 0 and m < 2) else []),
                      sem="Sw%d" % par)
            for half in range(2):
                nm, pt_ = banks[(m * 2 + half) % 4]
                last = 0
                for k in range(44):
                    last = mm(pt_, stg[:, k * 128:k * 128 + 128],
                              pch[:, (k * 2 + half) * 512:
                                  (k * 2 + half) * 512 + 512],
                              k == 0, k == 43,
                              [("Sw%d" % par, ldw), ("Spl", ldp), bank[nm]]
                              if k == 0 else [])
                ag = act(g2_sb[:], pt_, AF.Silu, scale=sd2_sb[:, m:m + 1],
                         waits=[("St", last), g_done])
                bank[nm] = ("Ss", ag)
                colh = slice(c3 * 1024 + half * 512,
                             c3 * 1024 + half * 512 + 512)
                if dbg:
                    dma(g_d[128 * m:128 * m + 128, colh], g2_sb[:],
                        [("Ss", ag)], sem="So", throttle=False)
                lds = dma(_t(sgl_sb, half, 512),
                          s_dr[128 * m:128 * m + 128, colh],
                          [sgl_free[half]] +
                          (s_stored if (c3 == 0 and m == 0) else []),
                          sem="Sl%d" % half)
                ldx = dma(_t(xtl_sb, half, 512),
                          xt2[128 * m:128 * m + 128, colh],
                          [xtl_free[half]], sem="Sx%d" % half)
                v1 = vtt(_t(ot_sb, half, 512), g2_sb[:],
                         _t(sgl_sb, half, 512), AL.mult,
                         [("Ss", ag), ("Sl%d" % half, lds),
                          ("Sx%d" % half, ldx), ot_free[half]])
                g_done = ("Sv", v1)
                sgl_free[half] = ("Sv", v1)
                v2 = vtt(_t(ot_sb, half, 512), _t(ot_sb, half, 512),
                         _t(xtl_sb, half, 512), AL.add, [])
                xtl_free[half] = ("Sv", v2)
                od = dma(outt[128 * m:128 * m + 128, colh],
                         _t(ot_sb, half, 512), [("Sv", v2)],
                         sem="So%d" % half)
                ot_free[half] = ("So%d" % half, od)
            wst_free[par] = ("St", p.v("St"))
        pch_free = ("St", p.v("St"))

    p.emit("sync", lambda e: e.nop(),
           [("So0", p.v("So0")), ("So1", p.v("So1")), ("So", p.v("So"))],
           None)
    p.build()
    return nc



def _tile_w(wT):
    K, M = wT.shape
    nk, nm = K // 128, M // 128
    a = wT.reshape(nk, 128, nm, 128).transpose(1, 2, 0, 3).reshape(128, -1)
    return np.ascontiguousarray(a)


def _pp(vec):
    n = vec.shape[0] // 128
    return np.ascontiguousarray(vec.reshape(n, 128).T)


def _mk_e8():
    e = np.zeros((128, 512), np.float32)
    for g4 in range(4):
        for v in range(4):
            for dd in range(8):
                for s in range(16):
                    e[32 * g4 + 8 * v + dd, 128 * v + 16 * dd + s] = 1.0
    return e


def _mk_e16():
    e = np.zeros((128, 256), np.float32)
    for g4 in range(4):
        for v in range(2):
            for dd in range(8):
                for s in range(16):
                    e[32 * g4 + 16 * v + s, 128 * v + 16 * dd + s] = 1.0
    return e


def _mk_smat():
    s = np.zeros((128, 128), np.float32)
    for m in range(4):
        for pp_ in range(128):
            s[pp_, 32 * m + 8 * m + pp_ // 16] = 1.0
    return s


def _mk_alog_ea(alog_slice):
    out = np.full((128, 4096), -1e30, np.float32)
    for j in range(128):
        g4 = (j % 16) // 4
        eidx = 4 * (j // 16) + (j % 4)
        r8 = 8 * (j % 4)
        for dd in range(8):
            out[32 * g4 + r8 + dd,
                128 * eidx + 16 * dd:128 * eidx + 16 * dd + 16] = \
                alog_slice[8 * j + dd, :]
    return out


_CACHE = {}


def kernel(**inp):
    x = np.ascontiguousarray(np.asarray(inp["L0_attn_hidden_states"],
                                        np.float32))
    Wx = np.asarray(inp["L0_attn_Wx"], np.float32)
    bx = np.asarray(inp["L0_attn_bx"], np.float32)
    Wz = np.asarray(inp["L0_attn_Wz"], np.float32)
    bz = np.asarray(inp["L0_attn_bz"], np.float32)
    conv_w = np.asarray(inp["L0_attn_conv_w"], np.float32)
    conv_b = np.asarray(inp["L0_attn_conv_b"], np.float32)
    WB = np.asarray(inp["L0_attn_WB"], np.float32)
    WC = np.asarray(inp["L0_attn_WC"], np.float32)
    Wdtd = np.asarray(inp["L0_attn_Wdt_down"], np.float32)
    Wdt = np.asarray(inp["L0_attn_Wdt"], np.float32)
    bdt = np.asarray(inp["L0_attn_bdt"], np.float32)
    A_log = np.asarray(inp["L0_attn_A_log"], np.float32)
    Dv = np.asarray(inp["L0_attn_D"], np.float32)
    Wout = np.asarray(inp["L0_attn_Wout"], np.float32)
    bout = np.asarray(inp["L0_attn_bout"], np.float32)
    wqg = np.asarray(inp["L0_ffn_wq_gate"], np.int32)
    sg = np.asarray(inp["L0_ffn_s_gate"], np.float32)
    wqu = np.asarray(inp["L0_ffn_wq_up"], np.int32)
    su = np.asarray(inp["L0_ffn_s_up"], np.float32)
    wqd = np.asarray(inp["L0_ffn_wq_down"], np.int32)
    sdn = np.asarray(inp["L0_ffn_s_down"], np.float32)

    if "p1" not in _CACHE:
        _CACHE["p1"] = build_p1()
        _CACHE["p2"] = build_p2()
    nc1, nc2_ = _CACHE["p1"], _CACHE["p2"]

    e8 = _mk_e8()
    e16 = _mk_e16()
    e16b = e16.astype(BFNP)
    smat = _mk_smat().astype(BFNP)
    cst = np.tile(np.array([[-1.0, 1.0, 1e-5, -0.5]], np.float32), (128, 1))
    onesd = np.full((128, 1), 1.0 / 2048.0, BFNP)
    ones1 = np.ones((1, 128), BFNP)

    xts = [np.ascontiguousarray(x[b].T) for b in range(2)]

    in1 = []
    for core in range(8):
        b, q = core // 4, core % 4
        sl1 = slice(DSL * q, DSL * (q + 1))
        sl4 = slice(256 * q, 256 * (q + 1))
        in1.append({
            "xt": xts[b],
            "wx": _tile_w(np.ascontiguousarray(Wx[sl4].T)),
            "wbm": _tile_w(np.ascontiguousarray(WB[sl4].T)),
            "wc": _tile_w(np.ascontiguousarray(WC[sl1].T)),
            "wdtd": _tile_w(np.ascontiguousarray(Wdtd.T)),
            "wdt": _tile_w(np.ascontiguousarray(Wdt[sl1].T)),
            "alog": _mk_alog_ea(A_log[sl1]),
            "e8": e8, "e16": e16, "e16b": e16b, "smat": smat,
            "cw": np.ascontiguousarray(
                conv_w[sl1, 0, :].reshape(8, 128, 4).transpose(1, 0, 2)
                .reshape(128, 32)),
            "cb": _pp(conv_b[sl1]),
            "bx": _pp(bx[sl4]),
            "bdt": _pp(bdt[sl1]),
            "dvec": _pp(Dv[sl1]),
            "cst": cst, "onesd": onesd, "ones1": ones1,
        })
    r1 = run_bass_kernel_spmd(nc1, in1, core_ids=list(range(8)))
    ypl = [np.asarray(r1.results[i]["yp"]) for i in range(8)]

    wz_t = _tile_w(np.ascontiguousarray(Wz.T))
    wo_t = _tile_w(np.ascontiguousarray(Wout.T))
    wg_t = _tile_w(np.ascontiguousarray(wqg.T))
    wu_t = _tile_w(np.ascontiguousarray(wqu.T))
    wd_t = _tile_w(np.ascontiguousarray(wqd.T))
    bz_p = _pp(bz)
    bo_p = _pp(bout)
    sg_p = _pp(sg)
    su_p = _pp(su)
    sd_p = _pp(sdn)

    in2 = []
    for core in range(8):
        b, q = core // 4, core % 4
        ybf = np.concatenate([ypl[4 * b + qq] for qq in range(4)], axis=0)
        in2.append({
            "xt2": np.ascontiguousarray(xts[b][:, T2 * q:T2 * (q + 1)]),
            "yp2": np.ascontiguousarray(ybf[:, T2 * q:T2 * (q + 1)]),
            "wz": wz_t, "wo": wo_t, "wg": wg_t, "wu": wu_t, "wd": wd_t,
            "bz": bz_p, "bo": bo_p, "sg": sg_p, "su": su_p, "sdn": sd_p,
            "onesd": onesd, "ones1": ones1, "cst": cst,
        })
    r2 = run_bass_kernel_spmd(nc2_, in2, core_ids=list(range(8)))

    out = np.empty((2, L, DM), np.float32)
    for core in range(8):
        b, q = core // 4, core % 4
        out[b, T2 * q:T2 * (q + 1), :] = np.asarray(
            r2.results[core]["outt"]).T
    return out


# version 09
# speedup vs baseline: 1.4803x; 1.0126x over previous
import numpy as np
import ml_dtypes
import concourse.bass as bass
import concourse.mybir as mybir
from concourse.bass_utils import run_bass_kernel_spmd

F32 = mybir.dt.float32
F32R = mybir.dt.float32r
BF16 = mybir.dt.bfloat16
I32 = mybir.dt.int32
AL = mybir.AluOpType
AF = mybir.ActivationFunctionType
BFNP = ml_dtypes.bfloat16

DM = 2048
DIN = 4096
DSL = 1024
L = 8192
TC = 512
T2 = 2048
NT = 128
HID = 5632

ENGINES = ("sync", "tensor", "scalar", "vector", "gpsimd")


class Prog:

    def __init__(self, nc):
        self.nc = nc
        self.ops = {e: [] for e in ENGINES}
        self.cnt = {}
        self.sems = {}

    def sem(self, name):
        if name not in self.sems:
            self.sems[name] = self.nc.alloc_semaphore(name)
            self.cnt[name] = 0
        return self.sems[name]

    def emit(self, eng, fn, waits=(), inc=None, incn=1):
        waits = tuple((s, v) for (s, v) in waits if v > 0)
        for s, v in waits:
            assert v <= self.cnt[s], (eng, s, v, self.cnt[s])
        self.ops[eng].append((fn, waits, inc, incn))
        if inc is not None:
            self.cnt[inc] += incn
            return self.cnt[inc]
        return None

    def v(self, name):
        return self.cnt.get(name, 0)

    def build(self):
        sems = self.sems
        with self.nc.Block() as blk:
            for ename in ENGINES:
                ops = self.ops[ename]
                if not ops:
                    continue

                def make(ops=ops):
                    def body(eng):
                        for fn, waits, inc, incn in ops:
                            for s, v in waits:
                                eng.wait_ge(sems[s], v)
                            inst = fn(eng)
                            if inc is not None:
                                inst.then_inc(sems[inc], incn)
                    return body

                getattr(blk, ename)(make())


def _t(h, i, w):
    return h[:, i * w:(i + 1) * w]


class Common:
    def __init__(self, p):
        self.p = p

    def dma(self, dst, src, waits=(), sem="Sd", throttle=True):
        w = list(waits)
        if throttle:
            w.append((sem, self.p.v(sem)))
        return self.p.emit("sync", lambda e, d=dst, s=src: e.dma_start(d, s),
                           w, sem, 16)

    def mm(self, pt, lh, rh, st, sp, waits=(), tp=None, skip=False):
        def f(e, pt=pt, lh=lh, rh=rh, st=st, sp=sp, tp=tp, skip=skip):
            return e.matmul(pt, lh, rh, start=st, stop=sp, tile_position=tp,
                            skip_group_check=skip)
        return self.p.emit("tensor", f, waits, "St")

    def act(self, out, in_, fn, bias=0.0, scale=1.0, waits=()):
        def f(e, out=out, in_=in_, fn=fn, bias=bias, scale=scale):
            return e.activation(out, in_, fn, bias=bias, scale=scale)
        return self.p.emit("scalar", f, waits, "Ss")

    def sccp(self, out, in_, waits=()):
        return self.p.emit("scalar", lambda e, o=out, i=in_: e.copy(o, i),
                           waits, "Ss")

    def vtt(self, out, a, b, op, waits=()):
        return self.p.emit("vector", lambda e, o=out, a=a, b=b, op=op:
                           e.tensor_tensor(o, a, b, op), waits, "Sv")

    def vcp(self, out, in_, waits=()):
        return self.p.emit("vector", lambda e, o=out, i=in_:
                           e.tensor_copy(o, i), waits, "Sv")




def build_p1(l=L, dbg=False, barrier=False):
    nch = l // TC
    nc = bass.Bass("TRN2", target_bir_lowering=False,
                   dynamic_dma_scratch_size=2048)
    p = Prog(nc)
    for s in ("Sd", "St", "Ss", "Sv", "Sg", "So",
              "Si", "Swc", "Sx0", "Sx1", "Sgx0", "Sgx1"):
        p.sem(s)
    C = Common(p)
    dma, mm, act, sccp, vtt, vcp = C.dma, C.mm, C.act, C.sccp, C.vtt, C.vcp

    DT = nc.dram_tensor
    xt = DT("xt", [DM, l], F32, kind="ExternalInput").ap()
    wx = DT("wx", [128, 2 * 16 * 128], F32, kind="ExternalInput").ap()
    wbm = DT("wbm", [128, 2 * 16 * 128], F32, kind="ExternalInput").ap()
    wc = DT("wc", [128, 8 * 16 * 128], F32, kind="ExternalInput").ap()
    wdtd = DT("wdtd", [128, 16 * 128], F32, kind="ExternalInput").ap()
    wdt = DT("wdt", [128, 8 * 128], F32, kind="ExternalInput").ap()
    alog = DT("alog", [128, 4096], F32, kind="ExternalInput").ap()
    e8i = DT("e8", [128, 512], F32R, kind="ExternalInput").ap()
    e16i = DT("e16", [128, 256], F32R, kind="ExternalInput").ap()
    e16bi = DT("e16b", [128, 256], BF16, kind="ExternalInput").ap()
    smati = DT("smat", [128, 128], BF16, kind="ExternalInput").ap()
    cwi = DT("cw", [128, 32], F32, kind="ExternalInput").ap()
    cbi = DT("cb", [128, 8], F32, kind="ExternalInput").ap()
    bxi = DT("bx", [128, 2], F32, kind="ExternalInput").ap()
    bdti = DT("bdt", [128, 8], F32, kind="ExternalInput").ap()
    dvi = DT("dvec", [128, 8], F32, kind="ExternalInput").ap()
    csti = DT("cst", [128, 4], F32, kind="ExternalInput").ap()
    onesdi = DT("onesd", [128, 1], BF16, kind="ExternalInput").ap()
    ones1i = DT("ones1", [1, 128], BF16, kind="ExternalInput").ap()
    yp = DT("yp", [DSL, l], F32, kind="ExternalOutput").ap()
    if dbg:
        xn_d = DT("xn_d", [128, 16 * l], BF16, kind="ExternalOutput").ap()
        dl_d = DT("dl_d", [128, 8 * l], F32, kind="ExternalOutput").ap()
        u_d = DT("u_d", [128, 8 * l], F32, kind="ExternalOutput").ap()
        b_d = DT("b_d", [128, 2 * l], F32, kind="ExternalOutput").ap()
        c_d = DT("c_d", [128, 8 * l], BF16, kind="ExternalOutput").ap()
        h_d = DT("h_d", [128, 4 * l], F32, kind="ExternalOutput").ap()

    A = nc.alloc_sbuf_tensor
    wx_sb = A("wx_sb", [128, 2 * 16 * 128], BF16)
    wb_sb = A("wb_sb", [128, 2 * 16 * 128], BF16)
    wdtd_sb = A("wdtd_sb", [128, 16 * 128], BF16)
    wdt_sb = A("wdt_sb", [128, 8 * 128], F32R)
    wcb_sb = A("wcb_sb", [128, 2 * 2048], BF16)
    ea_sb = A("ea_sb", [128, 4096], F32R)
    e8_sb = A("e8_sb", [128, 512], F32R)
    e16_sb = A("e16_sb", [128, 256], F32R)
    e16b_sb = A("e16b_sb", [128, 256], BF16)
    s_sb = A("s_sb", [128, 128], BF16)
    cw_sb = A("cw_sb", [128, 32], F32)
    cb_sb = A("cb_sb", [128, 8], F32)
    bx_sb = A("bx_sb", [128, 2], F32)
    bdt2_sb = A("bdt2_sb", [128, 8], F32)
    d_sb = A("d_sb", [128, 8], F32)
    cst_sb = A("cst_sb", [128, 4], F32)
    onesd_sb = A("onesd_sb", [128, 1], BF16)
    ones1_sb = A("ones1_sb", [1, 128], BF16)
    x_ld = A("x_ld", [128, 2 * 512], F32)
    xw = A("xw", [128, 16 * 512], BF16)
    xsq = A("xsq", [128, 512], BF16)
    mean_f = A("mean_f", [1, 512], F32)
    msq_f = A("msq_f", [1, 512], F32)
    m2_f = A("m2_f", [1, 512], F32)
    sd_f = A("sd_f", [1, 512], F32)
    rr_f = A("rr_f", [1, 512], F32)
    mean_bf = A("mean_bf", [1, 512], BF16)
    rr_bf = A("rr_bf", [1, 512], BF16)
    mbc_sb = A("mbc_sb", [128, 512], F32)
    rbc_sb = A("rbc_sb", [128, 512], F32)
    dt1_sb = A("dt1_sb", [128, 512], F32R)
    dl_sb = A("dl_sb", [128, 8 * 512], F32R)
    u_sb = A("u_sb", [128, 8 * 512], F32R)
    wcs_sb = A("wcs_sb", [128, 1024], F32)
    w_sb = A("w_sb", [128, 8 * 512], F32R)
    c_sb = A("c_sb", [128, 8 * 512], BF16)
    b_sb = A("b_sb", [128, 2 * 512], F32R)
    xb_sb = A("xb_sb", [128, 2 * 515], F32)
    xbc_sb = A("xbc_sb", [128, 515], F32)
    tmpa_sb = A("tmpa_sb", [128, 512], F32)
    tmpb_sb = A("tmpb_sb", [128, 512], F32)
    acc_sb = A("acc_sb", [128, 2 * 512], F32)
    da_sb = A("da_sb", [128, 2 * 1024], F32)
    dbu_sb = A("dbu_sb", [128, 1024], F32)
    hp_sb = A("hp_sb", [128, 4 * 512], F32)
    st_sb = A("st_sb", [128, 128], F32)
    be_sb = A("be_sb", [128, 2 * 512], F32)
    ce_sb = A("ce_sb", [128, 2 * 512], F32)
    ch_sb = A("ch_sb", [128, 3 * 1024], BF16)
    yps_sb = A("yps_sb", [128, 512], F32)
    fence_sb = A("fence_sb", [128, 16], F32)

    AP_ = nc.alloc_psum_tensor
    P_AD = AP_("P_AD", [128, 2 * 512], F32)
    P_WE = AP_("P_WE", [128, 2 * 512], F32)
    P_CE = AP_("P_CE", [128, 512], F32)
    P_Y = AP_("P_Y", [128, 512], F32)
    P_PR = AP_("P_PR", [128, 512], F32)
    P_XE = AP_("P_XE", [128, 512], F32)

    NEG1 = cst_sb[:, 0:1]
    ONE = cst_sb[:, 1:2]
    EPS1 = cst_sb[0:1, 2:3]
    NEGH = cst_sb[0:1, 3:4]

    for dst, src in ((cw_sb, cwi), (cb_sb, cbi), (bx_sb, bxi),
                     (bdt2_sb, bdti), (d_sb, dvi), (cst_sb, csti),
                     (e8_sb, e8i), (e16_sb, e16i), (e16b_sb, e16bi),
                     (s_sb, smati), (onesd_sb, onesdi), (ones1_sb, ones1i)):
        dma(dst[:], src, sem="Si", throttle=False)
    v0 = p.v("Si")
    bdt2v = p.emit("vector",
                   lambda e: e.tensor_scalar_mul(bdt2_sb[:], bdt2_sb[:], 2.0),
                   [("Si", v0)], "Sv")
    for (src, dst, ncols) in ((wx, wx_sb, 2 * 16), (wbm, wb_sb, 2 * 16),
                              (wdtd, wdtd_sb, 16), (wdt, wdt_sb, 8)):
        total = ncols * 128
        off = 0
        while off < total:
            n = min(1024, total - off)
            stg = wcs_sb[:, 0:n]
            ld = dma(stg, src[:, off:off + n], [("Sv", p.v("Sv"))], sem="Swc")
            p.emit("vector",
                   lambda e, o=dst[:, off:off + n], i=stg: e.tensor_copy(o, i),
                   [("Swc", ld)], "Sv")
            off += n
    for kk in range(8):
        ld = dma(da_sb[:, 0:512], alog[:, kk * 512:(kk + 1) * 512],
                 [("Sv", p.v("Sv")), ("Ss", p.v("Ss"))], sem="Swc")
        a1 = act(da_sb[:, 512:1024], da_sb[:, 0:512], AF.Exp,
                 waits=[("Swc", ld)])
        p.emit("vector",
               lambda e, kk=kk: e.tensor_scalar_mul(
                   _t(ea_sb, kk, 512), da_sb[:, 512:1024], -1.0),
               [("Ss", a1)], "Sv")
    p.emit("vector", lambda e: e.memset(st_sb[:], 0.0), [], "Sv")
    p.emit("vector", lambda e: e.memset(xb_sb[:, 0:3], 0.0), [], "Sv")
    p.emit("vector", lambda e: e.memset(xb_sb[:, 515:518], 0.0), [], "Sv")

    bank = {k: ("Ss", 0) for k in ("AD", "WE", "CE", "Y", "PR", "XE")}
    xw_free = ("St", 0)
    dl_free = [("St", 0)] * 8
    w_free = [("St", 0)] * 8
    c_free = [("St", 0)] * 8
    b_free = [("St", 0)] * 2
    u_freev = [("Sv", 0)] * 8
    xld_cast = [("Sv", 0), ("Sv", 0)]
    yps_free = ("So", 0)
    da_free = [("Sv", 0), ("Sv", 0)]
    we_freed = ("Sv", 0)
    ce_freed = [("Sg", 0), ("Sg", 0)]
    ch_free = [("St", 0), ("St", 0), ("St", 0)]
    be_freed = [("Sv", 0), ("Sv", 0)]
    xbc_free = ("Sv", 0)
    tmp_free = ("Sv", 0)
    acc_free = ("Ss", 0)
    xb_read = [("Sgx0", 0), ("Sgx1", 0)]
    halo_done = ("Ss", 0)
    wcb_free = [("St", 0), ("St", 0)]
    wcs_free = ("Sv", 0)
    slot_free = {0: ("Sg", 0), 1: ("Sg", 0), 2: ("Sg", 0), 3: ("Sg", 0)}
    projb = [("PR", P_PR[:]), ("XE", P_XE[:])]
    pb_i = [0]

    def nb():
        nm, pt = projb[pb_i[0] % 2]
        pb_i[0] += 1
        return nm, pt

    for c in range(nch):
        if barrier and c > 0:
            bvals = [(sn, p.v(sn)) for sn in p.sems]
            for en in ENGINES:
                p.emit(en, lambda e: e.nop(), bvals, None)
        col = slice(c * TC, (c + 1) * TC)
        last_q = 0
        for t in range(16):
            par = t % 2
            sx = "Sx%d" % par
            ld = dma(_t(x_ld, par, 512), xt[128 * t:128 * (t + 1), col],
                     [xld_cast[par]], sem=sx)
            cwv = p.emit("vector",
                         lambda e, t=t, par=par: e.tensor_copy(
                             _t(xw, t, 512), _t(x_ld, par, 512)),
                         [(sx, ld), xw_free] if t == 0 else [(sx, ld)],
                         "Sv")
            xld_cast[par] = ("Sv", cwv)
            sq = p.emit("vector",
                        lambda e, t=t: e.tensor_tensor(
                            xsq[:], _t(xw, t, 512), _t(xw, t, 512), AL.mult),
                        [("St", last_q)], "Sv")
            mm(P_PR[0:1, :], onesd_sb[:], _t(xw, t, 512), t == 0, t == 15,
               [("Sv", cwv)] + ([bank["PR"], ("Si", v0)] if t == 0 else []))
            last_q = mm(P_XE[0:1, :], onesd_sb[:], xsq[:], t == 0, t == 15,
                        [("Sv", sq)] + ([bank["XE"]] if t == 0 else []))
        stats_done = p.v("St")
        c1 = vcp(mean_f[:], P_PR[0:1, :], [("St", stats_done)])
        vcp(mean_bf[:], P_PR[0:1, :])
        c2 = vcp(msq_f[:], P_XE[0:1, :])
        vtt(m2_f[:], mean_f[:], mean_f[:], AL.mult, [])
        v_var = vtt(sd_f[:], msq_f[:], m2_f[:], AL.subtract, [])
        a_ln = act(rr_f[:], sd_f[:], AF.Ln, bias=EPS1,
                   waits=[("Sv", v_var)])
        a_rs = act(sd_f[:], rr_f[:], AF.Exp, scale=NEGH)
        v_rrb = vcp(rr_bf[:], sd_f[:], [("Ss", a_rs)])
        mm(P_PR[:], ones1_sb[:], mean_bf[:], True, True,
           [("Sv", v_rrb), bank["PR"]])
        rb = mm(P_XE[:], ones1_sb[:], rr_bf[:], True, True, [bank["XE"]])
        e_mb = sccp(mbc_sb[:], P_PR[:], [("St", rb)])
        e_rb = sccp(rbc_sb[:], P_XE[:], [])
        bank["PR"] = ("Ss", e_mb)
        bank["XE"] = ("Ss", e_rb)
        lnt = 0
        tfree = [0, 0]
        for t in range(16):
            par = t % 2
            sx = "Sx%d" % par
            ld2 = dma(_t(x_ld, par, 512), xt[128 * t:128 * (t + 1), col],
                      [xld_cast[par]], sem=sx)
            tb = tmpa_sb if par == 0 else tmpb_sb
            w1 = [(sx, ld2)]
            if t == 0:
                w1 += [("Ss", e_rb), tmp_free, ("St", stats_done)]
            elif t == 1:
                w1 += [tmp_free]
            else:
                w1 += [("Sv", tfree[par])]
            sub = p.emit("gpsimd",
                         lambda e, par=par, tb=tb: e.tensor_tensor(
                             tb[:], _t(x_ld, par, 512), mbc_sb[:],
                             AL.subtract),
                         w1, "Sg")
            lnt = vtt(_t(xw, t, 512), tb[:], rbc_sb[:], AL.mult,
                      [("Sg", sub)])
            tfree[par] = lnt
            xld_cast[par] = ("Sg", sub)
        tmp_free = ("Sv", lnt)
        xn_done = p.v("Sv")

        nm, pt = nb()
        lastm = 0
        for k in range(16):
            lastm = mm(pt, wdtd_sb[:, k * 128:k * 128 + 128], _t(xw, k, 512),
                       k == 0, k == 15,
                       [bank[nm], ("Sv", xn_done)] if k == 0 else [])
        e1 = sccp(dt1_sb[:], pt, [("St", lastm)])
        bank[nm] = ("Ss", e1)

        def proj16(w_handle, mt, extra, nk=16):
            nm_, pt_ = nb()
            last_ = 0
            for k in range(nk):
                lh = w_handle[:, (mt * nk + k) * 128:(mt * nk + k) * 128 + 128]
                last_ = mm(pt_, lh, _t(xw, k, 512), k == 0, k == nk - 1,
                           ([bank[nm_]] + extra) if k == 0 else [])
            return nm_, pt_, last_

        def emit_delta(m):
            nonlocal tmp_free
            nm_, pt_ = nb()
            mv = mm(pt_, wdt_sb[:, 128 * m:128 * m + 128], dt1_sb[:],
                    True, True, [bank[nm_], ("Ss", e1), dl_free[m]])
            bias = bdt2_sb[:, m:m + 1]
            act(tmpa_sb[:], pt_, AF.Abs, bias=bias,
                waits=[("St", mv), tmp_free])
            act(tmpb_sb[:], tmpa_sb[:], AF.Exp, scale=NEG1)
            act(tmpa_sb[:], tmpb_sb[:], AF.Ln, bias=ONE)
            a4 = act(tmpb_sb[:], pt_, AF.Relu, bias=bias)
            bank[nm_] = ("Ss", a4)
            dv_ = vtt(_t(dl_sb, m, 512), tmpa_sb[:], tmpb_sb[:], AL.add,
                      [("Ss", a4)])
            tmp_free = ("Sv", dv_)
            return dv_

        def emit_cproj(m):
            nonlocal wcs_free
            wpar = m % 2
            for hh in range(2):
                ldw = dma(wcs_sb[:], wc[:, m * 2048 + 1024 * hh:
                                        m * 2048 + 1024 * hh + 1024],
                          [wcs_free], sem="Swc")
                cv = p.emit("vector",
                            lambda e, wpar=wpar, hh=hh: e.tensor_copy(
                                wcb_sb[:, wpar * 2048 + 1024 * hh:
                                       wpar * 2048 + 1024 * hh + 1024],
                                wcs_sb[:]),
                            [("Swc", ldw)] +
                            ([wcb_free[wpar]] if hh == 0 else []), "Sv")
                wcs_free = ("Sv", cv)
            nm_, pt_ = nb()
            last_ = 0
            for k in range(16):
                last_ = mm(pt_, wcb_sb[:, wpar * 2048 + k * 128:
                                       wpar * 2048 + k * 128 + 128],
                           _t(xw, k, 512), k == 0, k == 15,
                           [bank[nm_], ("Sv", cv), c_free[m]] if k == 0
                           else [])
            wcb_free[wpar] = ("St", last_)
            ev_ = sccp(_t(c_sb, m, 512), pt_, [("St", last_)])
            bank[nm_] = ("Ss", ev_)
            return ev_

        def emit_xb_b(m):
            nm_, pt_, lastm_ = proj16(wx_sb, m, [halo_done] if m == 0 else [])
            ev_ = act(xb_sb[:, 515 * m + 3:515 * m + 515], pt_, AF.Identity,
                      bias=bx_sb[:, m:m + 1],
                      waits=[("St", lastm_)] + xb_read)
            bank[nm_] = ("Ss", ev_)
            nm_, pt_, lastm_ = proj16(wb_sb, m, [b_free[m]])
            ev_ = sccp(_t(b_sb, m, 512), pt_, [("St", lastm_)])
            bank[nm_] = ("Ss", ev_)
            return ev_

        def emit_conv(k):
            nonlocal xbc_free, acc_free
            bt, o = k // 4, (32 * k) % 128
            sgx = "Sgx%d" % (k % 2)
            g_ = 0
            for hb in range(2):
                for rr in range(4):
                    first = hb == 0 and rr == 0
                    g_ = dma(xbc_sb[hb * 64 + rr * 16:hb * 64 + rr * 16 + 16,
                                    :],
                             xb_sb[o + 16 * hb:o + 16 * hb + 16,
                                   515 * bt:515 * bt + 515],
                             [("Ss", p.v("Ss")), xbc_free] if first else [],
                             sem=sgx, throttle=first)
            t0 = p.emit("vector",
                        lambda e, k=k: e.tensor_scalar(
                            _t(acc_sb, 0, 512), xbc_sb[:, 0:512],
                            cw_sb[:, 4 * k:4 * k + 1], None, AL.mult),
                        [(sgx, g_), acc_free], "Sv")
            for tp_ in range(1, 4):
                t0 = p.emit("vector",
                            lambda e, k=k, tp_=tp_: e.scalar_tensor_tensor(
                                _t(acc_sb, tp_ % 2, 512),
                                xbc_sb[:, tp_:tp_ + 512],
                                cw_sb[:, 4 * k + tp_:4 * k + tp_ + 1],
                                _t(acc_sb, (tp_ + 1) % 2, 512),
                                AL.mult, AL.add),
                            [], "Sv")
            xbc_free = ("Sv", t0)
            su = act(_t(u_sb, k, 512), _t(acc_sb, 1, 512), AF.Silu,
                     bias=cb_sb[:, k:k + 1],
                     waits=[("Sv", t0), u_freev[k]])
            acc_free = ("Ss", su)
            wv = vtt(_t(w_sb, k, 512), _t(dl_sb, k, 512), _t(u_sb, k, 512),
                     AL.mult, [("Ss", su), w_freetk[k]])
            return wv

        w_freetk = w_free

        cpst = {}

        def emit_cproj_stage(m, hh):
            nonlocal wcs_free
            wpar = m % 2
            ldw = dma(wcs_sb[:], wc[:, m * 2048 + 1024 * hh:
                                    m * 2048 + 1024 * hh + 1024],
                      [wcs_free], sem="Swc")
            cv = p.emit("vector",
                        lambda e, wpar=wpar, hh=hh: e.tensor_copy(
                            wcb_sb[:, wpar * 2048 + 1024 * hh:
                                   wpar * 2048 + 1024 * hh + 1024],
                            wcs_sb[:]),
                        [("Swc", ldw)] +
                        ([wcb_free[wpar]] if hh == 0 else []), "Sv")
            wcs_free = ("Sv", cv)
            if hh == 0:
                cpst[m] = [wpar, cv, 0]
            else:
                cpst[m][2] = cv

        def emit_cproj_mms(m, ks, ke):
            wpar, cv0, cv1 = cpst[m]
            if ks == 0:
                cpst[(m, "b")] = nb()
            nm_, pt_ = cpst[(m, "b")]
            last_ = 0
            for kk in range(ks, ke):
                w0 = []
                if kk == 0:
                    w0 = [bank[nm_], ("Sv", cv0), c_free[m]]
                elif kk == 8:
                    w0 = [("Sv", cv1)]
                last_ = mm(pt_, wcb_sb[:, wpar * 2048 + kk * 128:
                                       wpar * 2048 + kk * 128 + 128],
                           _t(xw, kk, 512), kk == 0, kk == 15, w0)
            if ke == 16:
                wcb_free[wpar] = ("St", last_)
                ev_ = sccp(_t(c_sb, m, 512), pt_, [("St", last_)])
                bank[nm_] = ("Ss", ev_)
                cev[m] = ev_

        def emit_xbproj(which, m, ks, ke):
            key = (which, m)
            if ks == 0:
                cpst[key] = nb()
            nm_, pt_ = cpst[key]
            wh = wx_sb if which == "x" else wb_sb
            last_ = 0
            for kk in range(ks, ke):
                w0 = []
                if kk == 0:
                    w0 = [bank[nm_]]
                    if which == "x" and m == 0:
                        w0.append(halo_done)
                    if which == "b":
                        w0.append(b_free[m])
                last_ = mm(pt_, wh[:, (m * 16 + kk) * 128:
                                  (m * 16 + kk) * 128 + 128],
                           _t(xw, kk, 512), kk == 0, kk == 15, w0)
            if ke == 16:
                if which == "x":
                    ev_ = act(xb_sb[:, 515 * m + 3:515 * m + 515], pt_,
                              AF.Identity, bias=bx_sb[:, m:m + 1],
                              waits=[("St", last_)] + xb_read)
                else:
                    ev_ = sccp(_t(b_sb, m, 512), pt_, [("St", last_)])
                    bev[m] = ev_
                bank[nm_] = ("Ss", ev_)

        dlv = [0] * 8
        cev = [0] * 8
        wvv = [0] * 8
        bev = [0, 0]
        dlv[0] = emit_delta(0)
        cev[0] = emit_cproj(0)
        bev[0] = emit_xb_b(0)
        dlv[1] = emit_delta(1)
        cev[1] = emit_cproj(1)
        wvv[0] = emit_conv(0)

        def filler(k, pr):
            nonlocal xb_read
            if pr == 0 and k + 2 < 8:
                dlv[k + 2] = emit_delta(k + 2)
            elif pr == 1 and k + 2 < 8:
                emit_cproj_stage(k + 2, 0)
            elif pr == 2 and k + 2 < 8:
                emit_cproj_stage(k + 2, 1)
                emit_cproj_mms(k + 2, 0, 4)
            elif pr == 3 and k + 2 < 8:
                emit_cproj_mms(k + 2, 4, 10)
            elif pr == 4 and k + 2 < 8:
                emit_cproj_mms(k + 2, 10, 16)
            elif pr == 5 and k + 1 < 8:
                wvv[k + 1] = emit_conv(k + 1)
            elif pr == 6:
                if k == 2:
                    emit_xbproj("x", 1, 0, 8)
                elif k == 3:
                    emit_xbproj("b", 1, 0, 8)
                elif k == 7:
                    xbr = [("Sgx0", p.v("Sgx0")), ("Sgx1", p.v("Sgx1"))]
                    sccp(xb_sb[:, 0:3], xb_sb[:, 512:515], xbr)
                    sccp(xb_sb[:, 515:518], xb_sb[:, 1027:1030], [])
                    xb_read = xbr
            elif pr == 7:
                if k == 2:
                    emit_xbproj("x", 1, 8, 16)
                elif k == 3:
                    emit_xbproj("b", 1, 8, 16)

        pend_ch = [None]
        pend_ys = [None]
        pend_evac = [None]

        def emit_evac(kb, ysb):
            nonlocal yps_free
            ev = p.emit("vector",
                        lambda e, kb=kb: e.scalar_tensor_tensor(
                            yps_sb[:], _t(u_sb, kb, 512),
                            d_sb[:, kb:kb + 1], P_Y[:], AL.mult, AL.add),
                        [("St", ysb), yps_free], "Sv")
            bank["Y"] = ("Sv", ev)
            dma(yp[128 * kb:128 * kb + 128, col], yps_sb[:],
                [("Sv", ev)], sem="So")
            yps_free = ("So", p.v("So"))
            u_freev[kb] = ("Sv", ev)

        def drain_ch(fence_w):
            if pend_ch[0] is None:
                return
            Pq, ecq = pend_ch[0]
            pend_ch[0] = None
            c2, c3 = Pq % 2, Pq % 3
            sl0q = 2 * (Pq % 2)
            ch = p.emit("gpsimd",
                        lambda e, c2=c2, c3=c3, sl0q=sl0q: e.tensor_tensor(
                            _t(ch_sb, c3, 1024),
                            hp_sb[:, sl0q * 512:sl0q * 512 + 1024],
                            ce_sb[:, c2 * 512:c2 * 512 + 512]
                            .unsqueeze(1).broadcast_to([128, 2, 512]),
                            AL.mult),
                        [fence_w, ("Ss", ecq), ch_free[c3]], "Sg")
            ce_freed[c2] = ("Sg", ch)
            slot_free[sl0q] = ("Sg", ch)
            slot_free[sl0q + 1] = ("Sg", ch)
            pend_ys[0] = (Pq, ch)

        def drain_ysum():
            if pend_ys[0] is None:
                return
            Pq, ch = pend_ys[0]
            pend_ys[0] = None
            c3 = Pq % 3
            if pend_evac[0] is not None:
                kb, ysb = pend_evac[0]
                pend_evac[0] = None
                emit_evac(kb, ysb)
            ys = 0
            for jj, half in ((2 * Pq, 0), (2 * Pq + 1, 1)):
                cc = (jj % 16) // 4
                tpp = (0, 96) if cc == 3 else None
                ys = mm(P_Y[32 * cc:32 * cc + 32, :],
                        s_sb[:, 32 * (jj % 4):32 * (jj % 4) + 32],
                        ch_sb[:, c3 * 1024 + half * 512:
                              c3 * 1024 + half * 512 + 512],
                        jj % 4 == 0, jj % 4 == 3,
                        [("Sg", ch)] + ([bank["Y"]] if jj % 16 == 0
                                        else []),
                        tp=tpp, skip=True)
            ch_free[c3] = ("St", ys)
            if Pq % 8 == 7:
                pend_evac[0] = (Pq // 8, ys)

        for k in range(8):
            for pr in range(8):
                P = 8 * k + pr
                j0, j1 = 2 * P, 2 * P + 1
                ppar = P % 2
                cpar = P % 2
                base = 32 * ((j0 % 16) // 4)
                kt = k
                be_w = None
                if P % 4 == 0:
                    g = P // 4
                    gp = g % 2
                    bb = 32 * ((g % 8) // 2)
                    mb_ = mm(P_CE[:],
                             e16_sb[bb:bb + 32, 128 * (g % 2):
                                    128 * (g % 2) + 128],
                             b_sb[bb:bb + 32,
                                  (g // 8) * 512:(g // 8) * 512 + 512],
                             True, True,
                             [bank["CE"], ("Ss", bev[g // 8])],
                             tp=(bb, 0))
                    be_w = mb_
                for jj, half in ((j0, 0), (j1, 1)):
                    eidx = 4 * (jj // 16) + (jj % 4)
                    mm(P_AD[:, half * 512:half * 512 + 512],
                       ea_sb[base:base + 32, 128 * eidx:128 * eidx + 128],
                       dl_sb[base:base + 32, kt * 512:kt * 512 + 512],
                       True, True,
                       ([bank["AD"], ("Sv", dlv[k])] if half == 0 else []),
                       tp=(base, 0))
                ma = p.v("St")
                da = act(_t(da_sb, ppar, 1024), P_AD[:], AF.Exp,
                         waits=[("St", ma), da_free[ppar]])
                bank["AD"] = ("Ss", da)
                if be_w is not None:
                    gp = (P // 4) % 2
                    eb = sccp(be_sb[:, gp * 512:gp * 512 + 512], P_CE[:],
                              [("St", be_w), be_freed[gp]])
                    bank["CE"] = ("Ss", eb)
                    be_ready = eb
                for jj, half in ((j0, 0), (j1, 1)):
                    mm(P_WE[:, half * 512:half * 512 + 512],
                       e8_sb[base:base + 32,
                             128 * (jj % 4):128 * (jj % 4) + 128],
                       w_sb[base:base + 32, kt * 512:kt * 512 + 512],
                       True, True,
                       ([bank["WE"], we_freed, ("Sv", wvv[k])]
                        if half == 0 else []),
                       tp=(base, 0))
                mw = p.v("St")
                bc_ = 32 * ((P % 8) // 2)
                mc_ = mm(P_CE[:],
                         e16b_sb[bc_:bc_ + 32, 128 * (P % 2):
                                 128 * (P % 2) + 128],
                         c_sb[bc_:bc_ + 32, kt * 512:kt * 512 + 512],
                         True, True, [bank["CE"], ("Ss", cev[k])],
                         tp=(bc_, 0))
                ec = sccp(ce_sb[:, cpar * 512:cpar * 512 + 512], P_CE[:],
                          [("St", mc_), ce_freed[cpar]])
                bank["CE"] = ("Ss", ec)
                gp = (P // 4) % 2
                db = vtt(dbu_sb[:], P_WE[:],
                         be_sb[:, gp * 512:gp * 512 + 512]
                         .unsqueeze(1).broadcast_to([128, 2, 512]),
                         AL.mult,
                         [("St", mw), ("Ss", be_ready), ("Ss", da)])
                we_freed = ("Sv", db)
                if P % 4 == 3:
                    be_freed[gp] = ("Sv", db)
                if j0 % 4 == 0 and P > 0:
                    p.emit("vector",
                           lambda e, j0=j0: e.tensor_copy(
                               st_sb[:, j0 - 4:j0].unsqueeze(2),
                               hp_sb[:].rearrange("p (s t) -> p s t", s=4)
                               [:, :, 511:512]),
                           [], "Sv")
                drain_ysum()
                drain_ch(("Sv", db))
                sc = 0
                for jj, half in ((j0, 0), (j1, 1)):
                    slot = jj % 4
                    sc = p.emit("vector",
                                lambda e, slot=slot, ppar=ppar, jj=jj,
                                half=half:
                                e.tensor_tensor_scan(
                                    _t(hp_sb, slot, 512),
                                    da_sb[:, ppar * 1024 + half * 512:
                                          ppar * 1024 + half * 512 + 512],
                                    dbu_sb[:, half * 512:half * 512 + 512],
                                    st_sb[:, jj:jj + 1], AL.mult, AL.add),
                                [slot_free[slot]], "Sv")
                da_free[ppar] = ("Sv", sc)
                pend_ch[0] = (P, ec)
                filler(k, pr)
            dl_free[k] = ("St", p.v("St"))
            w_free[k] = ("St", p.v("St"))
            c_free[k] = ("St", p.v("St"))
        flet = p.emit("vector",
                      lambda e: e.tensor_copy(fence_sb[:], da_sb[:, 0:16]),
                      [], "Sv")
        drain_ysum()
        drain_ch(("Sv", flet))
        drain_ysum()
        kb_, ysb_ = pend_evac[0]
        pend_evac[0] = None
        emit_evac(kb_, ysb_)
        p.emit("vector",
               lambda e: e.tensor_copy(
                   st_sb[:, NT - 4:NT].unsqueeze(2),
                   hp_sb[:].rearrange("p (s t) -> p s t", s=4)
                   [:, :, 511:512]),
               [], "Sv")
        halo_done = ("Ss", p.v("Ss"))
        b_free = [("St", p.v("St"))] * 2
        xw_free = ("St", p.v("St"))
        if dbg:
            nchk = l // TC
            for bufs, dst, nt_, cast in ((xw, xn_d, 16, False),
                                         (dl_sb, dl_d, 8, True),
                                         (u_sb, u_d, 8, True),
                                         (b_sb, b_d, 2, True),
                                         (c_sb, c_d, 8, False)):
                for tt in range(nt_):
                    srcap = _t(bufs, tt, 512)
                    if cast:
                        srcap = srcap.bitcast(F32)
                    dma(dst[:, (tt * nchk + c) * 512:
                            (tt * nchk + c) * 512 + 512],
                        srcap, [("Sv", p.v("Sv")), ("Sg", p.v("Sg"))],
                        sem="So", throttle=False)
            for tt in range(4):
                dma(h_d[:, (tt * nchk + c) * 512:(tt * nchk + c) * 512 + 512],
                    _t(hp_sb, tt, 512), [("Sv", p.v("Sv"))], sem="So",
                    throttle=False)
            p.emit("sync", lambda e: e.nop(),
                   [(sn, p.v(sn)) for sn in p.sems if sn != "Sd"],
                   "Sd", 16)
            bar = ("Sd", p.v("Sd"))
            xld_cast = [bar, bar]
            dl_free = [bar] * 8
            w_free = [bar] * 8
            c_free = [bar] * 8
            u_freev = [bar] * 8
            b_free = [bar] * 2
            tmp_free = bar
            for _sl in range(4):
                slot_free[_sl] = bar

    std_o = DT("st_d", [128, 128], F32, kind="ExternalOutput").ap()
    dma(std_o, st_sb[:], [("Sv", p.v("Sv"))], sem="So")
    p.emit("sync", lambda e: e.nop(), [("So", p.v("So"))], None)
    p.build()
    return nc



def build_p2(t2=T2, dbg=False):
    nc = bass.Bass("TRN2", target_bir_lowering=False,
                   dynamic_dma_scratch_size=4096)
    p = Prog(nc)
    for s in ("Sd", "St", "Ss", "Sv", "Sg", "So", "Si",
              "Sx0", "Sx1", "Sw0", "Sw1", "Sy0", "Sy1",
              "Sst0", "Sst1", "Sp0", "Sp1", "Swd0", "Swd1",
              "Spl", "Sl0", "Sl1", "So0", "So1"):
        p.sem(s)
    C = Common(p)
    dma, mm, act, sccp, vtt, vcp = C.dma, C.mm, C.act, C.sccp, C.vtt, C.vcp
    ncc = t2 // 512

    DT = nc.dram_tensor
    xt2 = DT("xt2", [DM, t2], F32, kind="ExternalInput").ap()
    yp2 = DT("yp2", [DIN, t2], F32, kind="ExternalInput").ap()
    wz = DT("wz", [128, 32 * 16 * 128], F32, kind="ExternalInput").ap()
    wo = DT("wo", [128, 16 * 32 * 128], F32, kind="ExternalInput").ap()
    wg = DT("wg", [128, 44 * 16 * 128], I32, kind="ExternalInput").ap()
    wu = DT("wu", [128, 44 * 16 * 128], I32, kind="ExternalInput").ap()
    wd = DT("wd", [128, 16 * 44 * 128], I32, kind="ExternalInput").ap()
    bzi = DT("bz", [128, 32], F32, kind="ExternalInput").ap()
    boi = DT("bo", [128, 16], F32, kind="ExternalInput").ap()
    sgi = DT("sg", [128, 44], F32, kind="ExternalInput").ap()
    sui = DT("su", [128, 44], F32, kind="ExternalInput").ap()
    sdi = DT("sdn", [128, 16], F32, kind="ExternalInput").ap()
    onesdi = DT("onesd", [128, 1], BF16, kind="ExternalInput").ap()
    ones1i = DT("ones1", [1, 128], BF16, kind="ExternalInput").ap()
    csti = DT("cst", [128, 4], F32, kind="ExternalInput").ap()
    outt = DT("outt", [DM, t2], F32, kind="ExternalOutput").ap()
    s_dr = DT("s_dr", [DM, t2], F32,
              kind="ExternalOutput" if dbg else "Internal").ap()
    if dbg:
        g_d = DT("g_d", [DM, t2], F32, kind="ExternalOutput").ap()
        sp_d = DT("sp_d", [DIN, t2], BF16, kind="ExternalOutput").ap()
    p_dr = DT("p_dr", [HID, t2], BF16, kind="Internal").ap()

    A = nc.alloc_sbuf_tensor
    arena = A("arena", [128, 96 * 1024], mybir.dt.uint8)
    ARE = arena[:].bitcast(BF16)
    xn2 = ARE[:, 0:16 * t2]
    spc = ARE[:, 16 * t2:16 * t2 + 32 * 512]
    pch = ARE[:, 0:44 * 1024]
    x_ld = A("x_ld", [128, 2 * 512], F32)
    xsq = A("xsq", [128, 512], BF16)
    mean_f = A("mean_f", [1, 512], F32)
    msq_f = A("msq_f", [1, 512], F32)
    m2_f = A("m2_f", [1, 512], F32)
    sd_f = A("sd_f", [1, 512], F32)
    rr_f = A("rr_f", [1, 512], F32)
    mean_bf = A("mean_bf", [1, 512], BF16)
    rr_bf = A("rr_bf", [1, 512], BF16)
    mbc_sb = A("mbc_sb", [128, 512], F32)
    rbc_sb = A("rbc_sb", [128, 512], F32)
    tmpa_sb = A("tmpa_sb", [128, 512], F32)
    onesd_sb = A("onesd_sb", [128, 1], BF16)
    ones1_sb = A("ones1_sb", [1, 128], BF16)
    cst_sb = A("cst_sb", [128, 4], F32)
    bz_sb = A("bz_sb", [128, 32], F32)
    bo_sb = A("bo_sb", [128, 16], F32)
    sg_sb = A("sg_sb", [128, 44], F32)
    su_sb = A("su_sb", [128, 44], F32)
    sd2_sb = A("sd2_sb", [128, 16], F32)
    wst = A("wst", [128, 2 * 5632], F32)
    wbf = A("wbf", [128, 2 * 5632], BF16)
    ypt = A("ypt", [128, 2 * 512], F32)
    szt = A("szt", [128, 512], F32)
    gt_sb = A("gt_sb", [128, 512], F32)
    ut_sb = A("ut_sb", [128, 512], F32)
    pt_sb = A("pt_sb", [128, 2 * 512], BF16)
    st2_sb = A("st2_sb", [128, 2 * 512], F32)
    g2_sb = A("g2_sb", [128, 512], F32)
    sgl_sb = A("sgl_sb", [128, 2 * 512], F32)
    xtl_sb = A("xtl_sb", [128, 2 * 512], F32)
    ot_sb = A("ot_sb", [128, 2 * 512], F32)

    AP_ = nc.alloc_psum_tensor
    P_A = AP_("P_A", [128, 512], F32)
    P_B = AP_("P_B", [128, 512], F32)
    P_C = AP_("P_C", [128, 512], F32)
    P_D = AP_("P_D", [128, 512], F32)
    banks = [("A", P_A[:]), ("B", P_B[:]), ("C", P_C[:]), ("D", P_D[:])]

    EPS1 = cst_sb[0:1, 2:3]
    NEGH = cst_sb[0:1, 3:4]

    for dst, src in ((bz_sb, bzi), (bo_sb, boi), (sg_sb, sgi), (su_sb, sui),
                     (sd2_sb, sdi), (onesd_sb, onesdi), (ones1_sb, ones1i),
                     (cst_sb, csti)):
        dma(dst[:], src, sem="Si", throttle=False)
    v0 = p.v("Si")

    bank = {k: ("Ss", 0) for k in ("A", "B", "C", "D")}
    xld_cast = [("Sv", 0), ("Sv", 0)]
    wst_free = [("Sv", 0), ("Sv", 0)]
    wbf_free = [("St", 0), ("St", 0)]

    def xslice(t, c):
        off = (t * ncc + c) * 512
        return xn2[:, off:off + 512]

    for c in range(ncc):
        col = slice(c * 512, (c + 1) * 512)
        last_q = 0
        for t in range(16):
            par = t % 2
            sx = "Sx%d" % par
            ld = dma(_t(x_ld, par, 512), xt2[128 * t:128 * (t + 1), col],
                     [xld_cast[par]], sem=sx)
            cwv = p.emit("vector",
                         lambda e, t=t, par=par, c=c: e.tensor_copy(
                             xslice(t, c), _t(x_ld, par, 512)),
                         [(sx, ld)], "Sv")
            xld_cast[par] = ("Sv", cwv)
            sq = p.emit("vector",
                        lambda e, t=t, c=c: e.tensor_tensor(
                            xsq[:], xslice(t, c), xslice(t, c), AL.mult),
                        [("St", last_q)], "Sv")
            mm(P_A[0:1, :], onesd_sb[:], xslice(t, c), t == 0, t == 15,
               [("Sv", cwv)] + ([bank["A"], ("Si", v0)] if t == 0 else []))
            last_q = mm(P_B[0:1, :], onesd_sb[:], xsq[:], t == 0, t == 15,
                        [("Sv", sq)] + ([bank["B"]] if t == 0 else []))
        stats_done = p.v("St")
        c1 = vcp(mean_f[:], P_A[0:1, :], [("St", stats_done)])
        vcp(mean_bf[:], P_A[0:1, :])
        c2 = vcp(msq_f[:], P_B[0:1, :])
        vtt(m2_f[:], mean_f[:], mean_f[:], AL.mult, [])
        v_var = vtt(sd_f[:], msq_f[:], m2_f[:], AL.subtract, [])
        a_ln = act(rr_f[:], sd_f[:], AF.Ln, bias=EPS1,
                   waits=[("Sv", v_var)])
        a_rs = act(sd_f[:], rr_f[:], AF.Exp, scale=NEGH)
        v_rrb = vcp(rr_bf[:], sd_f[:], [("Ss", a_rs)])
        mm(P_A[:], ones1_sb[:], mean_bf[:], True, True,
           [("Sv", v_rrb), bank["A"]])
        rb = mm(P_B[:], ones1_sb[:], rr_bf[:], True, True, [bank["B"]])
        e_mb = sccp(mbc_sb[:], P_A[:], [("St", rb)])
        e_rb = sccp(rbc_sb[:], P_B[:], [])
        bank["A"] = ("Ss", e_mb)
        bank["B"] = ("Ss", e_rb)
        lnt = 0
        for t in range(16):
            par = t % 2
            sx = "Sx%d" % par
            ld2 = dma(_t(x_ld, par, 512), xt2[128 * t:128 * (t + 1), col],
                      [xld_cast[par]], sem=sx)
            w1 = [(sx, ld2), ("Ss", e_rb)]
            if t == 0:
                w1.append(("St", stats_done))
            lnt = vtt(tmpa_sb[:], _t(x_ld, par, 512), mbc_sb[:], AL.subtract,
                      w1)
            lnt = vtt(xslice(t, c), tmpa_sb[:], rbc_sb[:], AL.mult, [])
            xld_cast[par] = ("Sv", lnt)
    xn_done = p.v("Sv")

    def wload(src, mt, width, dt_in, par, waits=()):
        sw = "Sw%d" % par
        if dt_in == F32:
            stg = wst[:, par * 5632:par * 5632 + width]
        else:
            stg = wst[:, par * 5632:par * 5632 + width].bitcast(I32)
        ld = dma(stg, src[:, mt * width:mt * width + width],
                 [wst_free[par]] + list(waits), sem=sw)
        dstb = wbf[:, par * 5632:par * 5632 + width]
        if dt_in == F32:
            cv = p.emit("vector",
                        lambda e, o=dstb, i=stg: e.tensor_copy(o, i),
                        [(sw, ld), wbf_free[par]], "Sv")
        else:
            cv = p.emit("vector",
                        lambda e, o=dstb, i=stg: e.tensor_scalar(
                            o, i, 128.0, None, AL.subtract),
                        [(sw, ld), wbf_free[par]], "Sv")
        wst_free[par] = ("Sv", cv)
        return dstb, cv

    spc_free = ("St", 0)
    ypt_free = [("Sv", 0), ("Sv", 0)]
    szt_free = ("Sv", 0)
    st2_free = [("Sst0", 0), ("Sst1", 0)]
    for c in range(ncc):
        for m in range(32):
            par = m % 2
            dstb, cv = wload(wz, m, 2048, F32, par)
            nm, pt_ = banks[m % 4]
            last = 0
            for k in range(16):
                w0 = []
                if k == 0:
                    w0 = [("Sv", cv), bank[nm]]
                    if c == 0 and m < 4:
                        w0.append(("Sv", xn_done))
                last = mm(pt_, dstb[:, k * 128:k * 128 + 128], xslice(k, c),
                          k == 0, k == 15, w0)
            wbf_free[par] = ("St", last)
            a_sz = act(szt[:], pt_, AF.Silu, bias=bz_sb[:, m:m + 1],
                       waits=[("St", last), szt_free])
            bank[nm] = ("Ss", a_sz)
            ldy = dma(_t(ypt, par, 512),
                      yp2[128 * m:128 * m + 128, c * 512:c * 512 + 512],
                      [ypt_free[par]], sem="Sy%d" % par)
            sv = vtt(_t(spc, m, 512), _t(ypt, par, 512), szt[:], AL.mult,
                     [("Ss", a_sz), ("Sy%d" % par, ldy), spc_free] if m == 0
                     else [("Ss", a_sz), ("Sy%d" % par, ldy)])
            ypt_free[par] = ("Sv", sv)
            szt_free = ("Sv", sv)
        sp_done = p.v("Sv")
        if dbg:
            for mm_ in range(32):
                dma(sp_d[128 * mm_:128 * mm_ + 128, c * 512:c * 512 + 512],
                    _t(spc, mm_, 512), [("Sv", sp_done)], sem="So",
                    throttle=False)
        for m in range(16):
            par = m % 2
            dstb, cv = wload(wo, m, 4096, F32, par)
            nm, pt_ = banks[m % 4]
            last = 0
            for k in range(32):
                last = mm(pt_, dstb[:, k * 128:k * 128 + 128],
                          _t(spc, k, 512), k == 0, k == 31,
                          [("Sv", max(cv, sp_done)), bank[nm]] if k == 0
                          else [])
            wbf_free[par] = ("St", last)
            ev = act(_t(st2_sb, par, 512), pt_, AF.Identity,
                     bias=bo_sb[:, m:m + 1],
                     waits=[("St", last), st2_free[par]])
            bank[nm] = ("Ss", ev)
            sd_ = dma(s_dr[128 * m:128 * m + 128, c * 512:c * 512 + 512],
                      _t(st2_sb, par, 512), [("Ss", ev)], sem="Sst%d" % par)
            st2_free[par] = ("Sst%d" % par, sd_)
        spc_free = ("St", p.v("St"))

    pt_free = [("Sp0", 0), ("Sp1", 0)]
    gu_free = ("Sv", 0)
    for hm in range(44):
        dg, cg = wload(wg, hm, 2048, I32, 0)
        du, cu = wload(wu, hm, 2048, I32, 1)
        lg = lu = 0
        for c in range(ncc):
            for k in range(16):
                lg = mm(P_A[:], dg[:, k * 128:k * 128 + 128], xslice(k, c),
                        k == 0, k == 15,
                        [("Sv", cg), bank["A"]] if k == 0 else [])
            for k in range(16):
                lu = mm(P_B[:], du[:, k * 128:k * 128 + 128], xslice(k, c),
                        k == 0, k == 15,
                        [("Sv", cu), bank["B"]] if k == 0 else [])
            ag = act(gt_sb[:], P_A[:], AF.Silu, scale=sg_sb[:, hm:hm + 1],
                     waits=[("St", lg), gu_free])
            au = act(ut_sb[:], P_B[:], AF.Identity, scale=su_sb[:, hm:hm + 1],
                     waits=[("St", lu)])
            bank["A"] = ("Ss", ag)
            bank["B"] = ("Ss", au)
            ppar = c % 2
            pv = vtt(_t(pt_sb, ppar, 512), gt_sb[:], ut_sb[:], AL.mult,
                     [("Ss", au), pt_free[ppar]])
            gu_free = ("Sv", pv)
            pd = dma(p_dr[128 * hm:128 * hm + 128, c * 512:c * 512 + 512],
                     _t(pt_sb, ppar, 512), [("Sv", pv)], sem="Sp%d" % ppar)
            pt_free[ppar] = ("Sp%d" % ppar, pd)
        wbf_free[0] = ("St", lg)
        wbf_free[1] = ("St", lu)

    p_stored = [("Sp0", p.v("Sp0")), ("Sp1", p.v("Sp1"))]
    s_stored = [("Sst0", p.v("Sst0")), ("Sst1", p.v("Sst1"))]
    pch_free = ("St", p.v("St"))
    g_done = ("Sv", 0)
    sgl_free = [("Sv", 0), ("Sv", 0)]
    xtl_free = [("Sv", 0), ("Sv", 0)]
    ot_free = [("So0", 0), ("So1", 0)]
    for c3 in range(t2 // 1024):
        ldp = 0
        for hh in range(44):
            first = hh == 0
            ldp = dma(_t(pch, hh, 1024),
                      p_dr[128 * hh:128 * hh + 128,
                           c3 * 1024:c3 * 1024 + 1024],
                      ([pch_free] + (p_stored if c3 == 0 else []))
                      if first else [],
                      sem="Spl", throttle=first)
        for m in range(16):
            par = m % 2
            dstb, cv = wload(wd, m, 44 * 128, I32, par)
            for half in range(2):
                nm, pt_ = banks[(m * 2 + half) % 4]
                last = 0
                for k in range(44):
                    last = mm(pt_, dstb[:, k * 128:k * 128 + 128],
                              pch[:, (k * 2 + half) * 512:
                                  (k * 2 + half) * 512 + 512],
                              k == 0, k == 43,
                              [("Sv", cv), ("Spl", ldp), bank[nm]]
                              if k == 0 else [])
                ag = act(g2_sb[:], pt_, AF.Silu, scale=sd2_sb[:, m:m + 1],
                         waits=[("St", last), g_done])
                bank[nm] = ("Ss", ag)
                colh = slice(c3 * 1024 + half * 512,
                             c3 * 1024 + half * 512 + 512)
                if dbg:
                    dma(g_d[128 * m:128 * m + 128, colh], g2_sb[:],
                        [("Ss", ag)], sem="So", throttle=False)
                lds = dma(_t(sgl_sb, half, 512),
                          s_dr[128 * m:128 * m + 128, colh],
                          [sgl_free[half]] +
                          (s_stored if (c3 == 0 and m == 0) else []),
                          sem="Sl%d" % half)
                ldx = dma(_t(xtl_sb, half, 512),
                          xt2[128 * m:128 * m + 128, colh],
                          [xtl_free[half]], sem="Sx%d" % half)
                v1 = vtt(_t(ot_sb, half, 512), g2_sb[:],
                         _t(sgl_sb, half, 512), AL.mult,
                         [("Ss", ag), ("Sl%d" % half, lds),
                          ("Sx%d" % half, ldx), ot_free[half]])
                g_done = ("Sv", v1)
                sgl_free[half] = ("Sv", v1)
                v2 = vtt(_t(ot_sb, half, 512), _t(ot_sb, half, 512),
                         _t(xtl_sb, half, 512), AL.add, [])
                xtl_free[half] = ("Sv", v2)
                od = dma(outt[128 * m:128 * m + 128, colh],
                         _t(ot_sb, half, 512), [("Sv", v2)],
                         sem="So%d" % half)
                ot_free[half] = ("So%d" % half, od)
            wbf_free[par] = ("St", p.v("St"))
        pch_free = ("St", p.v("St"))

    p.emit("sync", lambda e: e.nop(),
           [("So0", p.v("So0")), ("So1", p.v("So1")), ("So", p.v("So"))],
           None)
    p.build()
    return nc



def _tile_w(wT):
    K, M = wT.shape
    nk, nm = K // 128, M // 128
    a = wT.reshape(nk, 128, nm, 128).transpose(1, 2, 0, 3).reshape(128, -1)
    return np.ascontiguousarray(a)


def _pp(vec):
    n = vec.shape[0] // 128
    return np.ascontiguousarray(vec.reshape(n, 128).T)


def _mk_e8():
    e = np.zeros((128, 512), np.float32)
    for g4 in range(4):
        for v in range(4):
            for dd in range(8):
                for s in range(16):
                    e[32 * g4 + 8 * v + dd, 128 * v + 16 * dd + s] = 1.0
    return e


def _mk_e16():
    e = np.zeros((128, 256), np.float32)
    for g4 in range(4):
        for v in range(2):
            for dd in range(8):
                for s in range(16):
                    e[32 * g4 + 16 * v + s, 128 * v + 16 * dd + s] = 1.0
    return e


def _mk_smat():
    s = np.zeros((128, 128), np.float32)
    for m in range(4):
        for pp_ in range(128):
            s[pp_, 32 * m + 8 * m + pp_ // 16] = 1.0
    return s


def _mk_alog_ea(alog_slice):
    out = np.full((128, 4096), -1e30, np.float32)
    for j in range(128):
        g4 = (j % 16) // 4
        eidx = 4 * (j // 16) + (j % 4)
        r8 = 8 * (j % 4)
        for dd in range(8):
            out[32 * g4 + r8 + dd,
                128 * eidx + 16 * dd:128 * eidx + 16 * dd + 16] = \
                alog_slice[8 * j + dd, :]
    return out


_CACHE = {}


def kernel(**inp):
    x = np.ascontiguousarray(np.asarray(inp["L0_attn_hidden_states"],
                                        np.float32))
    Wx = np.asarray(inp["L0_attn_Wx"], np.float32)
    bx = np.asarray(inp["L0_attn_bx"], np.float32)
    Wz = np.asarray(inp["L0_attn_Wz"], np.float32)
    bz = np.asarray(inp["L0_attn_bz"], np.float32)
    conv_w = np.asarray(inp["L0_attn_conv_w"], np.float32)
    conv_b = np.asarray(inp["L0_attn_conv_b"], np.float32)
    WB = np.asarray(inp["L0_attn_WB"], np.float32)
    WC = np.asarray(inp["L0_attn_WC"], np.float32)
    Wdtd = np.asarray(inp["L0_attn_Wdt_down"], np.float32)
    Wdt = np.asarray(inp["L0_attn_Wdt"], np.float32)
    bdt = np.asarray(inp["L0_attn_bdt"], np.float32)
    A_log = np.asarray(inp["L0_attn_A_log"], np.float32)
    Dv = np.asarray(inp["L0_attn_D"], np.float32)
    Wout = np.asarray(inp["L0_attn_Wout"], np.float32)
    bout = np.asarray(inp["L0_attn_bout"], np.float32)
    wqg = np.asarray(inp["L0_ffn_wq_gate"], np.int32)
    sg = np.asarray(inp["L0_ffn_s_gate"], np.float32)
    wqu = np.asarray(inp["L0_ffn_wq_up"], np.int32)
    su = np.asarray(inp["L0_ffn_s_up"], np.float32)
    wqd = np.asarray(inp["L0_ffn_wq_down"], np.int32)
    sdn = np.asarray(inp["L0_ffn_s_down"], np.float32)

    if "p1" not in _CACHE:
        _CACHE["p1"] = build_p1()
        _CACHE["p2"] = build_p2()
    nc1, nc2_ = _CACHE["p1"], _CACHE["p2"]

    e8 = _mk_e8()
    e16 = _mk_e16()
    e16b = e16.astype(BFNP)
    smat = _mk_smat().astype(BFNP)
    cst = np.tile(np.array([[-1.0, 1.0, 1e-5, -0.5]], np.float32), (128, 1))
    onesd = np.full((128, 1), 1.0 / 2048.0, BFNP)
    ones1 = np.ones((1, 128), BFNP)

    xts = [np.ascontiguousarray(x[b].T) for b in range(2)]

    in1 = []
    for core in range(8):
        b, q = core // 4, core % 4
        sl1 = slice(DSL * q, DSL * (q + 1))
        sl4 = slice(256 * q, 256 * (q + 1))
        in1.append({
            "xt": xts[b],
            "wx": _tile_w(np.ascontiguousarray(Wx[sl4].T)),
            "wbm": _tile_w(np.ascontiguousarray(WB[sl4].T)),
            "wc": _tile_w(np.ascontiguousarray(WC[sl1].T)),
            "wdtd": _tile_w(np.ascontiguousarray(Wdtd.T)),
            "wdt": _tile_w(np.ascontiguousarray(Wdt[sl1].T)),
            "alog": _mk_alog_ea(A_log[sl1]),
            "e8": e8, "e16": e16, "e16b": e16b, "smat": smat,
            "cw": np.ascontiguousarray(
                conv_w[sl1, 0, :].reshape(8, 128, 4).transpose(1, 0, 2)
                .reshape(128, 32)),
            "cb": _pp(conv_b[sl1]),
            "bx": _pp(bx[sl4]),
            "bdt": _pp(bdt[sl1]),
            "dvec": _pp(Dv[sl1]),
            "cst": cst, "onesd": onesd, "ones1": ones1,
        })
    r1 = run_bass_kernel_spmd(nc1, in1, core_ids=list(range(8)))
    ypl = [np.asarray(r1.results[i]["yp"]) for i in range(8)]

    wz_t = _tile_w(np.ascontiguousarray(Wz.T))
    wo_t = _tile_w(np.ascontiguousarray(Wout.T))
    wg_t = _tile_w(np.ascontiguousarray(wqg.T))
    wu_t = _tile_w(np.ascontiguousarray(wqu.T))
    wd_t = _tile_w(np.ascontiguousarray(wqd.T))
    bz_p = _pp(bz)
    bo_p = _pp(bout)
    sg_p = _pp(sg)
    su_p = _pp(su)
    sd_p = _pp(sdn)

    in2 = []
    for core in range(8):
        b, q = core // 4, core % 4
        ybf = np.concatenate([ypl[4 * b + qq] for qq in range(4)], axis=0)
        in2.append({
            "xt2": np.ascontiguousarray(xts[b][:, T2 * q:T2 * (q + 1)]),
            "yp2": np.ascontiguousarray(ybf[:, T2 * q:T2 * (q + 1)]),
            "wz": wz_t, "wo": wo_t, "wg": wg_t, "wu": wu_t, "wd": wd_t,
            "bz": bz_p, "bo": bo_p, "sg": sg_p, "su": su_p, "sdn": sd_p,
            "onesd": onesd, "ones1": ones1, "cst": cst,
        })
    r2 = run_bass_kernel_spmd(nc2_, in2, core_ids=list(range(8)))

    out = np.empty((2, L, DM), np.float32)
    for core in range(8):
        b, q = core // 4, core % 4
        out[b, T2 * q:T2 * (q + 1), :] = np.asarray(
            r2.results[core]["outt"]).T
    return out


# version 10 (final)
# speedup vs baseline: 1.6990x; 1.1477x over previous
import numpy as np
import ml_dtypes
import concourse.bass as bass
import concourse.mybir as mybir
from concourse.bass_utils import run_bass_kernel_spmd

F32 = mybir.dt.float32
F32R = mybir.dt.float32r
BF16 = mybir.dt.bfloat16
I32 = mybir.dt.int32
AL = mybir.AluOpType
AF = mybir.ActivationFunctionType
BFNP = ml_dtypes.bfloat16

DM = 2048
DIN = 4096
DSL = 1024
L = 8192
TC = 512
T2 = 2048
NT = 128
HID = 5632

ENGINES = ("sync", "tensor", "scalar", "vector", "gpsimd")


class Prog:

    def __init__(self, nc):
        self.nc = nc
        self.ops = {e: [] for e in ENGINES}
        self.cnt = {}
        self.sems = {}

    def sem(self, name):
        if name not in self.sems:
            self.sems[name] = self.nc.alloc_semaphore(name)
            self.cnt[name] = 0
        return self.sems[name]

    def emit(self, eng, fn, waits=(), inc=None, incn=1):
        waits = tuple((s, v) for (s, v) in waits if v > 0)
        for s, v in waits:
            assert v <= self.cnt[s], (eng, s, v, self.cnt[s])
        self.ops[eng].append((fn, waits, inc, incn))
        if inc is not None:
            self.cnt[inc] += incn
            return self.cnt[inc]
        return None

    def v(self, name):
        return self.cnt.get(name, 0)

    def build(self):
        sems = self.sems
        with self.nc.Block() as blk:
            for ename in ENGINES:
                ops = self.ops[ename]
                if not ops:
                    continue

                def make(ops=ops):
                    def body(eng):
                        for fn, waits, inc, incn in ops:
                            for s, v in waits:
                                eng.wait_ge(sems[s], v)
                            inst = fn(eng)
                            if inc is not None:
                                inst.then_inc(sems[inc], incn)
                    return body

                getattr(blk, ename)(make())


def _t(h, i, w):
    return h[:, i * w:(i + 1) * w]


class Common:
    def __init__(self, p):
        self.p = p

    def dma(self, dst, src, waits=(), sem="Sd", throttle=True):
        w = list(waits)
        if throttle:
            w.append((sem, self.p.v(sem)))
        return self.p.emit("sync", lambda e, d=dst, s=src: e.dma_start(d, s),
                           w, sem, 16)

    def mm(self, pt, lh, rh, st, sp, waits=(), tp=None, skip=False):
        def f(e, pt=pt, lh=lh, rh=rh, st=st, sp=sp, tp=tp, skip=skip):
            return e.matmul(pt, lh, rh, start=st, stop=sp, tile_position=tp,
                            skip_group_check=skip)
        return self.p.emit("tensor", f, waits, "St")

    def act(self, out, in_, fn, bias=0.0, scale=1.0, waits=()):
        def f(e, out=out, in_=in_, fn=fn, bias=bias, scale=scale):
            return e.activation(out, in_, fn, bias=bias, scale=scale)
        return self.p.emit("scalar", f, waits, "Ss")

    def sccp(self, out, in_, waits=()):
        return self.p.emit("scalar", lambda e, o=out, i=in_: e.copy(o, i),
                           waits, "Ss")

    def vtt(self, out, a, b, op, waits=()):
        return self.p.emit("vector", lambda e, o=out, a=a, b=b, op=op:
                           e.tensor_tensor(o, a, b, op), waits, "Sv")

    def vcp(self, out, in_, waits=()):
        return self.p.emit("vector", lambda e, o=out, i=in_:
                           e.tensor_copy(o, i), waits, "Sv")




def build_p1(l=L, dbg=False, barrier=False):
    nch = l // TC
    nc = bass.Bass("TRN2", target_bir_lowering=False,
                   dynamic_dma_scratch_size=2048)
    p = Prog(nc)
    for s in ("Sd", "St", "Ss", "Sv", "Sg", "So",
              "Si", "Swc", "Sx0", "Sx1", "Sgx0", "Sgx1"):
        p.sem(s)
    C = Common(p)
    dma, mm, act, sccp, vtt, vcp = C.dma, C.mm, C.act, C.sccp, C.vtt, C.vcp

    DT = nc.dram_tensor
    xt = DT("xt", [DM, l], F32, kind="ExternalInput").ap()
    wx = DT("wx", [128, 2 * 16 * 128], F32, kind="ExternalInput").ap()
    wbm = DT("wbm", [128, 2 * 16 * 128], F32, kind="ExternalInput").ap()
    wc = DT("wc", [128, 8 * 16 * 128], F32, kind="ExternalInput").ap()
    wdtd = DT("wdtd", [128, 16 * 128], F32, kind="ExternalInput").ap()
    wdt = DT("wdt", [128, 8 * 128], F32, kind="ExternalInput").ap()
    alog = DT("alog", [128, 4096], F32, kind="ExternalInput").ap()
    e8i = DT("e8", [128, 512], F32R, kind="ExternalInput").ap()
    e16i = DT("e16", [128, 256], F32R, kind="ExternalInput").ap()
    e16bi = DT("e16b", [128, 256], BF16, kind="ExternalInput").ap()
    smati = DT("smat", [128, 128], BF16, kind="ExternalInput").ap()
    cwi = DT("cw", [128, 32], F32, kind="ExternalInput").ap()
    cbi = DT("cb", [128, 8], F32, kind="ExternalInput").ap()
    bxi = DT("bx", [128, 2], F32, kind="ExternalInput").ap()
    bdti = DT("bdt", [128, 8], F32, kind="ExternalInput").ap()
    dvi = DT("dvec", [128, 8], F32, kind="ExternalInput").ap()
    csti = DT("cst", [128, 4], F32, kind="ExternalInput").ap()
    onesdi = DT("onesd", [128, 1], BF16, kind="ExternalInput").ap()
    ones1i = DT("ones1", [1, 128], BF16, kind="ExternalInput").ap()
    yp = DT("yp", [DSL, l], F32, kind="ExternalOutput").ap()
    if dbg:
        xn_d = DT("xn_d", [128, 16 * l], BF16, kind="ExternalOutput").ap()
        dl_d = DT("dl_d", [128, 8 * l], F32, kind="ExternalOutput").ap()
        u_d = DT("u_d", [128, 8 * l], F32, kind="ExternalOutput").ap()
        b_d = DT("b_d", [128, 2 * l], F32, kind="ExternalOutput").ap()
        c_d = DT("c_d", [128, 8 * l], BF16, kind="ExternalOutput").ap()
        h_d = DT("h_d", [128, 4 * l], F32, kind="ExternalOutput").ap()

    A = nc.alloc_sbuf_tensor
    wx_sb = A("wx_sb", [128, 2 * 16 * 128], BF16)
    wb_sb = A("wb_sb", [128, 2 * 16 * 128], BF16)
    wdtd_sb = A("wdtd_sb", [128, 16 * 128], BF16)
    wdt_sb = A("wdt_sb", [128, 8 * 128], F32R)
    wcb_sb = A("wcb_sb", [128, 2 * 2048], BF16)
    ea_sb = A("ea_sb", [128, 4096], F32R)
    e8_sb = A("e8_sb", [128, 512], F32R)
    e16_sb = A("e16_sb", [128, 256], F32R)
    e16b_sb = A("e16b_sb", [128, 256], BF16)
    s_sb = A("s_sb", [128, 128], BF16)
    cw_sb = A("cw_sb", [128, 32], F32)
    cb_sb = A("cb_sb", [128, 8], F32)
    bx_sb = A("bx_sb", [128, 2], F32)
    bdt2_sb = A("bdt2_sb", [128, 8], F32)
    d_sb = A("d_sb", [128, 8], F32)
    cst_sb = A("cst_sb", [128, 4], F32)
    onesd_sb = A("onesd_sb", [128, 1], BF16)
    ones1_sb = A("ones1_sb", [1, 128], BF16)
    x_ld = A("x_ld", [128, 2 * 512], F32)
    xw = A("xw", [128, 16 * 512], BF16)
    xsq = A("xsq", [128, 512], BF16)
    mean_f = A("mean_f", [1, 512], F32)
    msq_f = A("msq_f", [1, 512], F32)
    m2_f = A("m2_f", [1, 512], F32)
    sd_f = A("sd_f", [1, 512], F32)
    rr_f = A("rr_f", [1, 512], F32)
    mean_bf = A("mean_bf", [1, 512], BF16)
    rr_bf = A("rr_bf", [1, 512], BF16)
    mbc_sb = A("mbc_sb", [128, 512], F32)
    rbc_sb = A("rbc_sb", [128, 512], F32)
    dt1_sb = A("dt1_sb", [128, 512], F32R)
    dl_sb = A("dl_sb", [128, 8 * 512], F32R)
    u_sb = A("u_sb", [128, 8 * 512], F32R)
    wcs_sb = A("wcs_sb", [128, 1024], F32)
    w_sb = A("w_sb", [128, 8 * 512], F32R)
    c_sb = A("c_sb", [128, 8 * 512], BF16)
    b_sb = A("b_sb", [128, 2 * 512], F32R)
    xb_sb = A("xb_sb", [128, 2 * 515], F32)
    xbc_sb = A("xbc_sb", [128, 515], F32)
    tmpa_sb = A("tmpa_sb", [128, 512], F32)
    tmpb_sb = A("tmpb_sb", [128, 512], F32)
    acc_sb = A("acc_sb", [128, 2 * 512], F32)
    da_sb = A("da_sb", [128, 2 * 1024], F32)
    dbu_sb = A("dbu_sb", [128, 1024], F32)
    hp_sb = A("hp_sb", [128, 4 * 512], F32)
    st_sb = A("st_sb", [128, 128], F32)
    be_sb = A("be_sb", [128, 2 * 512], F32)
    ce_sb = A("ce_sb", [128, 2 * 512], F32)
    ch_sb = A("ch_sb", [128, 3 * 1024], BF16)
    yps_sb = A("yps_sb", [128, 512], F32)
    fence_sb = A("fence_sb", [128, 16], F32)

    AP_ = nc.alloc_psum_tensor
    P_AD = AP_("P_AD", [128, 2 * 512], F32)
    P_WE = AP_("P_WE", [128, 2 * 512], F32)
    P_CE = AP_("P_CE", [128, 512], F32)
    P_Y = AP_("P_Y", [128, 512], F32)
    P_PR = AP_("P_PR", [128, 512], F32)
    P_XE = AP_("P_XE", [128, 512], F32)

    NEG1 = cst_sb[:, 0:1]
    ONE = cst_sb[:, 1:2]
    EPS1 = cst_sb[0:1, 2:3]
    NEGH = cst_sb[0:1, 3:4]

    for dst, src in ((cw_sb, cwi), (cb_sb, cbi), (bx_sb, bxi),
                     (bdt2_sb, bdti), (d_sb, dvi), (cst_sb, csti),
                     (e8_sb, e8i), (e16_sb, e16i), (e16b_sb, e16bi),
                     (s_sb, smati), (onesd_sb, onesdi), (ones1_sb, ones1i)):
        dma(dst[:], src, sem="Si", throttle=False)
    v0 = p.v("Si")
    bdt2v = p.emit("vector",
                   lambda e: e.tensor_scalar_mul(bdt2_sb[:], bdt2_sb[:], 2.0),
                   [("Si", v0)], "Sv")
    for (src, dst, ncols) in ((wx, wx_sb, 2 * 16), (wbm, wb_sb, 2 * 16),
                              (wdtd, wdtd_sb, 16), (wdt, wdt_sb, 8)):
        total = ncols * 128
        off = 0
        while off < total:
            n = min(1024, total - off)
            stg = wcs_sb[:, 0:n]
            ld = dma(stg, src[:, off:off + n], [("Sv", p.v("Sv"))], sem="Swc")
            p.emit("vector",
                   lambda e, o=dst[:, off:off + n], i=stg: e.tensor_copy(o, i),
                   [("Swc", ld)], "Sv")
            off += n
    for kk in range(8):
        ld = dma(da_sb[:, 0:512], alog[:, kk * 512:(kk + 1) * 512],
                 [("Sv", p.v("Sv")), ("Ss", p.v("Ss"))], sem="Swc")
        a1 = act(da_sb[:, 512:1024], da_sb[:, 0:512], AF.Exp,
                 waits=[("Swc", ld)])
        p.emit("vector",
               lambda e, kk=kk: e.tensor_scalar_mul(
                   _t(ea_sb, kk, 512), da_sb[:, 512:1024], -1.0),
               [("Ss", a1)], "Sv")
    p.emit("vector", lambda e: e.memset(st_sb[:], 0.0), [], "Sv")
    p.emit("vector", lambda e: e.memset(xb_sb[:, 0:3], 0.0), [], "Sv")
    p.emit("vector", lambda e: e.memset(xb_sb[:, 515:518], 0.0), [], "Sv")

    bank = {k: ("Ss", 0) for k in ("AD", "WE", "CE", "Y", "PR", "XE")}
    xw_free = ("St", 0)
    dl_free = [("St", 0)] * 8
    w_free = [("St", 0)] * 8
    c_free = [("St", 0)] * 8
    b_free = [("St", 0)] * 2
    u_freev = [("Sv", 0)] * 8
    xld_cast = [("Sv", 0), ("Sv", 0)]
    yps_free = ("So", 0)
    da_free = [("Sv", 0), ("Sv", 0)]
    we_freed = ("Sv", 0)
    ce_freed = [("Sg", 0), ("Sg", 0)]
    ch_free = [("St", 0), ("St", 0), ("St", 0)]
    be_freed = [("Sv", 0), ("Sv", 0)]
    xbc_free = ("Sv", 0)
    tmp_free = ("Sv", 0)
    acc_free = ("Ss", 0)
    xb_read = [("Sgx0", 0), ("Sgx1", 0)]
    halo_done = ("Ss", 0)
    wcb_free = [("St", 0), ("St", 0)]
    wcs_free = ("Sv", 0)
    slot_free = {0: ("Sg", 0), 1: ("Sg", 0), 2: ("Sg", 0), 3: ("Sg", 0)}
    projb = [("PR", P_PR[:]), ("XE", P_XE[:])]
    pb_i = [0]

    def nb():
        nm, pt = projb[pb_i[0] % 2]
        pb_i[0] += 1
        return nm, pt

    for c in range(nch):
        if barrier and c > 0:
            bvals = [(sn, p.v(sn)) for sn in p.sems]
            for en in ENGINES:
                p.emit(en, lambda e: e.nop(), bvals, None)
        col = slice(c * TC, (c + 1) * TC)
        last_q = 0
        for t in range(16):
            par = t % 2
            sx = "Sx%d" % par
            ld = dma(_t(x_ld, par, 512), xt[128 * t:128 * (t + 1), col],
                     [xld_cast[par]], sem=sx)
            cwv = p.emit("vector",
                         lambda e, t=t, par=par: e.tensor_copy(
                             _t(xw, t, 512), _t(x_ld, par, 512)),
                         [(sx, ld), xw_free] if t == 0 else [(sx, ld)],
                         "Sv")
            xld_cast[par] = ("Sv", cwv)
            sq = p.emit("vector",
                        lambda e, t=t: e.tensor_tensor(
                            xsq[:], _t(xw, t, 512), _t(xw, t, 512), AL.mult),
                        [("St", last_q)], "Sv")
            mm(P_PR[0:1, :], onesd_sb[:], _t(xw, t, 512), t == 0, t == 15,
               [("Sv", cwv)] + ([bank["PR"], ("Si", v0)] if t == 0 else []))
            last_q = mm(P_XE[0:1, :], onesd_sb[:], xsq[:], t == 0, t == 15,
                        [("Sv", sq)] + ([bank["XE"]] if t == 0 else []))
        stats_done = p.v("St")
        c1 = vcp(mean_f[:], P_PR[0:1, :], [("St", stats_done)])
        vcp(mean_bf[:], P_PR[0:1, :])
        c2 = vcp(msq_f[:], P_XE[0:1, :])
        vtt(m2_f[:], mean_f[:], mean_f[:], AL.mult, [])
        v_var = vtt(sd_f[:], msq_f[:], m2_f[:], AL.subtract, [])
        a_ln = act(rr_f[:], sd_f[:], AF.Ln, bias=EPS1,
                   waits=[("Sv", v_var)])
        a_rs = act(sd_f[:], rr_f[:], AF.Exp, scale=NEGH)
        v_rrb = vcp(rr_bf[:], sd_f[:], [("Ss", a_rs)])
        mm(P_PR[:], ones1_sb[:], mean_bf[:], True, True,
           [("Sv", v_rrb), bank["PR"]])
        rb = mm(P_XE[:], ones1_sb[:], rr_bf[:], True, True, [bank["XE"]])
        e_mb = sccp(mbc_sb[:], P_PR[:], [("St", rb)])
        e_rb = sccp(rbc_sb[:], P_XE[:], [])
        bank["PR"] = ("Ss", e_mb)
        bank["XE"] = ("Ss", e_rb)
        lnt = 0
        tfree = [0, 0]
        for t in range(16):
            par = t % 2
            sx = "Sx%d" % par
            ld2 = dma(_t(x_ld, par, 512), xt[128 * t:128 * (t + 1), col],
                      [xld_cast[par]], sem=sx)
            tb = tmpa_sb if par == 0 else tmpb_sb
            w1 = [(sx, ld2)]
            if t == 0:
                w1 += [("Ss", e_rb), tmp_free, ("St", stats_done)]
            elif t == 1:
                w1 += [tmp_free]
            else:
                w1 += [("Sv", tfree[par])]
            sub = p.emit("gpsimd",
                         lambda e, par=par, tb=tb: e.tensor_tensor(
                             tb[:], _t(x_ld, par, 512), mbc_sb[:],
                             AL.subtract),
                         w1, "Sg")
            lnt = vtt(_t(xw, t, 512), tb[:], rbc_sb[:], AL.mult,
                      [("Sg", sub)])
            tfree[par] = lnt
            xld_cast[par] = ("Sg", sub)
        tmp_free = ("Sv", lnt)
        xn_done = p.v("Sv")

        nm, pt = nb()
        lastm = 0
        for k in range(16):
            lastm = mm(pt, wdtd_sb[:, k * 128:k * 128 + 128], _t(xw, k, 512),
                       k == 0, k == 15,
                       [bank[nm], ("Sv", xn_done)] if k == 0 else [])
        e1 = sccp(dt1_sb[:], pt, [("St", lastm)])
        bank[nm] = ("Ss", e1)

        def proj16(w_handle, mt, extra, nk=16):
            nm_, pt_ = nb()
            last_ = 0
            for k in range(nk):
                lh = w_handle[:, (mt * nk + k) * 128:(mt * nk + k) * 128 + 128]
                last_ = mm(pt_, lh, _t(xw, k, 512), k == 0, k == nk - 1,
                           ([bank[nm_]] + extra) if k == 0 else [])
            return nm_, pt_, last_

        def emit_delta(m):
            nonlocal tmp_free
            nm_, pt_ = nb()
            mv = mm(pt_, wdt_sb[:, 128 * m:128 * m + 128], dt1_sb[:],
                    True, True, [bank[nm_], ("Ss", e1), dl_free[m]])
            bias = bdt2_sb[:, m:m + 1]
            act(tmpa_sb[:], pt_, AF.Abs, bias=bias,
                waits=[("St", mv), tmp_free])
            act(tmpb_sb[:], tmpa_sb[:], AF.Exp, scale=NEG1)
            act(tmpa_sb[:], tmpb_sb[:], AF.Ln, bias=ONE)
            a4 = act(tmpb_sb[:], pt_, AF.Relu, bias=bias)
            bank[nm_] = ("Ss", a4)
            dv_ = vtt(_t(dl_sb, m, 512), tmpa_sb[:], tmpb_sb[:], AL.add,
                      [("Ss", a4)])
            tmp_free = ("Sv", dv_)
            return dv_

        def emit_cproj(m):
            nonlocal wcs_free
            wpar = m % 2
            for hh in range(2):
                ldw = dma(wcs_sb[:], wc[:, m * 2048 + 1024 * hh:
                                        m * 2048 + 1024 * hh + 1024],
                          [wcs_free], sem="Swc")
                cv = p.emit("vector",
                            lambda e, wpar=wpar, hh=hh: e.tensor_copy(
                                wcb_sb[:, wpar * 2048 + 1024 * hh:
                                       wpar * 2048 + 1024 * hh + 1024],
                                wcs_sb[:]),
                            [("Swc", ldw)] +
                            ([wcb_free[wpar]] if hh == 0 else []), "Sv")
                wcs_free = ("Sv", cv)
            nm_, pt_ = nb()
            last_ = 0
            for k in range(16):
                last_ = mm(pt_, wcb_sb[:, wpar * 2048 + k * 128:
                                       wpar * 2048 + k * 128 + 128],
                           _t(xw, k, 512), k == 0, k == 15,
                           [bank[nm_], ("Sv", cv), c_free[m]] if k == 0
                           else [])
            wcb_free[wpar] = ("St", last_)
            ev_ = sccp(_t(c_sb, m, 512), pt_, [("St", last_)])
            bank[nm_] = ("Ss", ev_)
            return ev_

        def emit_xb_b(m):
            nm_, pt_, lastm_ = proj16(wx_sb, m, [halo_done] if m == 0 else [])
            ev_ = act(xb_sb[:, 515 * m + 3:515 * m + 515], pt_, AF.Identity,
                      bias=bx_sb[:, m:m + 1],
                      waits=[("St", lastm_)] + xb_read)
            bank[nm_] = ("Ss", ev_)
            nm_, pt_, lastm_ = proj16(wb_sb, m, [b_free[m]])
            ev_ = sccp(_t(b_sb, m, 512), pt_, [("St", lastm_)])
            bank[nm_] = ("Ss", ev_)
            return ev_

        def emit_conv(k):
            nonlocal xbc_free, acc_free
            bt, o = k // 4, (32 * k) % 128
            sgx = "Sgx%d" % (k % 2)
            g_ = 0
            for hb in range(2):
                for rr in range(4):
                    first = hb == 0 and rr == 0
                    g_ = dma(xbc_sb[hb * 64 + rr * 16:hb * 64 + rr * 16 + 16,
                                    :],
                             xb_sb[o + 16 * hb:o + 16 * hb + 16,
                                   515 * bt:515 * bt + 515],
                             [("Ss", p.v("Ss")), xbc_free] if first else [],
                             sem=sgx, throttle=first)
            t0 = p.emit("vector",
                        lambda e, k=k: e.tensor_scalar(
                            _t(acc_sb, 0, 512), xbc_sb[:, 0:512],
                            cw_sb[:, 4 * k:4 * k + 1], None, AL.mult),
                        [(sgx, g_), acc_free], "Sv")
            for tp_ in range(1, 4):
                t0 = p.emit("vector",
                            lambda e, k=k, tp_=tp_: e.scalar_tensor_tensor(
                                _t(acc_sb, tp_ % 2, 512),
                                xbc_sb[:, tp_:tp_ + 512],
                                cw_sb[:, 4 * k + tp_:4 * k + tp_ + 1],
                                _t(acc_sb, (tp_ + 1) % 2, 512),
                                AL.mult, AL.add),
                            [], "Sv")
            xbc_free = ("Sv", t0)
            su = act(_t(u_sb, k, 512), _t(acc_sb, 1, 512), AF.Silu,
                     bias=cb_sb[:, k:k + 1],
                     waits=[("Sv", t0), u_freev[k]])
            acc_free = ("Ss", su)
            wv = vtt(_t(w_sb, k, 512), _t(dl_sb, k, 512), _t(u_sb, k, 512),
                     AL.mult, [("Ss", su), w_freetk[k]])
            return wv

        w_freetk = w_free

        cpst = {}

        def emit_cproj_stage(m, hh):
            nonlocal wcs_free
            wpar = m % 2
            ldw = dma(wcs_sb[:], wc[:, m * 2048 + 1024 * hh:
                                    m * 2048 + 1024 * hh + 1024],
                      [wcs_free], sem="Swc")
            cv = p.emit("vector",
                        lambda e, wpar=wpar, hh=hh: e.tensor_copy(
                            wcb_sb[:, wpar * 2048 + 1024 * hh:
                                   wpar * 2048 + 1024 * hh + 1024],
                            wcs_sb[:]),
                        [("Swc", ldw)] +
                        ([wcb_free[wpar]] if hh == 0 else []), "Sv")
            wcs_free = ("Sv", cv)
            if hh == 0:
                cpst[m] = [wpar, cv, 0]
            else:
                cpst[m][2] = cv

        def emit_cproj_mms(m, ks, ke):
            wpar, cv0, cv1 = cpst[m]
            if ks == 0:
                cpst[(m, "b")] = nb()
            nm_, pt_ = cpst[(m, "b")]
            last_ = 0
            for kk in range(ks, ke):
                w0 = []
                if kk == 0:
                    w0 = [bank[nm_], ("Sv", cv0), c_free[m]]
                elif kk == 8:
                    w0 = [("Sv", cv1)]
                last_ = mm(pt_, wcb_sb[:, wpar * 2048 + kk * 128:
                                       wpar * 2048 + kk * 128 + 128],
                           _t(xw, kk, 512), kk == 0, kk == 15, w0)
            if ke == 16:
                wcb_free[wpar] = ("St", last_)
                ev_ = sccp(_t(c_sb, m, 512), pt_, [("St", last_)])
                bank[nm_] = ("Ss", ev_)
                cev[m] = ev_

        def emit_xbproj(which, m, ks, ke):
            key = (which, m)
            if ks == 0:
                cpst[key] = nb()
            nm_, pt_ = cpst[key]
            wh = wx_sb if which == "x" else wb_sb
            last_ = 0
            for kk in range(ks, ke):
                w0 = []
                if kk == 0:
                    w0 = [bank[nm_]]
                    if which == "x" and m == 0:
                        w0.append(halo_done)
                    if which == "b":
                        w0.append(b_free[m])
                last_ = mm(pt_, wh[:, (m * 16 + kk) * 128:
                                  (m * 16 + kk) * 128 + 128],
                           _t(xw, kk, 512), kk == 0, kk == 15, w0)
            if ke == 16:
                if which == "x":
                    ev_ = act(xb_sb[:, 515 * m + 3:515 * m + 515], pt_,
                              AF.Identity, bias=bx_sb[:, m:m + 1],
                              waits=[("St", last_)] + xb_read)
                else:
                    ev_ = sccp(_t(b_sb, m, 512), pt_, [("St", last_)])
                    bev[m] = ev_
                bank[nm_] = ("Ss", ev_)

        dlv = [0] * 8
        cev = [0] * 8
        wvv = [0] * 8
        bev = [0, 0]
        dlv[0] = emit_delta(0)
        cev[0] = emit_cproj(0)
        bev[0] = emit_xb_b(0)
        dlv[1] = emit_delta(1)
        cev[1] = emit_cproj(1)
        wvv[0] = emit_conv(0)

        def filler(k, pr):
            nonlocal xb_read
            if pr == 0 and k + 2 < 8:
                dlv[k + 2] = emit_delta(k + 2)
            elif pr == 1 and k + 2 < 8:
                emit_cproj_stage(k + 2, 0)
            elif pr == 2 and k + 2 < 8:
                emit_cproj_stage(k + 2, 1)
                emit_cproj_mms(k + 2, 0, 4)
            elif pr == 3 and k + 2 < 8:
                emit_cproj_mms(k + 2, 4, 10)
            elif pr == 4 and k + 2 < 8:
                emit_cproj_mms(k + 2, 10, 16)
            elif pr == 5 and k + 1 < 8:
                wvv[k + 1] = emit_conv(k + 1)
            elif pr == 6:
                if k == 2:
                    emit_xbproj("x", 1, 0, 8)
                elif k == 3:
                    emit_xbproj("b", 1, 0, 8)
                elif k == 7:
                    xbr = [("Sgx0", p.v("Sgx0")), ("Sgx1", p.v("Sgx1"))]
                    sccp(xb_sb[:, 0:3], xb_sb[:, 512:515], xbr)
                    sccp(xb_sb[:, 515:518], xb_sb[:, 1027:1030], [])
                    xb_read = xbr
            elif pr == 7:
                if k == 2:
                    emit_xbproj("x", 1, 8, 16)
                elif k == 3:
                    emit_xbproj("b", 1, 8, 16)

        pend_ch = [None]
        pend_ys = [None]
        pend_evac = [None]

        def emit_evac(kb, ysb):
            nonlocal yps_free
            ev = p.emit("vector",
                        lambda e, kb=kb: e.scalar_tensor_tensor(
                            yps_sb[:], _t(u_sb, kb, 512),
                            d_sb[:, kb:kb + 1], P_Y[:], AL.mult, AL.add),
                        [("St", ysb), yps_free], "Sv")
            bank["Y"] = ("Sv", ev)
            dma(yp[128 * kb:128 * kb + 128, col], yps_sb[:],
                [("Sv", ev)], sem="So")
            yps_free = ("So", p.v("So"))
            u_freev[kb] = ("Sv", ev)

        def drain_ch(fence_w):
            if pend_ch[0] is None:
                return
            Pq, ecq = pend_ch[0]
            pend_ch[0] = None
            c2, c3 = Pq % 2, Pq % 3
            sl0q = 2 * (Pq % 2)
            ch = p.emit("gpsimd",
                        lambda e, c2=c2, c3=c3, sl0q=sl0q: e.tensor_tensor(
                            _t(ch_sb, c3, 1024),
                            hp_sb[:, sl0q * 512:sl0q * 512 + 1024],
                            ce_sb[:, c2 * 512:c2 * 512 + 512]
                            .unsqueeze(1).broadcast_to([128, 2, 512]),
                            AL.mult),
                        [fence_w, ("Ss", ecq), ch_free[c3]], "Sg")
            ce_freed[c2] = ("Sg", ch)
            slot_free[sl0q] = ("Sg", ch)
            slot_free[sl0q + 1] = ("Sg", ch)
            pend_ys[0] = (Pq, ch)

        def drain_ysum():
            if pend_ys[0] is None:
                return
            Pq, ch = pend_ys[0]
            pend_ys[0] = None
            c3 = Pq % 3
            if pend_evac[0] is not None:
                kb, ysb = pend_evac[0]
                pend_evac[0] = None
                emit_evac(kb, ysb)
            ys = 0
            for jj, half in ((2 * Pq, 0), (2 * Pq + 1, 1)):
                cc = (jj % 16) // 4
                tpp = (0, 96) if cc == 3 else None
                ys = mm(P_Y[32 * cc:32 * cc + 32, :],
                        s_sb[:, 32 * (jj % 4):32 * (jj % 4) + 32],
                        ch_sb[:, c3 * 1024 + half * 512:
                              c3 * 1024 + half * 512 + 512],
                        jj % 4 == 0, jj % 4 == 3,
                        [("Sg", ch)] + ([bank["Y"]] if jj % 16 == 0
                                        else []),
                        tp=tpp, skip=True)
            ch_free[c3] = ("St", ys)
            if Pq % 8 == 7:
                pend_evac[0] = (Pq // 8, ys)

        for k in range(8):
            for pr in range(8):
                P = 8 * k + pr
                j0, j1 = 2 * P, 2 * P + 1
                ppar = P % 2
                cpar = P % 2
                base = 32 * ((j0 % 16) // 4)
                kt = k
                be_w = None
                if P % 4 == 0:
                    g = P // 4
                    gp = g % 2
                    bb = 32 * ((g % 8) // 2)
                    mb_ = mm(P_CE[:],
                             e16_sb[bb:bb + 32, 128 * (g % 2):
                                    128 * (g % 2) + 128],
                             b_sb[bb:bb + 32,
                                  (g // 8) * 512:(g // 8) * 512 + 512],
                             True, True,
                             [bank["CE"], ("Ss", bev[g // 8])],
                             tp=(bb, 0))
                    be_w = mb_
                for jj, half in ((j0, 0), (j1, 1)):
                    eidx = 4 * (jj // 16) + (jj % 4)
                    mm(P_AD[:, half * 512:half * 512 + 512],
                       ea_sb[base:base + 32, 128 * eidx:128 * eidx + 128],
                       dl_sb[base:base + 32, kt * 512:kt * 512 + 512],
                       True, True,
                       ([bank["AD"], ("Sv", dlv[k])] if half == 0 else []),
                       tp=(base, 0))
                ma = p.v("St")
                da = act(_t(da_sb, ppar, 1024), P_AD[:], AF.Exp,
                         waits=[("St", ma), da_free[ppar]])
                bank["AD"] = ("Ss", da)
                if be_w is not None:
                    gp = (P // 4) % 2
                    eb = sccp(be_sb[:, gp * 512:gp * 512 + 512], P_CE[:],
                              [("St", be_w), be_freed[gp]])
                    bank["CE"] = ("Ss", eb)
                    be_ready = eb
                for jj, half in ((j0, 0), (j1, 1)):
                    mm(P_WE[:, half * 512:half * 512 + 512],
                       e8_sb[base:base + 32,
                             128 * (jj % 4):128 * (jj % 4) + 128],
                       w_sb[base:base + 32, kt * 512:kt * 512 + 512],
                       True, True,
                       ([bank["WE"], we_freed, ("Sv", wvv[k])]
                        if half == 0 else []),
                       tp=(base, 0))
                mw = p.v("St")
                bc_ = 32 * ((P % 8) // 2)
                mc_ = mm(P_CE[:],
                         e16b_sb[bc_:bc_ + 32, 128 * (P % 2):
                                 128 * (P % 2) + 128],
                         c_sb[bc_:bc_ + 32, kt * 512:kt * 512 + 512],
                         True, True, [bank["CE"], ("Ss", cev[k])],
                         tp=(bc_, 0))
                ec = sccp(ce_sb[:, cpar * 512:cpar * 512 + 512], P_CE[:],
                          [("St", mc_), ce_freed[cpar]])
                bank["CE"] = ("Ss", ec)
                gp = (P // 4) % 2
                db = vtt(dbu_sb[:], P_WE[:],
                         be_sb[:, gp * 512:gp * 512 + 512]
                         .unsqueeze(1).broadcast_to([128, 2, 512]),
                         AL.mult,
                         [("St", mw), ("Ss", be_ready), ("Ss", da)])
                we_freed = ("Sv", db)
                if P % 4 == 3:
                    be_freed[gp] = ("Sv", db)
                if j0 % 4 == 0 and P > 0:
                    p.emit("vector",
                           lambda e, j0=j0: e.tensor_copy(
                               st_sb[:, j0 - 4:j0].unsqueeze(2),
                               hp_sb[:].rearrange("p (s t) -> p s t", s=4)
                               [:, :, 511:512]),
                           [], "Sv")
                drain_ysum()
                drain_ch(("Sv", db))
                sc = 0
                for jj, half in ((j0, 0), (j1, 1)):
                    slot = jj % 4
                    sc = p.emit("vector",
                                lambda e, slot=slot, ppar=ppar, jj=jj,
                                half=half:
                                e.tensor_tensor_scan(
                                    _t(hp_sb, slot, 512),
                                    da_sb[:, ppar * 1024 + half * 512:
                                          ppar * 1024 + half * 512 + 512],
                                    dbu_sb[:, half * 512:half * 512 + 512],
                                    st_sb[:, jj:jj + 1], AL.mult, AL.add),
                                [slot_free[slot]], "Sv")
                da_free[ppar] = ("Sv", sc)
                pend_ch[0] = (P, ec)
                filler(k, pr)
            dl_free[k] = ("St", p.v("St"))
            w_free[k] = ("St", p.v("St"))
            c_free[k] = ("St", p.v("St"))
        flet = p.emit("vector",
                      lambda e: e.tensor_copy(fence_sb[:], da_sb[:, 0:16]),
                      [], "Sv")
        drain_ysum()
        drain_ch(("Sv", flet))
        drain_ysum()
        kb_, ysb_ = pend_evac[0]
        pend_evac[0] = None
        emit_evac(kb_, ysb_)
        p.emit("vector",
               lambda e: e.tensor_copy(
                   st_sb[:, NT - 4:NT].unsqueeze(2),
                   hp_sb[:].rearrange("p (s t) -> p s t", s=4)
                   [:, :, 511:512]),
               [], "Sv")
        halo_done = ("Ss", p.v("Ss"))
        b_free = [("St", p.v("St"))] * 2
        xw_free = ("St", p.v("St"))
        if dbg:
            nchk = l // TC
            for bufs, dst, nt_, cast in ((xw, xn_d, 16, False),
                                         (dl_sb, dl_d, 8, True),
                                         (u_sb, u_d, 8, True),
                                         (b_sb, b_d, 2, True),
                                         (c_sb, c_d, 8, False)):
                for tt in range(nt_):
                    srcap = _t(bufs, tt, 512)
                    if cast:
                        srcap = srcap.bitcast(F32)
                    dma(dst[:, (tt * nchk + c) * 512:
                            (tt * nchk + c) * 512 + 512],
                        srcap, [("Sv", p.v("Sv")), ("Sg", p.v("Sg"))],
                        sem="So", throttle=False)
            for tt in range(4):
                dma(h_d[:, (tt * nchk + c) * 512:(tt * nchk + c) * 512 + 512],
                    _t(hp_sb, tt, 512), [("Sv", p.v("Sv"))], sem="So",
                    throttle=False)
            p.emit("sync", lambda e: e.nop(),
                   [(sn, p.v(sn)) for sn in p.sems if sn != "Sd"],
                   "Sd", 16)
            bar = ("Sd", p.v("Sd"))
            xld_cast = [bar, bar]
            dl_free = [bar] * 8
            w_free = [bar] * 8
            c_free = [bar] * 8
            u_freev = [bar] * 8
            b_free = [bar] * 2
            tmp_free = bar
            for _sl in range(4):
                slot_free[_sl] = bar

    std_o = DT("st_d", [128, 128], F32, kind="ExternalOutput").ap()
    dma(std_o, st_sb[:], [("Sv", p.v("Sv"))], sem="So")
    p.emit("sync", lambda e: e.nop(), [("So", p.v("So"))], None)
    p.build()
    return nc



def build_p2(t2=T2, dbg=False):
    nc = bass.Bass("TRN2", target_bir_lowering=False,
                   dynamic_dma_scratch_size=4096)
    p = Prog(nc)
    for s in ("Sd", "St", "Ss", "Sv", "Sg", "So", "Si",
              "Sx0", "Sx1", "Sw0", "Sw1", "Sy0", "Sy1",
              "Sst0", "Sst1", "Sp0", "Sp1", "Swd0", "Swd1",
              "Spl", "Sl0", "Sl1", "So0", "So1"):
        p.sem(s)
    C = Common(p)
    dma, mm, act, sccp, vtt, vcp = C.dma, C.mm, C.act, C.sccp, C.vtt, C.vcp
    ncc = t2 // 512

    DT = nc.dram_tensor
    xt2 = DT("xt2", [DM, t2], F32, kind="ExternalInput").ap()
    yp2 = DT("yp2", [DIN, t2], F32, kind="ExternalInput").ap()
    wz = DT("wz", [128, 32 * 16 * 128], F32, kind="ExternalInput").ap()
    wo = DT("wo", [128, 16 * 32 * 128], F32, kind="ExternalInput").ap()
    wg = DT("wg", [128, 44 * 16 * 128], I32, kind="ExternalInput").ap()
    wu = DT("wu", [128, 44 * 16 * 128], I32, kind="ExternalInput").ap()
    wd = DT("wd", [128, 16 * 44 * 128], I32, kind="ExternalInput").ap()
    bzi = DT("bz", [128, 32], F32, kind="ExternalInput").ap()
    boi = DT("bo", [128, 16], F32, kind="ExternalInput").ap()
    sgi = DT("sg", [128, 44], F32, kind="ExternalInput").ap()
    sui = DT("su", [128, 44], F32, kind="ExternalInput").ap()
    sdi = DT("sdn", [128, 16], F32, kind="ExternalInput").ap()
    onesdi = DT("onesd", [128, 1], BF16, kind="ExternalInput").ap()
    ones1i = DT("ones1", [1, 128], BF16, kind="ExternalInput").ap()
    csti = DT("cst", [128, 4], F32, kind="ExternalInput").ap()
    outt = DT("outt", [DM, t2], F32, kind="ExternalOutput").ap()
    s_dr = DT("s_dr", [DM, t2], F32,
              kind="ExternalOutput" if dbg else "Internal").ap()
    if dbg:
        g_d = DT("g_d", [DM, t2], F32, kind="ExternalOutput").ap()
        sp_d = DT("sp_d", [DIN, t2], BF16, kind="ExternalOutput").ap()
    p_dr = DT("p_dr", [HID, t2], BF16, kind="Internal").ap()

    A = nc.alloc_sbuf_tensor
    arena = A("arena", [128, 96 * 1024], mybir.dt.uint8)
    ARE = arena[:].bitcast(BF16)
    xn2 = ARE[:, 0:16 * t2]
    spc = ARE[:, 16 * t2:16 * t2 + 32 * 512]
    pch = ARE[:, 0:44 * 1024]
    x_ld = A("x_ld", [128, 2 * 512], F32)
    xsq = A("xsq", [128, 512], BF16)
    mean_f = A("mean_f", [1, 512], F32)
    msq_f = A("msq_f", [1, 512], F32)
    m2_f = A("m2_f", [1, 512], F32)
    sd_f = A("sd_f", [1, 512], F32)
    rr_f = A("rr_f", [1, 512], F32)
    mean_bf = A("mean_bf", [1, 512], BF16)
    rr_bf = A("rr_bf", [1, 512], BF16)
    mbc_sb = A("mbc_sb", [128, 512], F32)
    rbc_sb = A("rbc_sb", [128, 512], F32)
    tmpa_sb = A("tmpa_sb", [128, 512], F32)
    onesd_sb = A("onesd_sb", [128, 1], BF16)
    ones1_sb = A("ones1_sb", [1, 128], BF16)
    cst_sb = A("cst_sb", [128, 4], F32)
    bz_sb = A("bz_sb", [128, 32], F32)
    bo_sb = A("bo_sb", [128, 16], F32)
    sg_sb = A("sg_sb", [128, 44], F32)
    su_sb = A("su_sb", [128, 44], F32)
    sd2_sb = A("sd2_sb", [128, 16], F32)
    wst = A("wst", [128, 2 * 5632], F32)
    wbf = A("wbf", [128, 2 * 5632], BF16)
    ypt = A("ypt", [128, 2 * 512], F32)
    szt = A("szt", [128, 512], F32)
    gt_sb = A("gt_sb", [128, 512], F32)
    ut_sb = A("ut_sb", [128, 512], F32)
    pt_sb = A("pt_sb", [128, 2 * 512], BF16)
    st2_sb = A("st2_sb", [128, 2 * 512], F32)
    g2_sb = A("g2_sb", [128, 512], F32)
    sgl_sb = A("sgl_sb", [128, 2 * 512], F32)
    xtl_sb = A("xtl_sb", [128, 2 * 512], F32)
    ot_sb = A("ot_sb", [128, 2 * 512], F32)

    AP_ = nc.alloc_psum_tensor
    P_A = AP_("P_A", [128, 512], F32)
    P_B = AP_("P_B", [128, 512], F32)
    P_C = AP_("P_C", [128, 512], F32)
    P_D = AP_("P_D", [128, 512], F32)
    banks = [("A", P_A[:]), ("B", P_B[:]), ("C", P_C[:]), ("D", P_D[:])]

    EPS1 = cst_sb[0:1, 2:3]
    NEGH = cst_sb[0:1, 3:4]

    for dst, src in ((bz_sb, bzi), (bo_sb, boi), (sg_sb, sgi), (su_sb, sui),
                     (sd2_sb, sdi), (onesd_sb, onesdi), (ones1_sb, ones1i),
                     (cst_sb, csti)):
        dma(dst[:], src, sem="Si", throttle=False)
    v0 = p.v("Si")

    bank = {k: ("Ss", 0) for k in ("A", "B", "C", "D")}
    xld_cast = [("Sv", 0), ("Sv", 0)]
    wst_free = [("Sv", 0), ("Sv", 0)]
    wbf_free = [("St", 0), ("St", 0)]

    def xslice(t, c):
        off = (t * ncc + c) * 512
        return xn2[:, off:off + 512]

    for c in range(ncc):
        col = slice(c * 512, (c + 1) * 512)
        last_q = 0
        for t in range(16):
            par = t % 2
            sx = "Sx%d" % par
            ld = dma(_t(x_ld, par, 512), xt2[128 * t:128 * (t + 1), col],
                     [xld_cast[par]], sem=sx)
            cwv = p.emit("vector",
                         lambda e, t=t, par=par, c=c: e.tensor_copy(
                             xslice(t, c), _t(x_ld, par, 512)),
                         [(sx, ld)], "Sv")
            xld_cast[par] = ("Sv", cwv)
            sq = p.emit("vector",
                        lambda e, t=t, c=c: e.tensor_tensor(
                            xsq[:], xslice(t, c), xslice(t, c), AL.mult),
                        [("St", last_q)], "Sv")
            mm(P_A[0:1, :], onesd_sb[:], xslice(t, c), t == 0, t == 15,
               [("Sv", cwv)] + ([bank["A"], ("Si", v0)] if t == 0 else []))
            last_q = mm(P_B[0:1, :], onesd_sb[:], xsq[:], t == 0, t == 15,
                        [("Sv", sq)] + ([bank["B"]] if t == 0 else []))
        stats_done = p.v("St")
        c1 = vcp(mean_f[:], P_A[0:1, :], [("St", stats_done)])
        vcp(mean_bf[:], P_A[0:1, :])
        c2 = vcp(msq_f[:], P_B[0:1, :])
        vtt(m2_f[:], mean_f[:], mean_f[:], AL.mult, [])
        v_var = vtt(sd_f[:], msq_f[:], m2_f[:], AL.subtract, [])
        a_ln = act(rr_f[:], sd_f[:], AF.Ln, bias=EPS1,
                   waits=[("Sv", v_var)])
        a_rs = act(sd_f[:], rr_f[:], AF.Exp, scale=NEGH)
        v_rrb = vcp(rr_bf[:], sd_f[:], [("Ss", a_rs)])
        mm(P_A[:], ones1_sb[:], mean_bf[:], True, True,
           [("Sv", v_rrb), bank["A"]])
        rb = mm(P_B[:], ones1_sb[:], rr_bf[:], True, True, [bank["B"]])
        e_mb = sccp(mbc_sb[:], P_A[:], [("St", rb)])
        e_rb = sccp(rbc_sb[:], P_B[:], [])
        bank["A"] = ("Ss", e_mb)
        bank["B"] = ("Ss", e_rb)
        lnt = 0
        for t in range(16):
            par = t % 2
            sx = "Sx%d" % par
            ld2 = dma(_t(x_ld, par, 512), xt2[128 * t:128 * (t + 1), col],
                      [xld_cast[par]], sem=sx)
            w1 = [(sx, ld2), ("Ss", e_rb)]
            if t == 0:
                w1.append(("St", stats_done))
            lnt = vtt(tmpa_sb[:], _t(x_ld, par, 512), mbc_sb[:], AL.subtract,
                      w1)
            lnt = vtt(xslice(t, c), tmpa_sb[:], rbc_sb[:], AL.mult, [])
            xld_cast[par] = ("Sv", lnt)
    xn_done = p.v("Sv")

    def wload(src, mt, width, dt_in, par, waits=()):
        sw = "Sw%d" % par
        if dt_in == F32:
            stg = wst[:, par * 5632:par * 5632 + width]
        else:
            stg = wst[:, par * 5632:par * 5632 + width].bitcast(I32)
        ld = dma(stg, src[:, mt * width:mt * width + width],
                 [wst_free[par]] + list(waits), sem=sw)
        dstb = wbf[:, par * 5632:par * 5632 + width]
        if dt_in == F32:
            cv = p.emit("vector",
                        lambda e, o=dstb, i=stg: e.tensor_copy(o, i),
                        [(sw, ld), wbf_free[par]], "Sv")
        else:
            cv = p.emit("vector",
                        lambda e, o=dstb, i=stg: e.tensor_scalar(
                            o, i, 128.0, None, AL.subtract),
                        [(sw, ld), wbf_free[par]], "Sv")
        wst_free[par] = ("Sv", cv)
        return dstb, cv

    spc_free = ("St", 0)
    ypt_free = [("Sv", 0), ("Sv", 0)]
    szt_free = ("Sv", 0)
    st2_free = [("Sst0", 0), ("Sst1", 0)]
    for c in range(ncc):
        zpend = wload(wz, 0, 2048, F32, 0)
        for m in range(32):
            par = m % 2
            dstb, cv = zpend
            if m + 1 < 32:
                zpend = wload(wz, m + 1, 2048, F32, (m + 1) % 2)
            nm, pt_ = banks[m % 4]
            last = 0
            for k in range(16):
                w0 = []
                if k == 0:
                    w0 = [("Sv", cv), bank[nm]]
                    if c == 0 and m < 4:
                        w0.append(("Sv", xn_done))
                last = mm(pt_, dstb[:, k * 128:k * 128 + 128], xslice(k, c),
                          k == 0, k == 15, w0)
            wbf_free[par] = ("St", last)
            a_sz = act(szt[:], pt_, AF.Silu, bias=bz_sb[:, m:m + 1],
                       waits=[("St", last), szt_free])
            bank[nm] = ("Ss", a_sz)
            ldy = dma(_t(ypt, par, 512),
                      yp2[128 * m:128 * m + 128, c * 512:c * 512 + 512],
                      [ypt_free[par]], sem="Sy%d" % par)
            sv = vtt(_t(spc, m, 512), _t(ypt, par, 512), szt[:], AL.mult,
                     [("Ss", a_sz), ("Sy%d" % par, ldy), spc_free] if m == 0
                     else [("Ss", a_sz), ("Sy%d" % par, ldy)])
            ypt_free[par] = ("Sv", sv)
            szt_free = ("Sv", sv)
        sp_done = p.v("Sv")
        if dbg:
            for mm_ in range(32):
                dma(sp_d[128 * mm_:128 * mm_ + 128, c * 512:c * 512 + 512],
                    _t(spc, mm_, 512), [("Sv", sp_done)], sem="So",
                    throttle=False)
        opend = wload(wo, 0, 4096, F32, 0)
        for m in range(16):
            par = m % 2
            dstb, cv = opend
            if m + 1 < 16:
                opend = wload(wo, m + 1, 4096, F32, (m + 1) % 2)
            nm, pt_ = banks[m % 4]
            last = 0
            for k in range(32):
                last = mm(pt_, dstb[:, k * 128:k * 128 + 128],
                          _t(spc, k, 512), k == 0, k == 31,
                          [("Sv", max(cv, sp_done)), bank[nm]] if k == 0
                          else [])
            wbf_free[par] = ("St", last)
            ev = act(_t(st2_sb, par, 512), pt_, AF.Identity,
                     bias=bo_sb[:, m:m + 1],
                     waits=[("St", last), st2_free[par]])
            bank[nm] = ("Ss", ev)
            sd_ = dma(s_dr[128 * m:128 * m + 128, c * 512:c * 512 + 512],
                      _t(st2_sb, par, 512), [("Ss", ev)], sem="Sst%d" % par)
            st2_free[par] = ("Sst%d" % par, sd_)
        spc_free = ("St", p.v("St"))

    pt_free = [("Sp0", 0), ("Sp1", 0)]
    gu_free = ("Sv", 0)
    wstF = [("Sv", max(wst_free[0][1], wst_free[1][1]))] * 4
    wbfF = [("St", max(wbf_free[0][1], wbf_free[1][1]))] * 4

    def f2_load(hm):
        out = []
        for wsrc, sl, sem in ((wg, hm % 2, "Sw%d" % (hm % 2)),
                              (wu, 2 + hm % 2, "Swd%d" % (hm % 2))):
            stg = wst[:, sl * 2048:sl * 2048 + 2048].bitcast(I32)
            ld = dma(stg, wsrc[:, hm * 2048:hm * 2048 + 2048],
                     [wstF[sl]], sem=sem)
            dstb = wbf[:, sl * 2048:sl * 2048 + 2048]
            cv = p.emit("vector",
                        lambda e, o=dstb, i=stg: e.tensor_scalar(
                            o, i, 128.0, None, AL.subtract),
                        [(sem, ld), wbfF[sl]], "Sv")
            wstF[sl] = ("Sv", cv)
            out.append((dstb, cv, sl))
        return out

    fpend = f2_load(0)
    for hm in range(44):
        (dg, cg, slg), (du, cu, slu) = fpend
        if hm + 1 < 44:
            fpend = f2_load(hm + 1)
        lg = lu = 0
        for c in range(ncc):
            for k in range(16):
                lg = mm(P_A[:], dg[:, k * 128:k * 128 + 128], xslice(k, c),
                        k == 0, k == 15,
                        [("Sv", cg), bank["A"]] if k == 0 else [])
            for k in range(16):
                lu = mm(P_B[:], du[:, k * 128:k * 128 + 128], xslice(k, c),
                        k == 0, k == 15,
                        [("Sv", cu), bank["B"]] if k == 0 else [])
            ag = act(gt_sb[:], P_A[:], AF.Silu, scale=sg_sb[:, hm:hm + 1],
                     waits=[("St", lg), gu_free])
            au = act(ut_sb[:], P_B[:], AF.Identity, scale=su_sb[:, hm:hm + 1],
                     waits=[("St", lu)])
            bank["A"] = ("Ss", ag)
            bank["B"] = ("Ss", au)
            ppar = c % 2
            pv = vtt(_t(pt_sb, ppar, 512), gt_sb[:], ut_sb[:], AL.mult,
                     [("Ss", au), pt_free[ppar]])
            gu_free = ("Sv", pv)
            pd = dma(p_dr[128 * hm:128 * hm + 128, c * 512:c * 512 + 512],
                     _t(pt_sb, ppar, 512), [("Sv", pv)], sem="Sp%d" % ppar)
            pt_free[ppar] = ("Sp%d" % ppar, pd)
        wbfF[slg] = ("St", lg)
        wbfF[slu] = ("St", lu)

    vall = max(v for _, v in wstF)
    ball = max(v for _, v in wbfF)
    wst_free = [("Sv", vall), ("Sv", vall)]
    wbf_free = [("St", ball), ("St", ball)]
    p_stored = [("Sp0", p.v("Sp0")), ("Sp1", p.v("Sp1"))]
    s_stored = [("Sst0", p.v("Sst0")), ("Sst1", p.v("Sst1"))]
    pch_free = ("St", p.v("St"))
    g_done = ("Sv", 0)
    sgl_free = [("Sv", 0), ("Sv", 0)]
    xtl_free = [("Sv", 0), ("Sv", 0)]
    ot_free = [("So0", 0), ("So1", 0)]
    for c3 in range(t2 // 1024):
        ldp = 0
        for hh in range(44):
            first = hh == 0
            ldp = dma(_t(pch, hh, 1024),
                      p_dr[128 * hh:128 * hh + 128,
                           c3 * 1024:c3 * 1024 + 1024],
                      ([pch_free] + (p_stored if c3 == 0 else []))
                      if first else [],
                      sem="Spl", throttle=first)
        dpend = wload(wd, 0, 44 * 128, I32, 0)
        for m in range(16):
            par = m % 2
            dstb, cv = dpend
            if m + 1 < 16:
                dpend = wload(wd, m + 1, 44 * 128, I32, (m + 1) % 2)
            for half in range(2):
                nm, pt_ = banks[(m * 2 + half) % 4]
                last = 0
                for k in range(44):
                    last = mm(pt_, dstb[:, k * 128:k * 128 + 128],
                              pch[:, (k * 2 + half) * 512:
                                  (k * 2 + half) * 512 + 512],
                              k == 0, k == 43,
                              [("Sv", cv), ("Spl", ldp), bank[nm]]
                              if k == 0 else [])
                ag = act(g2_sb[:], pt_, AF.Silu, scale=sd2_sb[:, m:m + 1],
                         waits=[("St", last), g_done])
                bank[nm] = ("Ss", ag)
                colh = slice(c3 * 1024 + half * 512,
                             c3 * 1024 + half * 512 + 512)
                if dbg:
                    dma(g_d[128 * m:128 * m + 128, colh], g2_sb[:],
                        [("Ss", ag)], sem="So", throttle=False)
                lds = dma(_t(sgl_sb, half, 512),
                          s_dr[128 * m:128 * m + 128, colh],
                          [sgl_free[half]] +
                          (s_stored if (c3 == 0 and m == 0) else []),
                          sem="Sl%d" % half)
                ldx = dma(_t(xtl_sb, half, 512),
                          xt2[128 * m:128 * m + 128, colh],
                          [xtl_free[half]], sem="Sx%d" % half)
                v1 = vtt(_t(ot_sb, half, 512), g2_sb[:],
                         _t(sgl_sb, half, 512), AL.mult,
                         [("Ss", ag), ("Sl%d" % half, lds),
                          ("Sx%d" % half, ldx), ot_free[half]])
                g_done = ("Sv", v1)
                sgl_free[half] = ("Sv", v1)
                v2 = vtt(_t(ot_sb, half, 512), _t(ot_sb, half, 512),
                         _t(xtl_sb, half, 512), AL.add, [])
                xtl_free[half] = ("Sv", v2)
                od = dma(outt[128 * m:128 * m + 128, colh],
                         _t(ot_sb, half, 512), [("Sv", v2)],
                         sem="So%d" % half)
                ot_free[half] = ("So%d" % half, od)
            wbf_free[par] = ("St", p.v("St"))
        pch_free = ("St", p.v("St"))

    p.emit("sync", lambda e: e.nop(),
           [("So0", p.v("So0")), ("So1", p.v("So1")), ("So", p.v("So"))],
           None)
    p.build()
    return nc



def _tile_w(wT):
    K, M = wT.shape
    nk, nm = K // 128, M // 128
    a = wT.reshape(nk, 128, nm, 128).transpose(1, 2, 0, 3).reshape(128, -1)
    return np.ascontiguousarray(a)


def _pp(vec):
    n = vec.shape[0] // 128
    return np.ascontiguousarray(vec.reshape(n, 128).T)


def _mk_e8():
    e = np.zeros((128, 512), np.float32)
    for g4 in range(4):
        for v in range(4):
            for dd in range(8):
                for s in range(16):
                    e[32 * g4 + 8 * v + dd, 128 * v + 16 * dd + s] = 1.0
    return e


def _mk_e16():
    e = np.zeros((128, 256), np.float32)
    for g4 in range(4):
        for v in range(2):
            for dd in range(8):
                for s in range(16):
                    e[32 * g4 + 16 * v + s, 128 * v + 16 * dd + s] = 1.0
    return e


def _mk_smat():
    s = np.zeros((128, 128), np.float32)
    for m in range(4):
        for pp_ in range(128):
            s[pp_, 32 * m + 8 * m + pp_ // 16] = 1.0
    return s


def _mk_alog_ea(alog_slice):
    out = np.full((128, 4096), -1e30, np.float32)
    for j in range(128):
        g4 = (j % 16) // 4
        eidx = 4 * (j // 16) + (j % 4)
        r8 = 8 * (j % 4)
        for dd in range(8):
            out[32 * g4 + r8 + dd,
                128 * eidx + 16 * dd:128 * eidx + 16 * dd + 16] = \
                alog_slice[8 * j + dd, :]
    return out


_CACHE = {}


def kernel(**inp):
    x = np.ascontiguousarray(np.asarray(inp["L0_attn_hidden_states"],
                                        np.float32))
    Wx = np.asarray(inp["L0_attn_Wx"], np.float32)
    bx = np.asarray(inp["L0_attn_bx"], np.float32)
    Wz = np.asarray(inp["L0_attn_Wz"], np.float32)
    bz = np.asarray(inp["L0_attn_bz"], np.float32)
    conv_w = np.asarray(inp["L0_attn_conv_w"], np.float32)
    conv_b = np.asarray(inp["L0_attn_conv_b"], np.float32)
    WB = np.asarray(inp["L0_attn_WB"], np.float32)
    WC = np.asarray(inp["L0_attn_WC"], np.float32)
    Wdtd = np.asarray(inp["L0_attn_Wdt_down"], np.float32)
    Wdt = np.asarray(inp["L0_attn_Wdt"], np.float32)
    bdt = np.asarray(inp["L0_attn_bdt"], np.float32)
    A_log = np.asarray(inp["L0_attn_A_log"], np.float32)
    Dv = np.asarray(inp["L0_attn_D"], np.float32)
    Wout = np.asarray(inp["L0_attn_Wout"], np.float32)
    bout = np.asarray(inp["L0_attn_bout"], np.float32)
    wqg = np.asarray(inp["L0_ffn_wq_gate"], np.int32)
    sg = np.asarray(inp["L0_ffn_s_gate"], np.float32)
    wqu = np.asarray(inp["L0_ffn_wq_up"], np.int32)
    su = np.asarray(inp["L0_ffn_s_up"], np.float32)
    wqd = np.asarray(inp["L0_ffn_wq_down"], np.int32)
    sdn = np.asarray(inp["L0_ffn_s_down"], np.float32)

    if "p1" not in _CACHE:
        _CACHE["p1"] = build_p1()
        _CACHE["p2"] = build_p2()
    nc1, nc2_ = _CACHE["p1"], _CACHE["p2"]

    e8 = _mk_e8()
    e16 = _mk_e16()
    e16b = e16.astype(BFNP)
    smat = _mk_smat().astype(BFNP)
    cst = np.tile(np.array([[-1.0, 1.0, 1e-5, -0.5]], np.float32), (128, 1))
    onesd = np.full((128, 1), 1.0 / 2048.0, BFNP)
    ones1 = np.ones((1, 128), BFNP)

    xts = [np.ascontiguousarray(x[b].T) for b in range(2)]

    in1 = []
    for core in range(8):
        b, q = core // 4, core % 4
        sl1 = slice(DSL * q, DSL * (q + 1))
        sl4 = slice(256 * q, 256 * (q + 1))
        in1.append({
            "xt": xts[b],
            "wx": _tile_w(np.ascontiguousarray(Wx[sl4].T)),
            "wbm": _tile_w(np.ascontiguousarray(WB[sl4].T)),
            "wc": _tile_w(np.ascontiguousarray(WC[sl1].T)),
            "wdtd": _tile_w(np.ascontiguousarray(Wdtd.T)),
            "wdt": _tile_w(np.ascontiguousarray(Wdt[sl1].T)),
            "alog": _mk_alog_ea(A_log[sl1]),
            "e8": e8, "e16": e16, "e16b": e16b, "smat": smat,
            "cw": np.ascontiguousarray(
                conv_w[sl1, 0, :].reshape(8, 128, 4).transpose(1, 0, 2)
                .reshape(128, 32)),
            "cb": _pp(conv_b[sl1]),
            "bx": _pp(bx[sl4]),
            "bdt": _pp(bdt[sl1]),
            "dvec": _pp(Dv[sl1]),
            "cst": cst, "onesd": onesd, "ones1": ones1,
        })
    r1 = run_bass_kernel_spmd(nc1, in1, core_ids=list(range(8)))
    ypl = [np.asarray(r1.results[i]["yp"]) for i in range(8)]

    wz_t = _tile_w(np.ascontiguousarray(Wz.T))
    wo_t = _tile_w(np.ascontiguousarray(Wout.T))
    wg_t = _tile_w(np.ascontiguousarray(wqg.T))
    wu_t = _tile_w(np.ascontiguousarray(wqu.T))
    wd_t = _tile_w(np.ascontiguousarray(wqd.T))
    bz_p = _pp(bz)
    bo_p = _pp(bout)
    sg_p = _pp(sg)
    su_p = _pp(su)
    sd_p = _pp(sdn)

    in2 = []
    for core in range(8):
        b, q = core // 4, core % 4
        ybf = np.concatenate([ypl[4 * b + qq] for qq in range(4)], axis=0)
        in2.append({
            "xt2": np.ascontiguousarray(xts[b][:, T2 * q:T2 * (q + 1)]),
            "yp2": np.ascontiguousarray(ybf[:, T2 * q:T2 * (q + 1)]),
            "wz": wz_t, "wo": wo_t, "wg": wg_t, "wu": wu_t, "wd": wd_t,
            "bz": bz_p, "bo": bo_p, "sg": sg_p, "su": su_p, "sdn": sd_p,
            "onesd": onesd, "ones1": ones1, "cst": cst,
        })
    r2 = run_bass_kernel_spmd(nc2_, in2, core_ids=list(range(8)))

    out = np.empty((2, L, DM), np.float32)
    for core in range(8):
        b, q = core // 4, core % 4
        out[b, T2 * q:T2 * (q + 1), :] = np.asarray(
            r2.results[core]["outt"]).T
    return out
